# version 08
# baseline (speedup 1.0000x reference)
import numpy as np
import ml_dtypes
import concourse.bass as bass
import concourse.mybir as mybir
from concourse.bass_utils import run_bass_kernel_spmd

F32 = mybir.dt.float32
F32R = mybir.dt.float32r
BF16 = mybir.dt.bfloat16
I32 = mybir.dt.int32
AF = mybir.ActivationFunctionType
ALU = mybir.AluOpType
AX = mybir.AxisListType.X

D = 2048
T = 2048
KC = 16
NB = 512
TH = 1024


def lhsT_tiles(w_km):
    K, M = w_km.shape
    kc, mc = K // 128, M // 128
    t = w_km.reshape(kc, 128, mc, 128).transpose(2, 1, 0, 3)
    return np.ascontiguousarray(t.reshape(mc, 128, kc * 128))


def skinny_lhsT(w_km):
    K, M = w_km.shape
    kc = K // 128
    return np.ascontiguousarray(w_km.reshape(kc, 128, M).transpose(1, 0, 2))



ENGS = ("tensor", "vector", "scalar", "gpsimd", "sync")


class Sem:
    def __init__(self, nc, name):
        self.h = nc.alloc_semaphore(name)
        self.n = 0
        self.name = name

    def tick(self, k=1):
        self.n += k
        return self.n


class Prog:
    def __init__(self, nc):
        self.nc = nc
        self.ops = {e: [] for e in ENGS}
        self.sems = {}

    def sem(self, name):
        if name not in self.sems:
            self.sems[name] = Sem(self.nc, name)
        return self.sems[name]

    def wait(self, eng, sem, val):
        assert val <= sem.n, f"wait {sem.name}>={val} but only {sem.n} emitted"
        if val > 0:
            self.ops[eng].append(lambda e: e.wait_ge(sem.h, val))

    def op(self, eng, fn, inc=None):
        if inc is None:
            self.ops[eng].append(fn)
            return None
        sem, k = inc
        v = sem.tick(k)
        self.ops[eng].append(lambda e: fn(e).then_inc(sem.h, k))
        return v

    def barrier(self):
        vals = {}
        for eng in ENGS:
            s = self.sem("bar_" + eng)
            vals[eng] = self.op(eng, lambda e: e.nop(), inc=(s, 1))
        for eng in ENGS:
            for other in ENGS:
                if other != eng:
                    self.wait(eng, self.sem("bar_" + other), vals[other])

    def finalize(self):
        nc = self.nc
        with nc.Block() as block:
            @block.tensor
            def _(t):
                for f in self.ops["tensor"]:
                    f(t)

            @block.vector
            def _(v):
                for f in self.ops["vector"]:
                    f(v)

            @block.scalar
            def _(s):
                for f in self.ops["scalar"]:
                    f(s)

            @block.gpsimd
            def _(g):
                for f in self.ops["gpsimd"]:
                    f(g)

            @block.sync
            def _(sy):
                for f in self.ops["sync"]:
                    f(sy)
        return nc


class Ring:
    def __init__(self, n):
        self.n = n
        self.guard = [None] * n

    def acquire(self, P, eng, i):
        g = self.guard[i % self.n]
        if g is not None:
            P.wait(eng, g[0], g[1])

    def release(self, i, sem, val):
        self.guard[i % self.n] = (sem, val)


class DmaRing:
    _uid = 0

    def __init__(self, P, name, n):
        DmaRing._uid += 1
        self.P, self.n = P, n
        self.sems = [P.sem(f"dr{DmaRing._uid}_{name}_{i}") for i in range(n)]
        self.guard = [None] * n

    def load(self, eng, i, fn):
        g = self.guard[i % self.n]
        if g:
            for gs, gv in g:
                self.P.wait(eng, gs, gv)
        s = self.sems[i % self.n]
        v = self.P.op(eng, fn, inc=(s, 16))
        return (s, v)

    def consumed(self, i, sem, val):
        self.guard[i % self.n] = [(sem, val)]

    def also_consumed(self, i, sem, val):
        g = self.guard[i % self.n]
        if g is None:
            g = self.guard[i % self.n] = []
        g.append((sem, val))


def vbf(ar_ap, off_f32, n):
    return ar_ap.bitcast(BF16)[:, 2 * off_f32: 2 * off_f32 + n]


def vf(ar_ap, off_f32, n):
    return ar_ap[:, off_f32: off_f32 + n]


class Ctx:
    pass


def build_common(nc, P):
    c = Ctx()
    c.nc, c.P = nc, P
    c.A_r = nc.alloc_sbuf_tensor("arA", [128, 16384], F32R).ap()
    c.A = c.A_r.bitcast(F32)
    c.B = nc.alloc_sbuf_tensor("arB", [128, 16384], F32).ap()
    c.C_r = nc.alloc_sbuf_tensor("arC", [128, 8192], F32R).ap()
    c.C = c.C_r.bitcast(F32)
    c.W = nc.alloc_sbuf_tensor("arW", [128, 4280], F32).ap()
    c.FR = nc.alloc_sbuf_tensor("arFR", [128, 2560], F32R).ap()
    c.S = nc.alloc_sbuf_tensor("arS", [128, 2048], F32).ap()
    c.L = nc.alloc_sbuf_tensor("arL", [128, 1536], F32).ap()
    c.R = nc.alloc_sbuf_tensor("arR", [128, 1024], F32).ap()
    c.PS = [nc.alloc_psum_tensor(f"ps{i}", [128, 1024], F32).ap() for i in range(4)]
    c.psg = {}

    c.idf = nc.alloc_sbuf_tensor("idf", [128, 128], F32).ap()
    c.idb = nc.alloc_sbuf_tensor("idb", [128, 128], BF16).ap()
    c.ones_col_f = nc.alloc_sbuf_tensor("ones_col_f", [128, 1], F32).ap()
    c.ones_col_b = nc.alloc_sbuf_tensor("ones_col_b", [128, 1], BF16).ap()
    c.ones_row_f = nc.alloc_sbuf_tensor("ones_row_f", [1, 128], F32).ap()
    c.ones_row_b = nc.alloc_sbuf_tensor("ones_row_b", [1, 128], BF16).ap()
    c.m128_row_b = nc.alloc_sbuf_tensor("m128_row_b", [1, 128], BF16).ap()
    c.eps_col = nc.alloc_sbuf_tensor("eps_col", [1, 1], F32).ap()
    c.pi2_col = nc.alloc_sbuf_tensor("pi2_col", [128, 1], F32).ap()
    c.maskd = nc.alloc_sbuf_tensor("maskd", [128, 128], BF16).ap()

    ident = nc.dram_tensor("c_ident", [128, 128], F32, kind="ExternalInput").ap()
    maskd_d = nc.dram_tensor("c_maskd", [128, 128], F32, kind="ExternalInput").ap()
    crow = nc.dram_tensor("c_rows", [4, 128], F32, kind="ExternalInput").ap()
    sLDG = P.sem("sLDG")
    for out, in_ in [(c.idf, ident), (c.idb, ident),
                     (c.ones_row_f, crow[0:1, :]), (c.ones_row_b, crow[0:1, :]),
                     (c.m128_row_b, crow[1:2, :]), (c.eps_col, crow[2:3, 0:1]),
                     (c.ones_col_f, crow[0:1, 0:1].broadcast_to([128, 1])),
                     (c.ones_col_b, crow[0:1, 0:1].broadcast_to([128, 1])),
                     (c.pi2_col, crow[3:4, 0:1].broadcast_to([128, 1])),
                     (c.maskd, maskd_d)]:
        v = P.op("gpsimd", lambda e, out=out, in_=in_: e.dma_start(out=out, in_=in_),
                 inc=(sLDG, 16))
    for eng in ENGS:
        P.wait(eng, sLDG, v)
    return c


def ps_wait(c, P, eng, i, halves=(0, 1)):
    for h in halves:
        g = c.psg.get((i, h))
        if g is not None:
            P.wait(eng, g[0], g[1])


def ps_set(c, i, sem, val, halves=(0, 1)):
    for h in halves:
        c.psg[(i, h)] = (sem, val)


def host_consts():
    crow = np.zeros((4, 128), dtype=np.float32)
    crow[0, :] = 1.0
    crow[1, :] = -128.0
    crow[2, :] = 1e-5
    crow[3, :] = np.pi / 2
    maskd = np.triu(np.full((128, 128), -1e9, dtype=np.float32), k=1)
    return {"c_ident": np.eye(128, dtype=np.float32), "c_rows": crow,
            "c_maskd": maskd}



def emit_ln(c, x_in_ap, t0, width, y_ar, y_off, router=None, y_row=None, y_col=0,
            ylo_ar=None, ylo_off=0, split4=False):
    if y_row is None:
        y_row = width // 2
    P = c.P
    sPE = P.sem("sPE"); sACT = P.sem("sACT")
    sDVE = P.sem("sDVE"); sPOOL = P.sem("sPOOL")
    xw = c.C[:, 0:8192].rearrange("p (kc t) -> p kc t", kc=16)
    x_src = x_in_ap.rearrange("(kc p) t -> p kc t", p=128)
    sq = [vbf(c.L, 0, 512), vbf(c.L, 0, 512)]
    t1s = vf(c.L, 256, 256) if False else vf(c.L, 0, 512)
    bcm, bcr = vf(c.L, 512, 512), vf(c.L, 1024, 512)
    row_m = c.R[0:1, 0:512]
    row_v = c.R[0:1, 512:1024]

    for sub in range(width // NB):
        tt0 = t0 + sub * NB
        if split4 and sub == 0:
            s_xw = c.ring_xw.sems[0]
            s_xwg = P.sem("sXWG")
            g_xw = c.ring_xw.guard[0]
            vtot = None
            vtot_g = None
            for (k0, k1), eng in zip(((0, 6), (6, 11), (11, 16)),
                                     ("scalar", "sync", "gpsimd")):
                if g_xw:
                    for gs_, gv_ in g_xw:
                        P.wait(eng, gs_, gv_)
                sm = s_xwg if eng == "gpsimd" else s_xw
                v_ = P.op(eng, lambda e, k0=k0, k1=k1, tt0=tt0: e.dma_start(
                    out=xw[:, k0:k1, :],
                    in_=x_src[:, k0:k1, tt0:tt0 + NB]), inc=(sm, 16))
                if eng == "gpsimd":
                    vtot_g = v_
                else:
                    vtot = v_
            c.ring_xw.guard[0] = None
            tok_x = (s_xw, vtot)
            P.wait("tensor", s_xwg, vtot_g)
            P.wait("scalar", s_xwg, vtot_g)
        else:
            tok_x = c.ring_xw.load("scalar", 0, lambda e, tt0=tt0: e.dma_start(
                out=xw, in_=x_src[:, :, tt0:tt0 + NB]))
        bk_s = c.PS[3][0:1, 0:512]
        bk_q = c.PS[3][0:1, 512:1024]
        P.wait("tensor", *tok_x)
        P.wait("scalar", *tok_x)
        ps_wait(c, P, "tensor", 3)
        sq_pe = {}
        for kc in range(16):
            if kc >= 1:
                P.wait("scalar", sPE, sq_pe[kc - 1])
            if kc == 0:
                c.ring_sq.acquire(P, "scalar", 0)
            vv = P.op("scalar", lambda e, kc=kc: e.activation(
                sq[kc % 2], xw[:, kc, :], AF.Square), inc=(sACT, 1))
            P.op("tensor", lambda e, kc=kc: e.matmul(
                bk_s, c.ones_col_f, xw[:, kc, :], start=(kc == 0), stop=(kc == 15)))
            P.wait("tensor", sACT, vv)
            sq_pe[kc] = P.op("tensor", lambda e, kc=kc: e.matmul(
                bk_q, c.ones_col_b, sq[kc % 2], start=(kc == 0), stop=(kc == 15)),
                inc=(sPE, 1))
        v_stats = sq_pe[15]

        if router is not None:
            ps_wait(c, P, "tensor", 0, halves=(sub,))
            for kc in range(16):
                P.op("tensor", lambda e, kc=kc, sub=sub: e.matmul(
                    c.PS[0][0:8, sub * 512:(sub + 1) * 512],
                    router["rw_r"][:, kc * 8:(kc + 1) * 8],
                    xw[:, kc, :], start=(kc == 0), stop=False))

        P.wait("scalar", sPE, v_stats)
        va = P.op("scalar", lambda e: e.activation(row_m, bk_s, AF.Copy, scale=1.0 / 2048.0),
                  inc=(sACT, 1))
        P.wait("scalar", sACT, va)
        vb = P.op("scalar", lambda e: e.activation(row_v, row_m, AF.Square), inc=(sACT, 1))
        P.wait("vector", sACT, vb)
        vc_ = P.op("vector", lambda e: e.scalar_tensor_tensor(
            row_v, bk_q, 1.0 / 2048.0, row_v, ALU.mult, ALU.subtract), inc=(sDVE, 1))
        ps_set(c, 3, sDVE, vc_)
        P.wait("scalar", sDVE, vc_)
        vd = P.op("scalar", lambda e: e.activation(row_v, row_v, AF.Sqrt, bias=c.eps_col),
                  inc=(sACT, 1))
        P.wait("vector", sACT, vd)
        ve = P.op("vector", lambda e: e.reciprocal(row_v, row_v), inc=(sDVE, 1))

        P.wait("tensor", sACT, va)
        P.wait("tensor", sDVE, ve)
        ps_wait(c, P, "tensor", 2)
        P.op("tensor", lambda e: e.matmul(c.PS[2][:, 0:512], c.ones_row_f, row_m,
                                          start=True, stop=True))
        vg = P.op("tensor", lambda e: e.matmul(c.PS[2][:, 512:1024], c.ones_row_f, row_v,
                                               start=True, stop=True), inc=(sPE, 1))
        P.wait("scalar", sPE, vg)
        c.ring_bc.acquire(P, "scalar", 0)
        P.op("scalar", lambda e: e.activation(bcm, c.PS[2][:, 0:512], AF.Copy))
        vh = P.op("scalar", lambda e: e.activation(bcr, c.PS[2][:, 512:1024], AF.Copy),
                  inc=(sACT, 1))
        ps_set(c, 2, sACT, vh)

        vr1_sub = None
        if router is not None:
            vr1 = P.op("tensor", lambda e, sub=sub: e.matmul(
                c.PS[0][0:8, sub * 512:(sub + 1) * 512], router["negsw"],
                row_m, start=False, stop=True), inc=(sPE, 1))
            vr1_sub = vr1
            P.wait("vector", sPE, vr1)
            P.wait("vector", sACT, vh)
            vlg = P.op("vector", lambda e, sub=sub: e.tensor_tensor(
                router["out"][0:8, sub * 512:(sub + 1) * 512],
                c.PS[0][0:8, sub * 512:(sub + 1) * 512], bcr[0:8, :], ALU.mult),
                inc=(sDVE, 1))
            ps_set(c, 0, sDVE, vlg, halves=(sub,))

        P.wait("gpsimd", sACT, vh)
        P.wait("gpsimd", sPE, v_stats)
        t1_dve = {}
        pool_last = None
        for kc in range(16):
            if kc >= 1:
                P.wait("gpsimd", sDVE, t1_dve[kc - 1])
            vp = P.op("gpsimd", lambda e, kc=kc: e.tensor_tensor(
                t1s, xw[:, kc, :], bcm, ALU.subtract), inc=(sPOOL, 1))
            pool_last = vp
            P.wait("vector", sPOOL, vp)
            ytile = vbf(y_ar, y_off + kc * y_row, 2 * y_row)[:, y_col + sub * NB:y_col + (sub + 1) * NB]
            c.y_guard_wait(P, "vector")
            t1_dve[kc] = P.op("vector", lambda e, kc=kc, ytile=ytile: e.tensor_tensor(
                ytile, t1s, bcr, ALU.mult), inc=(sDVE, 1))
            if ylo_ar is not None:
                tmp = c.S[:, 0:512]
                if c.ylo_tmp_guard is not None:
                    P.wait("gpsimd", c.ylo_tmp_guard[0], c.ylo_tmp_guard[1])
                P.wait("gpsimd", sPOOL, vp)
                vtp = P.op("gpsimd", lambda e, kc=kc: e.tensor_tensor(
                    tmp, t1s, bcr, ALU.mult), inc=(sPOOL, 1))
                pool_last = vtp
                ylotile = vbf(ylo_ar, ylo_off + kc * y_row, 2 * y_row)[
                    :, y_col + sub * NB:y_col + (sub + 1) * NB]
                P.wait("vector", sPOOL, vtp)
                P.wait("vector", sDVE, t1_dve[kc])
                vlo = P.op("vector", lambda e, kc=kc, ylotile=ylotile, ytile=ytile: e.tensor_tensor(
                    ylotile, tmp, ytile, ALU.subtract), inc=(sDVE, 1))
                c.ylo_tmp_guard = (sDVE, vlo)
                t1_dve[kc] = vlo
        c.ring_xw.consumed(0, sPOOL, pool_last)
        if vr1_sub is not None:
            c.ring_xw.also_consumed(0, sPE, vr1_sub)
        c.ring_sq = Ring(1)
        c.ring_sq.release(0, sDVE, t1_dve[15])
        c.ring_bc.release(0, sDVE, t1_dve[15])
    return P.sems["sDVE"].n



def emit_l1_ffn(c, x_in_ap, x_out_ap, wg_ap, wu_ap, wd_ap, gs_ap, us_ap, ds_ap,
                load_scales):
    P = c.P
    sPE = P.sem("sPE"); sACT = P.sem("sACT"); sDVE = P.sem("sDVE")
    sSCL = P.sem("sSCL")

    y_off = 0
    act_off = 8192
    yv = lambda kc: vbf(c.B, y_off + kc * 512, 1024)
    actv = lambda ic: vbf(c.B, act_off + ic * 512, 1024)

    gs = vf(c.W, 4096, 64)
    us = vf(c.W, 4160, 64)
    dh = vf(c.W, 4224, 16)
    if load_scales:
        P.op("sync", lambda e: e.dma_start(out=gs, in_=gs_ap), inc=(sSCL, 16))
        P.op("sync", lambda e: e.dma_start(out=us, in_=us_ap), inc=(sSCL, 16))
        ds_tmp = vf(c.W, 4240, 16)
        v3 = P.op("sync", lambda e: e.dma_start(out=ds_tmp, in_=ds_ap), inc=(sSCL, 16))
        P.wait("vector", sSCL, v3)
        c.l1_dh_v = P.op("vector", lambda e: e.tensor_scalar(
            dh, ds_tmp, 0.5, None, ALU.mult), inc=(sDVE, 1))
        P.wait("scalar", sSCL, v3)
    v_scales = c.l1_dh_v

    gslot = [vbf(c.W, 0, 2048), vbf(c.W, 1024, 2048)]
    uslot = [vbf(c.W, 2048, 2048), vbf(c.W, 3072, 2048)]
    dslot = gslot
    ring_g, ring_u, ring_d = c.ring_g, c.ring_u, c.ring_g

    ysum_b = c.R[0:1, 0:512].bitcast(BF16)
    actsum_b = c.R[0:1, 512:1024].bitcast(BF16)

    act_war = [None] * 16
    silu_sl = [vf(c.S, 0, 1024), vf(c.S, 1024, 1024)]
    silu_guard = [None, None]
    down_pe_last = None

    for th in range(2):
        t0 = th * TH
        v_y = emit_ln(c, x_in_ap, t0, TH, c.B, y_off, split4=(th == 0))

        P.wait("tensor", sDVE, v_y)
        ps_wait(c, P, "tensor", 3)
        vys = None
        for tt in range(2):
            for kc in range(16):
                vys = P.op("tensor", lambda e, kc=kc, tt=tt: e.matmul(
                    c.PS[3][0:1, tt * 512:(tt + 1) * 512], c.ones_col_b,
                    yv(kc)[:, tt * 512:(tt + 1) * 512], start=(kc == 0), stop=(kc == 15)),
                    inc=(sPE, 1) if (tt == 1 and kc == 15) else None)
        P.wait("scalar", sPE, vys)
        vysum = P.op("scalar", lambda e: e.activation(ysum_b, c.PS[3][0:1, 0:1024], AF.Copy),
                     inc=(sACT, 1))
        ps_set(c, 3, sACT, vysum)
        P.wait("tensor", sACT, vysum)
        if th == 0:
            P.wait("tensor", sDVE, v_scales)
            P.wait("vector", sDVE, v_scales)

        for ib in range(4):
            for ic in range(16):
                icg = ib * 16 + ic
                par = icg % 2
                tok_g = ring_g.load("gpsimd", icg, lambda e, icg=icg, par=par: e.dma_start(
                    out=gslot[par], in_=wg_ap[icg]))
                tok_u = ring_u.load("gpsimd", icg, lambda e, icg=icg, par=par: e.dma_start(
                    out=uslot[par], in_=wu_ap[icg]))

                pg, pu = c.PS[par], c.PS[2 + par]
                P.wait("tensor", *tok_g)
                ps_wait(c, P, "tensor", par)
                for kc in range(16):
                    for tt in range(2):
                        P.op("tensor", lambda e, kc=kc, tt=tt, par=par, pg=pg: e.matmul(
                            pg[:, tt * 512:(tt + 1) * 512],
                            gslot[par][:, kc * 128:(kc + 1) * 128],
                            yv(kc)[:, tt * 512:(tt + 1) * 512],
                            start=(kc == 0), stop=False))
                for tt in range(2):
                    v_g = P.op("tensor", lambda e, tt=tt, par=par, pg=pg: e.matmul(
                        pg[:, tt * 512:(tt + 1) * 512], c.m128_row_b,
                        ysum_b[0:1, tt * 512:(tt + 1) * 512], start=False, stop=True),
                        inc=(sPE, 1) if tt == 1 else None)
                ring_g.consumed(icg, sPE, v_g)

                P.wait("tensor", *tok_u)
                ps_wait(c, P, "tensor", 2 + par)
                for kc in range(16):
                    for tt in range(2):
                        P.op("tensor", lambda e, kc=kc, tt=tt, par=par, pu=pu: e.matmul(
                            pu[:, tt * 512:(tt + 1) * 512],
                            uslot[par][:, kc * 128:(kc + 1) * 128],
                            yv(kc)[:, tt * 512:(tt + 1) * 512],
                            start=(kc == 0), stop=False))
                for tt in range(2):
                    v_u = P.op("tensor", lambda e, tt=tt, par=par, pu=pu: e.matmul(
                        pu[:, tt * 512:(tt + 1) * 512], c.m128_row_b,
                        ysum_b[0:1, tt * 512:(tt + 1) * 512], start=False, stop=True),
                        inc=(sPE, 1) if tt == 1 else None)
                ring_u.consumed(icg, sPE, v_u)

                P.wait("scalar", sPE, v_g)
                if silu_guard[par] is not None:
                    P.wait("scalar", *silu_guard[par])
                vsil = P.op("scalar", lambda e, par=par, icg=icg, pg=pg: e.activation(
                    silu_sl[par], pg[:, :], AF.Silu, scale=gs[:, icg:icg + 1]),
                    inc=(sACT, 1))
                ps_set(c, par, sACT, vsil)

                P.wait("vector", sPE, v_u)
                P.wait("vector", sACT, vsil)
                if act_war[ic] is not None:
                    P.wait("vector", act_war[ic][0], act_war[ic][1])
                vact = P.op("vector", lambda e, ic=ic, par=par, icg=icg, pu=pu: e.scalar_tensor_tensor(
                    actv(ic), pu[:, :], us[:, icg:icg + 1], silu_sl[par],
                    ALU.mult, ALU.mult), inc=(sDVE, 1))
                ps_set(c, 2 + par, sDVE, vact)
                silu_guard[par] = (sDVE, vact)

            v_acts = P.sems["sDVE"].n
            P.wait("tensor", sDVE, v_acts)
            ps_wait(c, P, "tensor", 3)
            vas = None
            for tt in range(2):
                for ic in range(16):
                    vas = P.op("tensor", lambda e, ic=ic, tt=tt: e.matmul(
                        c.PS[3][0:1, tt * 512:(tt + 1) * 512], c.ones_col_b,
                        actv(ic)[:, tt * 512:(tt + 1) * 512], start=(ic == 0), stop=(ic == 15)),
                        inc=(sPE, 1) if (tt == 1 and ic == 15) else None)
            P.wait("scalar", sPE, vas)
            vas2 = P.op("scalar", lambda e: e.activation(actsum_b, c.PS[3][0:1, 0:1024], AF.Copy),
                        inc=(sACT, 1))
            ps_set(c, 3, sACT, vas2)
            P.wait("tensor", sACT, vas2)

            for oc in range(16):
                dpar = oc % 2
                tok_d = ring_d.load("gpsimd", oc, lambda e, oc=oc, ib=ib, dpar=dpar: e.dma_start(
                    out=dslot[dpar], in_=wd_ap[oc, :, ib * 2048:(ib + 1) * 2048]))
                P.wait("tensor", *tok_d)
                ps_wait(c, P, "tensor", dpar)
                for tt in range(2):
                    for ic in range(16):
                        P.op("tensor", lambda e, ic=ic, tt=tt, dpar=dpar: e.matmul(
                            c.PS[dpar][:, tt * 512:(tt + 1) * 512],
                            dslot[dpar][:, ic * 128:(ic + 1) * 128],
                            actv(ic)[:, tt * 512:(tt + 1) * 512],
                            start=(ic == 0), stop=False))
                    vdn = P.op("tensor", lambda e, tt=tt, dpar=dpar: e.matmul(
                        c.PS[dpar][:, tt * 512:(tt + 1) * 512], c.m128_row_b,
                        actsum_b[0:1, tt * 512:(tt + 1) * 512], start=False, stop=True),
                        inc=(sPE, 1) if tt == 1 else None)
                ring_d.consumed(oc, sPE, vdn)
                down_pe_last = vdn
                P.wait("vector", sPE, vdn)
                oa = vf(c.A, oc * 1024, 1024)
                c.oa_guard_wait(P, "vector", oc)
                if ib == 0:
                    vev = P.op("vector", lambda e, oa=oa, dpar=dpar: e.tensor_copy(
                        oa, c.PS[dpar][:, :]), inc=(sDVE, 1))
                else:
                    vev = P.op("vector", lambda e, oa=oa, dpar=dpar: e.tensor_add(
                        oa, oa, c.PS[dpar][:, :]), inc=(sDVE, 1))
                ps_set(c, dpar, sDVE, vev)
                c.oa_guard_clear(oc)
            for ic in range(16):
                act_war[ic] = (sPE, down_pe_last)

        xsl = [vf(c.B, 8192, 1024), vf(c.B, 9216, 1024)]
        xosl = [vf(c.B, 10240, 1024), vf(c.B, 11264, 1024)]
        P.wait("sync", sPE, P.sems["sPE"].n)
        vstl = [None, None]
        for oc in range(16):
            sl = oc % 2
            tok_x = c.ring_xep.load("sync", oc, lambda e, oc=oc, sl=sl, t0=t0: e.dma_start(
                out=xsl[sl], in_=x_in_ap[oc * 128:(oc + 1) * 128, t0:t0 + TH]))
            P.wait("vector", *tok_x)
            c.ring_xo.acquire(P, "vector", oc)
            vxo = P.op("vector", lambda e, oc=oc, sl=sl: e.scalar_tensor_tensor(
                xosl[sl], vf(c.A, oc * 1024, 1024), dh[:, oc:oc + 1], xsl[sl],
                ALU.mult, ALU.add), inc=(sDVE, 1))
            c.ring_xep.consumed(oc, sDVE, vxo)
            c.oa_guard_set(oc, sDVE, vxo)
            P.wait("sync", sDVE, vxo)
            stsem = c.st_sems[sl]
            vst = P.op("sync", lambda e, oc=oc, sl=sl, t0=t0: e.dma_start(
                out=x_out_ap[oc * 128:(oc + 1) * 128, t0:t0 + TH], in_=xosl[sl]),
                inc=(stsem, 16))
            vstl[sl] = vst
            c.ring_xo.release(oc, stsem, vst)
        act_war[4] = act_war[5] = (c.st_sems[0], vstl[0])
        act_war[6] = act_war[7] = (c.st_sems[1], vstl[1])
        for st in c.st_sems:
            P.wait("sync", st, st.n)
        if th == 1:
            P.barrier()
            c.reset_phase_guards()
    return



def emit_l0_ffn(c, x_in_ap, x_out_ap, w, load_w):
    P = c.P
    sPE = P.sem("sPE"); sACT = P.sem("sACT"); sDVE = P.sem("sDVE"); sPOOL = P.sem("sPOOL")
    sSCL = P.sem("sSCL")

    y_off = 0
    act_off = 8192
    yv = lambda kc: vbf(c.B, y_off + kc * 512, 1024)
    actv = lambda ic: vbf(c.B, act_off + ic * 512, 1024)

    rw_f = vf(c.W, 4096, 128)
    negsw = vf(c.W, 4224, 8)[0:1, :]
    sgw_b = vbf(c.W, 4232, 32)
    combW = vbf(c.W, 4248, 64)
    if load_w:
        P.op("sync", lambda e: e.dma_start(out=rw_f, in_=w["rw"]), inc=(sSCL, 16))
        sSGW = P.sem("sSGW")
        v2 = P.op("gpsimd", lambda e: e.dma_start(out=sgw_b[:, 0:16], in_=w["sg"]),
                  inc=(sSGW, 16))
        P.wait("tensor", sSCL, P.sems["sSCL"].n)
        P.wait("tensor", sSGW, v2)
        ps_wait(c, P, "tensor", 3)
        vsw = None
        for kc in range(16):
            vsw = P.op("tensor", lambda e, kc=kc: e.matmul(
                c.PS[3][0:1, 0:8], c.ones_col_f,
                rw_f[:, kc * 8:(kc + 1) * 8],
                start=(kc == 0), stop=(kc == 15)),
                inc=(sPE, 1) if kc == 15 else None)
        P.wait("scalar", sPE, vsw)
        vnw = P.op("scalar", lambda e: e.activation(negsw, c.PS[3][0:1, 0:8], AF.Copy,
                                                    scale=-1.0), inc=(sACT, 1))
        ps_set(c, 3, sACT, vnw)
        c.l0_negsw_v = vnw
    P.wait("tensor", sACT, c.l0_negsw_v)

    logitsT = c.S[0:8, 0:1024]
    router = {"rw_r": rw_f, "negsw": negsw, "out": logitsT}

    gslot = [vbf(c.W, 0, 2048), vbf(c.W, 1024, 2048)]
    uslot = [vbf(c.W, 2048, 2048), vbf(c.W, 3072, 2048)]
    dslot = gslot
    ring_g, ring_u, ring_d = c.ring_g, c.ring_u, c.ring_g

    sgrow_b = c.R[0:1, 512:1024].bitcast(BF16)
    crow_b = c.R[0:1, 0:512].bitcast(BF16)
    bc = vf(c.B, 13824, 1024)
    sbc = vf(c.B, 14848, 1024)

    for th in range(2):
        t0 = th * TH
        v_y = emit_ln(c, x_in_ap, t0, TH, c.B, y_off, router=router, split4=(th == 0))
        v_logits = P.sems["sDVE"].n

        P.wait("tensor", sDVE, v_y)
        ps_wait(c, P, "tensor", 3)
        vsg = None
        for tt in range(2):
            for kc in range(16):
                vsg = P.op("tensor", lambda e, kc=kc, tt=tt: e.matmul(
                    c.PS[3][0:1, tt * 512:(tt + 1) * 512], sgw_b[:, kc:kc + 1],
                    yv(kc)[:, tt * 512:(tt + 1) * 512], start=(kc == 0), stop=(kc == 15)),
                    inc=(sPE, 1) if (tt == 1 and kc == 15) else None)
        P.wait("scalar", sPE, vsg)
        vsgr = P.op("scalar", lambda e: e.activation(sgrow_b, c.PS[3][0:1, 0:1024],
                                                     AF.Sigmoid), inc=(sACT, 1))
        ps_set(c, 3, sACT, vsgr)

        ltm = vf(c.C, 4096, 64)
        P.wait("tensor", sDVE, v_logits)
        ps_wait(c, P, "tensor", 3)
        vtr = None
        for i in range(8):
            vtr = P.op("tensor", lambda e, i=i: e.transpose(
                c.PS[3][:, i * 8:(i + 1) * 8].bitcast(F32),
                logitsT[:, i * 128:(i + 1) * 128], c.idf[0:8, 0:8]),
                inc=(sPE, 1) if i == 7 else None)
        P.wait("vector", sPE, vtr)
        P.wait("vector", sPOOL, P.sems["sPOOL"].n)
        vltm = P.op("vector", lambda e: e.tensor_copy(ltm, c.PS[3][:, 0:64]), inc=(sDVE, 1))
        ps_set(c, 3, sDVE, vltm)

        m1 = vf(c.C, 4160, 1)
        nm1 = vf(c.C, 4161, 1)
        m2 = vf(c.C, 4162, 1)
        ssum = vf(c.C, 4163, 1)
        rsum = vf(c.C, 4164, 1)
        eq1 = vf(c.C, 4168, 8)
        msk = vf(c.C, 4176, 8)
        e1t = vf(c.C, 4184, 8)
        wv = vf(c.C, 4192, 8)
        for i in range(8):
            lt = ltm[:, i * 8:(i + 1) * 8]
            P.wait("vector", sDVE, P.sems["sDVE"].n)
            v1 = P.op("vector", lambda e, lt=lt: e.reduce_max(m1, lt, axis=AX), inc=(sDVE, 1))
            v1b = P.op("vector", lambda e, lt=lt: e.reduce_max(nm1, lt, axis=AX, negate=True),
                       inc=(sDVE, 1))
            P.wait("vector", sDVE, v1b)
            v2_ = P.op("vector", lambda e, lt=lt: e.tensor_scalar(
                eq1, lt, m1, None, ALU.is_ge), inc=(sDVE, 1))
            P.wait("vector", sDVE, v2_)
            v3_ = P.op("vector", lambda e, lt=lt: e.scalar_tensor_tensor(
                msk, eq1, -1e30, lt, ALU.mult, ALU.add), inc=(sDVE, 1))
            P.wait("vector", sDVE, v3_)
            v4_ = P.op("vector", lambda e: e.reduce_max(m2, msk, axis=AX), inc=(sDVE, 1))
            P.wait("scalar", sDVE, v1b)
            v5_ = P.op("scalar", lambda e, lt=lt: e.activation(
                e1t, lt, AF.Exp, bias=nm1), inc=(sACT, 1))
            P.wait("vector", sDVE, v4_)
            P.wait("vector", sACT, v5_)
            v6_ = P.op("vector", lambda e, lt=lt: e.tensor_scalar(
                wv, lt, m2, None, ALU.is_ge), inc=(sDVE, 1))
            P.wait("vector", sDVE, v6_)
            v7_ = P.op("vector", lambda e: e.tensor_mul(wv, wv, e1t), inc=(sDVE, 1))
            P.wait("vector", sDVE, v7_)
            v8_ = P.op("vector", lambda e: e.reduce_sum(ssum, wv, axis=AX), inc=(sDVE, 1))
            P.wait("vector", sDVE, v8_)
            v9_ = P.op("vector", lambda e: e.reciprocal(rsum, ssum), inc=(sDVE, 1))
            P.wait("vector", sDVE, v9_)
            va_ = P.op("vector", lambda e, i=i: e.tensor_scalar(
                combW[:, i * 8:(i + 1) * 8], wv, rsum, None, ALU.mult), inc=(sDVE, 1))
            v_e1t_act = v5_
        v_comb = P.sems["sDVE"].n
        c.ring_xw.also_consumed(0, sDVE, v_comb)
        c.ring_xw.also_consumed(0, sACT, v_e1t_act)

        def piece_weights(p):
            if p < 4:
                q = p
                return (lambda i: w["shg"][11 * q + i],
                        lambda i: w["shu"][11 * q + i],
                        lambda oc: w["shd"][oc, :, 1408 * q:1408 * (q + 1)])
            e_ = p - 4
            return (lambda i: w["gu"][e_, i],
                    lambda i: w["gu"][e_, 11 + i],
                    lambda oc: w["dn"][e_, oc])

        silu_sl = [vf(c.S, 0, 1024), vf(c.S, 1024, 1024)]
        silu_guard = [None, None]
        sbc_guard = None
        act_war = [None] * 11
        down_pe_last = None

        for p in range(12):
            gap, uap, dap = piece_weights(p)
            if p == 0:
                P.wait("tensor", sACT, vsgr)
                P.wait("tensor", sDVE, v_comb)
                ps_wait(c, P, "tensor", 2)
                vbcm = None
                for tt in range(2):
                    vbcm = P.op("tensor", lambda e, tt=tt: e.matmul(
                        c.PS[2][:, tt * 512:(tt + 1) * 512], c.ones_row_b,
                        sgrow_b[0:1, tt * 512:(tt + 1) * 512], start=True, stop=True),
                        inc=(sPE, 1) if tt == 1 else None)
                P.wait("scalar", sPE, vbcm)
                P.wait("scalar", sDVE, v_comb)
                vbc = P.op("scalar", lambda e: e.activation(bc, c.PS[2][:, :], AF.Copy,
                                                            scale=0.5), inc=(sACT, 1))
                ps_set(c, 2, sACT, vbc)
                bc_ready = vbc
            elif p >= 4:
                e_ = p - 4
                ps_wait(c, P, "tensor", 3)
                vcr = None
                for i in range(8):
                    vcr = P.op("tensor", lambda e, i=i, e_=e_: e.matmul(
                        c.PS[3][0:1, i * 128:(i + 1) * 128],
                        combW[:, i * 8 + e_:i * 8 + e_ + 1], c.idb,
                        start=True, stop=True), inc=(sPE, 1) if i == 7 else None)
                P.wait("scalar", sPE, vcr)
                vcw = P.op("scalar", lambda e: e.activation(crow_b, c.PS[3][0:1, 0:1024],
                                                            AF.Copy), inc=(sACT, 1))
                ps_set(c, 3, sACT, vcw)
                P.wait("tensor", sACT, vcw)
                ps_wait(c, P, "tensor", 2)
                vbcm = None
                for tt in range(2):
                    vbcm = P.op("tensor", lambda e, tt=tt: e.matmul(
                        c.PS[2][:, tt * 512:(tt + 1) * 512], c.ones_row_b,
                        crow_b[0:1, tt * 512:(tt + 1) * 512], start=True, stop=True),
                        inc=(sPE, 1) if tt == 1 else None)
                P.wait("scalar", sPE, vbcm)
                if c.bc_pool_guard is not None:
                    P.wait("scalar", c.bc_pool_guard[0], c.bc_pool_guard[1])
                vbc = P.op("scalar", lambda e: e.activation(bc, c.PS[2][:, :], AF.Copy,
                                                            scale=0.5), inc=(sACT, 1))
                ps_set(c, 2, sACT, vbc)
                bc_ready = vbc

            for i in range(11):
                icg = p * 11 + i
                par = icg % 2
                tok_g = ring_g.load("gpsimd", icg, lambda e, par=par, a=gap(i): e.dma_start(
                    out=gslot[par], in_=a))
                tok_u = ring_u.load("gpsimd", icg, lambda e, par=par, a=uap(i): e.dma_start(
                    out=uslot[par], in_=a))
                pg, pu = c.PS[par], c.PS[2 + par]
                P.wait("tensor", *tok_g)
                ps_wait(c, P, "tensor", par)
                vgl = None
                for kc in range(16):
                    for tt in range(2):
                        vgl = P.op("tensor", lambda e, kc=kc, tt=tt, par=par, pg=pg: e.matmul(
                            pg[:, tt * 512:(tt + 1) * 512],
                            gslot[par][:, kc * 128:(kc + 1) * 128],
                            yv(kc)[:, tt * 512:(tt + 1) * 512],
                            start=(kc == 0), stop=(kc == 15)),
                            inc=(sPE, 1) if (kc == 15 and tt == 1) else None)
                v_g = vgl
                ring_g.consumed(icg, sPE, v_g)
                P.wait("tensor", *tok_u)
                ps_wait(c, P, "tensor", 2 + par)
                vul = None
                for kc in range(16):
                    for tt in range(2):
                        vul = P.op("tensor", lambda e, kc=kc, tt=tt, par=par, pu=pu: e.matmul(
                            pu[:, tt * 512:(tt + 1) * 512],
                            uslot[par][:, kc * 128:(kc + 1) * 128],
                            yv(kc)[:, tt * 512:(tt + 1) * 512],
                            start=(kc == 0), stop=(kc == 15)),
                            inc=(sPE, 1) if (kc == 15 and tt == 1) else None)
                v_u = vul
                ring_u.consumed(icg, sPE, v_u)

                P.wait("scalar", sPE, v_g)
                if silu_guard[par] is not None:
                    P.wait("scalar", *silu_guard[par])
                vsil = P.op("scalar", lambda e, par=par, pg=pg: e.activation(
                    silu_sl[par], pg[:, :], AF.Silu), inc=(sACT, 1))
                ps_set(c, par, sACT, vsil)

                P.wait("gpsimd", sACT, vsil)
                P.wait("gpsimd", sACT, bc_ready)
                if sbc_guard is not None:
                    P.wait("gpsimd", sbc_guard[0], sbc_guard[1])
                vsbc = P.op("gpsimd", lambda e, par=par: e.tensor_tensor(
                    sbc, silu_sl[par], bc, ALU.mult), inc=(sPOOL, 1))
                c.bc_pool_guard = (sPOOL, vsbc)
                silu_guard[par] = (sPOOL, vsbc)

                P.wait("vector", sPE, v_u)
                P.wait("vector", sPOOL, vsbc)
                if act_war[i] is not None:
                    P.wait("vector", act_war[i][0], act_war[i][1])
                vact = P.op("vector", lambda e, i=i, pu=pu: e.tensor_tensor(
                    actv(i), pu[:, :], sbc, ALU.mult), inc=(sDVE, 1))
                ps_set(c, 2 + par, sDVE, vact)
                sbc_guard = (sDVE, vact)
            v_acts = P.sems["sDVE"].n

            P.wait("tensor", sDVE, v_acts)
            for oc in range(16):
                dpar = oc % 2
                tok_d = ring_d.load("gpsimd", oc, lambda e, a=dap(oc), dpar=dpar: e.dma_start(
                    out=dslot[dpar][:, 0:1408], in_=a))
                P.wait("tensor", *tok_d)
                ps_wait(c, P, "tensor", dpar)
                vdn = None
                for tt in range(2):
                    for ic in range(11):
                        vdn = P.op("tensor", lambda e, ic=ic, tt=tt, dpar=dpar: e.matmul(
                            c.PS[dpar][:, tt * 512:(tt + 1) * 512],
                            dslot[dpar][:, ic * 128:(ic + 1) * 128],
                            actv(ic)[:, tt * 512:(tt + 1) * 512],
                            start=(ic == 0), stop=(ic == 10)),
                            inc=(sPE, 1) if (tt == 1 and ic == 10) else None)
                ring_d.consumed(oc, sPE, vdn)
                down_pe_last = vdn
                P.wait("vector", sPE, vdn)
                oa = vf(c.A, oc * 1024, 1024)
                c.oa_guard_wait(P, "vector", oc)
                if p == 0:
                    vev = P.op("vector", lambda e, oa=oa, dpar=dpar: e.tensor_copy(
                        oa, c.PS[dpar][:, :]), inc=(sDVE, 1))
                else:
                    vev = P.op("vector", lambda e, oa=oa, dpar=dpar: e.tensor_add(
                        oa, oa, c.PS[dpar][:, :]), inc=(sDVE, 1))
                ps_set(c, dpar, sDVE, vev)
                c.oa_guard_clear(oc)
            for i in range(11):
                act_war[i] = (sPE, down_pe_last)

        xsl = [vf(c.B, 0, 1024), vf(c.B, 1024, 1024)]
        xosl = [vf(c.B, 2048, 1024), vf(c.B, 3072, 1024)]
        P.wait("sync", sPE, P.sems["sPE"].n)
        for oc in range(16):
            sl = oc % 2
            tok_x = c.ring_xep.load("sync", oc, lambda e, oc=oc, sl=sl, t0=t0: e.dma_start(
                out=xsl[sl], in_=x_in_ap[oc * 128:(oc + 1) * 128, t0:t0 + TH]))
            P.wait("vector", *tok_x)
            c.ring_xo.acquire(P, "vector", oc)
            vxo = P.op("vector", lambda e, oc=oc, sl=sl: e.tensor_add(
                xosl[sl], vf(c.A, oc * 1024, 1024), xsl[sl]), inc=(sDVE, 1))
            c.ring_xep.consumed(oc, sDVE, vxo)
            c.oa_guard_set(oc, sDVE, vxo)
            P.wait("sync", sDVE, vxo)
            stsem = c.st_sems[sl]
            vst = P.op("sync", lambda e, oc=oc, sl=sl, t0=t0: e.dma_start(
                out=x_out_ap[oc * 128:(oc + 1) * 128, t0:t0 + TH], in_=xosl[sl]),
                inc=(stsem, 16))
            c.ring_xo.release(oc, stsem, vst)
        for st in c.st_sems:
            P.wait("sync", st, st.n)
        if th == 1:
            P.barrier()
            c.reset_phase_guards()
        else:
            P.wait("sync", P.sem("sDVE"), P.sems["sDVE"].n)
            for st in c.st_sems:
                P.wait("vector", st, st.n)
        c.bc_pool_guard = None
    return


def prep_l0_ffn(router_weight, gate_up_proj, down_proj, sh_gate_w, sh_up_w,
                sh_down_w, shared_gate_w):
    d = {}
    d["l0_shg"] = lhsT_tiles(np.ascontiguousarray(sh_gate_w.T))
    d["l0_shu"] = lhsT_tiles(np.ascontiguousarray(sh_up_w.T))
    d["l0_shd"] = lhsT_tiles(np.ascontiguousarray(sh_down_w.T))
    d["l0_gu"] = np.stack([lhsT_tiles(np.ascontiguousarray(gate_up_proj[e].T))
                           for e in range(8)])
    d["l0_dn"] = np.stack([lhsT_tiles(np.ascontiguousarray(down_proj[e].T))
                           for e in range(8)])
    d["l0_rw"] = skinny_lhsT(np.ascontiguousarray(router_weight.T)).reshape(128, 128)
    d["l0_sg"] = skinny_lhsT(np.ascontiguousarray(shared_gate_w.T)).reshape(128, 16)
    return d


def declare_l0_ffn(nc):
    a = {}
    a["shg"] = nc.dram_tensor("l0_shg", [44, 128, 2048], F32, kind="ExternalInput").ap()
    a["shu"] = nc.dram_tensor("l0_shu", [44, 128, 2048], F32, kind="ExternalInput").ap()
    a["shd"] = nc.dram_tensor("l0_shd", [16, 128, 5632], F32, kind="ExternalInput").ap()
    a["gu"] = nc.dram_tensor("l0_gu", [8, 22, 128, 2048], F32, kind="ExternalInput").ap()
    a["dn"] = nc.dram_tensor("l0_dn", [8, 16, 128, 1408], F32, kind="ExternalInput").ap()
    a["rw"] = nc.dram_tensor("l0_rw", [128, 128], F32, kind="ExternalInput").ap()
    a["sg"] = nc.dram_tensor("l0_sg", [128, 16], F32, kind="ExternalInput").ap()
    return a



SCALE_L0 = 1.0 / float(np.sqrt(128.0))


def emit_l0_attn(c, x_in_ap, x_out_ap, a, load_w):
    P = c.P
    sPE = P.sem("sPE"); sACT = P.sem("sACT"); sDVE = P.sem("sDVE"); sPOOL = P.sem("sPOOL")
    sLA = P.sem("sLA")
    sLS = P.sem("sLS")

    basis_b = [vbf(c.A, x * 512, 1024) for x in range(4)]
    wT = [vbf(c.A, 2048 + x * 1024, 2048)[0:64, :] for x in range(4)]
    res = [vbf(c.A, 6144 + x * 1024, 2048)[0:64, :] for x in range(3)]
    res_o = vbf(c.A, 6144, 2048)[0:64, :]
    stg1 = vf(c.A, 9216, 2048)[0:64, :]
    stg2 = vf(c.A, 11264, 2048)[0:64, :]
    stg3 = vf(c.A, 13312, 2048)[0:64, :]

    k_h = c.FR[:, 0:2048]
    v_h = vbf(c.C, 2048, 2048)
    q_blk = c.FR[:, 2048:2560]
    probsT = vbf(c.W, 0, 8192)
    probsT3 = probsT.rearrange("p (j n) -> p j n", j=16)
    m_a2 = [vf(c.C, 7424, 1), vf(c.C, 7426, 1)]
    m_b2 = [vf(c.C, 7425, 1), vf(c.C, 7427, 1)]
    mbc = vf(c.C, 0, 512)
    rbc = vf(c.C, 512, 512)
    tmpx = [vf(c.C, 1024, 512), vf(c.C, 1536, 512)]
    mrow = vf(c.C, 3072, 512)[0:1, :]
    drow = vf(c.C, 3584, 512)[0:1, :]
    rrow = vf(c.C, 4096, 512)[0:1, :]
    y_off = 0
    yv = lambda kc: vbf(c.B, y_off + kc * 1024, 2048)
    ao = lambda h: vbf(c.B, h * 1024, 2048)

    if load_w:
        for x in range(4):
            P.op("gpsimd", lambda e, x=x: e.dma_start(
                out=basis_b[x], in_=a["basis"][x].rearrange("p k h -> p (k h)")),
                inc=(sLA, 16))
    vprev_dve = None
    for x in range(4):
        tok1 = c.ring_stg1.load("sync", x, lambda e, x=x: e.dma_start(out=stg1, in_=a["phase"][x]))
        tok2 = c.ring_stg2.load("sync", x, lambda e, x=x: e.dma_start(out=stg2, in_=a["amp"][x]))
        P.wait("scalar", *tok1)
        if vprev_dve is not None:
            P.wait("scalar", sDVE, vprev_dve)
        vsin = P.op("scalar", lambda e: e.activation(stg3, stg1, AF.Sin,
                                                     bias=c.pi2_col[0:64, :]), inc=(sACT, 1))
        c.ring_stg1.consumed(x, sACT, vsin)
        P.wait("vector", *tok2)
        P.wait("vector", sACT, vsin)
        vw = P.op("vector", lambda e, x=x: e.tensor_tensor(wT[x], stg3, stg2, ALU.mult),
                  inc=(sDVE, 1))
        c.ring_stg2.consumed(x, sDVE, vw)
        vprev_dve = vw
    v_wT = vprev_dve
    P.wait("tensor", sLA, P.sems["sLA"].n)
    P.wait("tensor", sDVE, v_wT)

    v_y = None
    for half in range(2):
        emit_ln(c, x_in_ap, half * TH, TH, c.B, y_off, y_row=1024, y_col=half * TH,
                split4=(half == 0))
    v_y = P.sems["sDVE"].n

    P.wait("tensor", sDVE, v_y)
    for x in range(3):
        for tt in range(4):
            hh = tt % 2
            ps_wait(c, P, "tensor", 3, halves=(hh,))
            vm = None
            for kc in range(16):
                vm = P.op("tensor", lambda e, x=x, kc=kc, tt=tt, hh=hh: e.matmul(
                    c.PS[3][0:64, hh * 512:(hh + 1) * 512],
                    basis_b[x][:, kc * 64:(kc + 1) * 64],
                    yv(kc)[:, tt * 512:(tt + 1) * 512],
                    start=(kc == 0), stop=(kc == 15)),
                    inc=(sPE, 1) if kc == 15 else None)
            P.wait("scalar", sPE, vm)
            vev = P.op("scalar", lambda e, x=x, tt=tt, hh=hh: e.activation(
                res[x][:, tt * 512:(tt + 1) * 512], c.PS[3][0:64, hh * 512:(hh + 1) * 512],
                AF.Copy), inc=(sACT, 1))
            ps_set(c, 3, sACT, vev, halves=(hh,))
    v_res = P.sems["sACT"].n

    k_guard = None
    v_guard = None
    q_guard = None
    pbf_guard = None
    pT_guard = None
    for h in range(16):
        P.wait("tensor", sACT, v_res)
        for tt in range(4):
            hh = tt % 2
            ps_wait(c, P, "tensor", 3, halves=(hh,))
            if k_guard is not None and tt < 2:
                P.wait("tensor", k_guard[0], k_guard[1])
            vm = P.op("tensor", lambda e, h=h, tt=tt, hh=hh: e.matmul(
                c.PS[3][:, hh * 512:(hh + 1) * 512], wT[1][:, h * 128:(h + 1) * 128],
                res[1][:, tt * 512:(tt + 1) * 512], start=True, stop=True), inc=(sPE, 1))
            P.wait("scalar", sPE, vm)
            vev = P.op("scalar", lambda e, tt=tt, hh=hh: e.activation(
                k_h[:, tt * 512:(tt + 1) * 512], c.PS[3][:, hh * 512:(hh + 1) * 512],
                AF.Copy), inc=(sACT, 1))
            ps_set(c, 3, sACT, vev, halves=(hh,))
        v_kh = P.sems["sACT"].n
        for grp in range(4):
            hh = grp % 2
            ps_wait(c, P, "tensor", 3, halves=(hh,))
            if v_guard is not None and grp < 2:
                P.wait("tensor", v_guard[0], v_guard[1])
            vm = None
            for j in range(4):
                tc = grp * 4 + j
                vm = P.op("tensor", lambda e, h=h, tc=tc, j=j, hh=hh: e.matmul(
                    c.PS[3][:, hh * 512 + j * 128: hh * 512 + (j + 1) * 128],
                    res[2][:, tc * 128:(tc + 1) * 128],
                    wT[2][:, h * 128:(h + 1) * 128],
                    start=True, stop=True), inc=(sPE, 1) if j == 3 else None)
            P.wait("scalar", sPE, vm)
            vev = P.op("scalar", lambda e, grp=grp, hh=hh: e.activation(
                v_h[:, grp * 512:(grp + 1) * 512], c.PS[3][:, hh * 512:(hh + 1) * 512],
                AF.Copy), inc=(sACT, 1))
            ps_set(c, 3, sACT, vev, halves=(hh,))
        v_vh = P.sems["sACT"].n

        for qb in range(4):
            hh = 0
            ps_wait(c, P, "tensor", 3, halves=(hh,))
            vm = P.op("tensor", lambda e, h=h, qb=qb: e.matmul(
                c.PS[3][:, 0:512], wT[0][:, h * 128:(h + 1) * 128],
                res[0][:, qb * 512:(qb + 1) * 512], start=True, stop=True), inc=(sPE, 1))
            P.wait("scalar", sPE, vm)
            vq = P.op("scalar", lambda e: e.activation(q_blk, c.PS[3][:, 0:512], AF.Copy),
                      inc=(sACT, 1))
            ps_set(c, 3, sACT, vq, halves=(hh,))
            P.wait("tensor", sACT, vq)
            P.wait("tensor", sACT, v_kh)

            for qc in range(4):
                par = qc % 2
                pa, pb = 0, 1
                ma, mb = m_a2[par], m_b2[par]
                ps_wait(c, P, "tensor", pa)
                ps_wait(c, P, "tensor", pb)
                vsc = None
                for kt in range(4):
                    pst = pa if kt < 2 else pb
                    vsc = P.op("tensor", lambda e, qc=qc, kt=kt, pst=pst: e.matmul(
                        c.PS[pst][:, (kt % 2) * 512:(kt % 2 + 1) * 512],
                        q_blk[:, qc * 128:(qc + 1) * 128],
                        k_h[:, kt * 512:(kt + 1) * 512], start=True, stop=True),
                        inc=(sPE, 1) if kt == 3 else None)
                P.wait("vector", sPE, vsc)
                if c.stat_guard2[par] is not None:
                    P.wait("vector", c.stat_guard2[par][0], c.stat_guard2[par][1])
                vr1 = P.op("vector", lambda e, pa=pa, ma=ma: e.reduce_max(
                    ma, c.PS[pa][:, :], axis=AX), inc=(sDVE, 1))
                vr2 = P.op("vector", lambda e, pb=pb, mb=mb: e.reduce_max(
                    mb, c.PS[pb][:, :], axis=AX), inc=(sDVE, 1))
                ps_set(c, pa, sDVE, vr2)
                ps_set(c, pb, sDVE, vr2)
                P.wait("vector", sDVE, vr2)
                vr3 = P.op("vector", lambda e, ma=ma, mb=mb: e.tensor_max(ma, ma, mb),
                           inc=(sDVE, 1))
                c.stat_guard2[par] = (sDVE, vr3)
                P.wait("tensor", sDVE, vr3)
                if qc == 0:
                    ps_wait(c, P, "tensor", 3, halves=(0,))
                vmr = P.op("tensor", lambda e, qc=qc, ma=ma: e.matmul(
                    c.PS[3][0:1, qc * 128:(qc + 1) * 128], ma, c.idf,
                    start=(qc == 0), stop=(qc == 3)), inc=(sPE, 1))
                if qc == 3:
                    ps_set(c, 3, sPE, vmr, halves=(0,))
            P.wait("scalar", sPE, vmr)
            vmrow = P.op("scalar", lambda e: e.activation(mrow, c.PS[3][0:1, 0:512],
                                                          AF.Copy, scale=SCALE_L0),
                         inc=(sACT, 1))
            P.wait("tensor", sACT, vmrow)
            ps_wait(c, P, "tensor", 3, halves=(0,))
            vbcm_ = P.op("tensor", lambda e: e.matmul(
                c.PS[3][:, 0:512], c.ones_row_f, mrow, start=True, stop=True),
                inc=(sPE, 1))
            P.wait("scalar", sPE, vbcm_)
            if c.mbc_guard is not None:
                P.wait("scalar", c.mbc_guard[0], c.mbc_guard[1])
            vmbc = P.op("scalar", lambda e: e.activation(mbc, c.PS[3][:, 0:512], AF.Copy),
                        inc=(sACT, 1))
            ps_set(c, 3, sACT, vmbc, halves=(0,))

            P.wait("vector", sACT, vmbc)
            tx_guard = [None, None]
            vexp = None
            for kc in range(16):
                sl2 = kc % 2
                ps_wait(c, P, "tensor", 2, halves=(sl2,))
                vst_ = P.op("tensor", lambda e, kc=kc, sl2=sl2: e.matmul(
                    c.PS[2][:, sl2 * 512:(sl2 + 1) * 512],
                    k_h[:, kc * 128:(kc + 1) * 128], q_blk,
                    start=True, stop=True), inc=(sPE, 1))
                P.wait("vector", sPE, vst_)
                if tx_guard[sl2] is not None:
                    P.wait("vector", tx_guard[sl2][0], tx_guard[sl2][1])
                vtx = P.op("vector", lambda e, kc=kc, sl2=sl2: e.scalar_tensor_tensor(
                    tmpx[sl2], c.PS[2][:, sl2 * 512:(sl2 + 1) * 512], SCALE_L0,
                    mbc, ALU.mult, ALU.subtract), inc=(sDVE, 1))
                ps_set(c, 2, sDVE, vtx, halves=(sl2,))
                P.wait("scalar", sDVE, vtx)
                if kc == 0 and c.pT_war is not None:
                    P.wait("scalar", c.pT_war[0], c.pT_war[1])
                vexp = P.op("scalar", lambda e, kc=kc, sl2=sl2: e.activation(
                    probsT3[:, kc, :], tmpx[sl2], AF.Exp), inc=(sACT, 1))
                tx_guard[sl2] = (sACT, vexp)
            v_probs = P.sems["sACT"].n

            P.wait("tensor", sACT, v_probs)
            ps_wait(c, P, "tensor", 3, halves=(0,))
            vdn = None
            for kc in range(16):
                vdn = P.op("tensor", lambda e, kc=kc: e.matmul(
                    c.PS[3][0:1, 0:512], c.ones_col_b, probsT3[:, kc, :],
                    start=(kc == 0), stop=(kc == 15)), inc=(sPE, 1) if kc == 15 else None)
            P.wait("scalar", sPE, vdn)
            vdr = P.op("scalar", lambda e: e.activation(drow, c.PS[3][0:1, 0:512], AF.Copy),
                       inc=(sACT, 1))
            P.wait("vector", sACT, vdr)
            vrr = P.op("vector", lambda e: e.reciprocal(rrow, drow), inc=(sDVE, 1))
            P.wait("tensor", sDVE, vrr)
            vbr = P.op("tensor", lambda e: e.matmul(
                c.PS[3][:, 0:512], c.ones_row_f, rrow, start=True, stop=True),
                inc=(sPE, 1))
            P.wait("scalar", sPE, vbr)
            if c.rbc_guard is not None:
                P.wait("scalar", c.rbc_guard[0], c.rbc_guard[1])
            vrbc = P.op("scalar", lambda e: e.activation(rbc, c.PS[3][:, 0:512], AF.Copy),
                        inc=(sACT, 1))
            ps_set(c, 3, sACT, vrbc, halves=(0,))
            c.mbc_guard = None

            P.wait("tensor", sACT, v_vh)
            ps_wait(c, P, "tensor", 3, halves=(1,))
            vpv = None
            for kc in range(16):
                vpv = P.op("tensor", lambda e, kc=kc: e.matmul(
                    c.PS[3][:, 512:1024], v_h[:, kc * 128:(kc + 1) * 128],
                    probsT3[:, kc, :], start=(kc == 0), stop=(kc == 15)),
                    inc=(sPE, 1) if kc == 15 else None)
            c.pT_war = (sPE, vpv)
            P.wait("vector", sPE, vpv)
            P.wait("vector", sACT, vrbc)
            vout = P.op("vector", lambda e, h=h, qb=qb: e.tensor_mul(
                ao(h)[:, qb * 512:(qb + 1) * 512], c.PS[3][:, 512:1024], rbc),
                inc=(sDVE, 1))
            ps_set(c, 3, sDVE, vout, halves=(1,))
            c.rbc_guard = (sDVE, vout)
    v_heads = P.sems["sDVE"].n

    P.wait("tensor", sDVE, v_heads)
    for tt in range(4):
        hh = tt % 2
        ps_wait(c, P, "tensor", 3, halves=(hh,))
        vm = None
        for kc in range(16):
            vm = P.op("tensor", lambda e, kc=kc, tt=tt, hh=hh: e.matmul(
                c.PS[3][0:64, hh * 512:(hh + 1) * 512],
                basis_b[3][:, kc * 64:(kc + 1) * 64],
                ao(kc)[:, tt * 512:(tt + 1) * 512],
                start=(kc == 0), stop=(kc == 15)), inc=(sPE, 1) if kc == 15 else None)
        P.wait("scalar", sPE, vm)
        vev = P.op("scalar", lambda e, tt=tt, hh=hh: e.activation(
            res_o[:, tt * 512:(tt + 1) * 512], c.PS[3][0:64, hh * 512:(hh + 1) * 512],
            AF.Copy), inc=(sACT, 1))
        ps_set(c, 3, sACT, vev, halves=(hh,))
    v_reso = P.sems["sACT"].n
    P.wait("tensor", sACT, v_reso)

    xsl = [vf(c.C, 0, 2048), vf(c.C, 4096, 2048)]
    xosl = [vf(c.C, 2048, 2048), vf(c.C, 6144, 2048)]
    P.wait("sync", sDVE, P.sems["sDVE"].n)
    P.wait("sync", sPE, P.sems["sPE"].n)
    for oc in range(16):
        sl = oc % 2
        tok_x = c.ring_xep.load("sync", oc, lambda e, oc=oc, sl=sl: e.dma_start(
            out=xsl[sl], in_=x_in_ap[oc * 128:(oc + 1) * 128, :]))
        vdv = None
        for tt in range(4):
            hh = tt % 2
            ps_wait(c, P, "tensor", 3, halves=(hh,))
            vm = P.op("tensor", lambda e, oc=oc, tt=tt, hh=hh: e.matmul(
                c.PS[3][:, hh * 512:(hh + 1) * 512], wT[3][:, oc * 128:(oc + 1) * 128],
                res_o[:, tt * 512:(tt + 1) * 512], start=True, stop=True), inc=(sPE, 1))
            P.wait("vector", sPE, vm)
            if tt == 0:
                P.wait("vector", *tok_x)
                c.ring_xo.acquire(P, "vector", oc)
            vdv = P.op("vector", lambda e, tt=tt, hh=hh, sl=sl: e.tensor_add(
                xosl[sl][:, tt * 512:(tt + 1) * 512], c.PS[3][:, hh * 512:(hh + 1) * 512],
                xsl[sl][:, tt * 512:(tt + 1) * 512]), inc=(sDVE, 1))
            ps_set(c, 3, sDVE, vdv, halves=(hh,))
        c.ring_xep.consumed(oc, sDVE, vdv)
        P.wait("sync", sDVE, vdv)
        stsem = c.st_sems[sl]
        vst = P.op("sync", lambda e, oc=oc, sl=sl: e.dma_start(
            out=x_out_ap[oc * 128:(oc + 1) * 128, :], in_=xosl[sl]), inc=(stsem, 16))
        c.ring_xo.release(oc, stsem, vst)
    for st in c.st_sems:
        P.wait("sync", st, st.n)
    P.barrier()
    c.reset_phase_guards()
    return


def prep_l0_attn(basis_q, phase_q, amp_q, basis_k, phase_k, amp_k,
                 basis_v, phase_v, amp_v, basis_o, phase_o, amp_o):
    d = {}
    bs = [basis_q, basis_k, basis_v, basis_o]
    ph = [phase_q, phase_k, phase_v, phase_o]
    am = [amp_q, amp_k, amp_v, amp_o]
    d["l0a_basis"] = np.stack([skinny_lhsT(np.ascontiguousarray(b.T)) for b in bs])
    d["l0a_phase"] = np.stack([np.ascontiguousarray(p.T) for p in ph])
    d["l0a_amp"] = np.stack([np.ascontiguousarray(a.T) for a in am])
    return d


def declare_l0_attn(nc):
    a = {}
    a["basis"] = nc.dram_tensor("l0a_basis", [4, 128, 16, 64], F32, kind="ExternalInput").ap()
    a["phase"] = nc.dram_tensor("l0a_phase", [4, 64, 2048], F32, kind="ExternalInput").ap()
    a["amp"] = nc.dram_tensor("l0a_amp", [4, 64, 2048], F32, kind="ExternalInput").ap()
    return a



SCALE_L1 = 1.0 / float(np.sqrt(128.0))


def emit_l1_attn(c, x_in_ap, x_out_ap, a, spill):
    P = c.P
    sPE = P.sem("sPE"); sACT = P.sem("sACT"); sDVE = P.sem("sDVE")
    sVSP = [P.sem("sVSPa"), P.sem("sVSPb")]
    sQSP = [P.sem("sQSPa"), P.sem("sQSPb")]
    sKSP = [P.sem("sKSPa"), P.sem("sKSPb")]

    y_off = 0
    yv = lambda kc: vbf(c.B, y_off + kc * 1024, 2048)
    ao = lambda dc: vbf(c.B, dc * 1024, 2048)

    wf32 = [vf(c.W, 0, 2048)]
    whi = vbf(c.W, 2048, 2048)
    wlo = vbf(c.W, 3072, 2048)
    probsT = vbf(c.A, 4096, 4096)
    probsT3 = probsT.rearrange("p (j n) -> p j n", j=16)
    ylo = lambda kc: vbf(c.A, kc * 1024, 2048)

    kslot4 = [c.FR[:, j * 512:(j + 1) * 512] for j in range(4)]
    qslot = [c.FR[:, 2048:2304], c.FR[:, 2304:2560]]
    vslot = [vbf(c.C, 512, 2048), vbf(c.C, 1536, 2048)]
    psum_blk = [vf(c.C, 2560, 2048), vf(c.C, 4608, 2048)]
    probs_bf = vbf(c.C, 6656, 2048)
    m_a2 = [vf(c.C, 7680, 1), vf(c.C, 7688, 1)]
    m_b2 = [vf(c.C, 7681, 1), vf(c.C, 7689, 1)]
    negm2 = [vf(c.C, 7682, 1), vf(c.C, 7690, 1)]
    d1_2 = [vf(c.C, 7683, 1), vf(c.C, 7691, 1)]
    d2_2 = [vf(c.C, 7684, 1), vf(c.C, 7692, 1)]
    rcp2 = [vf(c.C, 7686, 1), vf(c.C, 7694, 1)]
    exp_sb2 = [vf(c.A, 8192, 2048), vf(c.A, 10240, 2048)]

    for half in range(2):
        emit_ln(c, x_in_ap, half * TH, TH, c.B, y_off, y_row=1024, y_col=half * TH,
                ylo_ar=c.A, ylo_off=0, split4=(half == 0))
    v_y = P.sems["sDVE"].n
    P.wait("tensor", sDVE, v_y)

    wvs = vbf(c.C, 4096, 8192).rearrange("p (kc n) -> p kc n", kc=16)
    vtmp = [vbf(c.C, 0, 1024), vbf(c.C, 512, 1024)]
    for dr in range(4):
        toks = []
        for kc in range(16):
            toks.append(c.ring_wv.load("gpsimd", dr * 16 + kc, lambda e, kc=kc, dr=dr: e.dma_start(
                out=wvs[:, kc, :], in_=a["wv"][kc, :, dr * 512:(dr + 1) * 512])))
        for t_ in toks:
            P.wait("tensor", *t_)
        for tc in range(16):
            sl = tc % 2
            hh = tc % 2
            ps_wait(c, P, "tensor", 3, halves=(hh,))
            vm = None
            for kc in range(16):
                vm = P.op("tensor", lambda e, tc=tc, kc=kc, hh=hh: e.matmul(
                    c.PS[3][:, hh * 512:(hh + 1) * 512],
                    yv(kc)[:, tc * 128:(tc + 1) * 128],
                    wvs[:, kc, :],
                    start=(kc == 0), stop=(kc == 15)), inc=(sPE, 1) if kc == 15 else None)
            vlast_wv = vm
            P.wait("scalar", sPE, vm)
            c.ring_vtmp.acquire(P, "scalar", tc)
            vevl = P.op("scalar", lambda e, sl=sl, hh=hh: e.activation(
                vtmp[sl][:, 0:512], c.PS[3][:, hh * 512:(hh + 1) * 512],
                AF.Copy), inc=(sACT, 1))
            ps_set(c, 3, sACT, vevl, halves=(hh,))
            P.wait("sync", sACT, vevl)
            vst = P.op("sync", lambda e, tc=tc, dr=dr, sl=sl: e.dma_start(
                out=spill["v_sp"][tc, :, dr * 512:(dr + 1) * 512], in_=vtmp[sl][:, 0:512]),
                inc=(sVSP[sl], 16))
            c.ring_vtmp.release(tc, sVSP[sl], vst)
        for kc in range(16):
            c.ring_wv.consumed(dr * 16 + kc, sPE, vlast_wv)

    ktmp = [c.FR[:, 0:512], c.FR[:, 512:1024]]
    P.wait("scalar", sVSP[0], sVSP[0].n)
    P.wait("scalar", sVSP[1], sVSP[1].n)
    whilo_guard = None
    for proj in range(2):
        wap = a["wk"] if proj == 0 else a["wq"]
        sp = spill["k_sp"] if proj == 0 else spill["q_sp"]
        ssp = sKSP if proj == 0 else sQSP
        for h in range(16):
            sl = h % 2
            idx = proj * 16 + h
            tok_w = c.ring_wf.load("sync", idx, lambda e, wap=wap, h=h: e.dma_start(
                out=wf32[0], in_=wap[h]))
            P.wait("scalar", *tok_w)
            if whilo_guard is not None:
                P.wait("scalar", whilo_guard[0], whilo_guard[1])
            vhi = P.op("scalar", lambda e: e.activation(whi, wf32[0], AF.Copy),
                       inc=(sACT, 1))
            P.wait("vector", sACT, vhi)
            vlo_ = P.op("vector", lambda e: e.tensor_sub(wlo, wf32[0], whi),
                        inc=(sDVE, 1))
            P.wait("tensor", sDVE, vlo_)
            vevl = None
            vmlast = None
            for tt in range(4):
                hh = tt % 2
                ps_wait(c, P, "tensor", 3, halves=(hh,))
                vm = None
                for term in range(3):
                    ya = yv if term != 1 else ylo
                    wa = whi if term != 2 else wlo
                    for kc in range(16):
                        vm = P.op("tensor", lambda e, tt=tt, kc=kc, hh=hh, ya=ya, wa=wa,
                                  term=term: e.matmul(
                            c.PS[3][:, hh * 512:(hh + 1) * 512],
                            wa[:, kc * 128:(kc + 1) * 128],
                            ya(kc)[:, tt * 512:(tt + 1) * 512],
                            start=(term == 0 and kc == 0), stop=(term == 2 and kc == 15)),
                            inc=(sPE, 1) if (term == 2 and kc == 15) else None)
                vmlast = vm
                P.wait("scalar", sPE, vm)
                ksl = tt % 2
                c.ring_ktmp.acquire(P, "scalar", tt)
                vevl = P.op("scalar", lambda e, ksl=ksl, hh=hh: e.activation(
                    ktmp[ksl], c.PS[3][:, hh * 512:(hh + 1) * 512],
                    AF.Copy), inc=(sACT, 1))
                ps_set(c, 3, sACT, vevl, halves=(hh,))
                P.wait("gpsimd", sACT, vevl)
                vst = P.op("gpsimd", lambda e, h=h, tt=tt, ksl=ksl, sp=sp: e.dma_start(
                    out=sp[h][:, tt * 512:(tt + 1) * 512], in_=ktmp[ksl]), inc=(ssp[ksl], 16))
                c.ring_ktmp.release(tt, ssp[ksl], vst)
            c.ring_wf.consumed(idx, sDVE, vlo_)
            whilo_guard = (sPE, vmlast)
    for sm in sVSP + sQSP + sKSP:
        P.wait("sync", sm, sm.n)
        P.wait("gpsimd", sm, sm.n)
    v_projs_done = P.sems["sPE"].n

    c.ring_k.guard = [[(sPE, v_projs_done)] for _ in range(c.ring_k.n)]
    c.ring_q.guard = [[(sPE, v_projs_done)] for _ in range(c.ring_q.n)]
    c.ring_ktmp = Ring(2)
    c.ring_v.guard = [[(sPE, v_projs_done)] for _ in range(c.ring_v.n)]
    c.ring_wo.guard = [[(sPE, v_projs_done)] for _ in range(c.ring_wo.n)]
    c.pT_war = (sPE, v_projs_done)

    for qb in range(8):
        nvk = 2 * (qb + 1)
        for h in range(16):
            tok_q = c.ring_q.load("gpsimd", qb * 16 + h, lambda e, h=h, qb=qb: e.dma_start(
                out=qslot[h % 2], in_=spill["q_sp"][h][:, qb * 256:(qb + 1) * 256]))
            P.wait("tensor", *tok_q)
            nq512 = min((2 * (qb + 1) * 128 + 511) // 512, 4)
            ktoks = []
            for j in range(nq512):
                ktoks.append(c.ring_k.load("gpsimd", h * 4 + j, lambda e, h=h, j=j: e.dma_start(
                    out=kslot4[j], in_=spill["k_sp"][h][:, j * 512:(j + 1) * 512])))
            for t_ in ktoks:
                P.wait("tensor", *t_)
            vsc_last = None
            for qc2 in range(2):
                qi = qb * 2 + qc2
                span = (qi + 1) * 128
                pa, pb = (0, 1) if qc2 == 0 else (2, 3)
                m_a, m_b = m_a2[qc2], m_b2[qc2]
                negm, d1, d2, rcp = negm2[qc2], d1_2[qc2], d2_2[qc2], rcp2[qc2]
                exp_sb = exp_sb2[qc2]
                ps_wait(c, P, "tensor", pa)
                ps_wait(c, P, "tensor", pb)
                n512 = span // 512
                rem = (span % 512) // 128
                vsc = None
                for ks in range(n512):
                    pst, hh = (pa, ks % 2) if ks < 2 else (pb, ks % 2)
                    vsc = P.op("tensor", lambda e, h=h, qc2=qc2, ks=ks, pst=pst, hh=hh: e.matmul(
                        c.PS[pst][:, hh * 512:(hh + 1) * 512],
                        qslot[h % 2][:, qc2 * 128:(qc2 + 1) * 128],
                        kslot4[ks], start=True, stop=True),
                        inc=(sPE, 1))
                if rem:
                    ks = n512
                    pst, hh = (pa, ks % 2) if ks < 2 else (pb, ks % 2)
                    vsc = P.op("tensor", lambda e, h=h, qc2=qc2, ks=ks, rem=rem, pst=pst, hh=hh: e.matmul(
                        c.PS[pst][:, hh * 512:hh * 512 + rem * 128],
                        qslot[h % 2][:, qc2 * 128:(qc2 + 1) * 128],
                        kslot4[ks][:, 0:rem * 128], start=True, stop=True),
                        inc=(sPE, 1))
                vsc_last = vsc
                dpos = qi * 128
                dpst = pa if dpos < 1024 else pb
                doff = dpos % 1024
                P.wait("vector", sPE, vsc)
                if c.stat_guard2[qc2] is not None:
                    P.wait("vector", c.stat_guard2[qc2][0], c.stat_guard2[qc2][1])
                vmk = P.op("vector", lambda e, dpst=dpst, doff=doff: e.tensor_tensor(
                    c.PS[dpst][:, doff:doff + 128], c.PS[dpst][:, doff:doff + 128],
                    c.maskd, ALU.add), inc=(sDVE, 1))
                P.wait("vector", sDVE, vmk)
                la = min(span, 1024)
                v1 = P.op("vector", lambda e, pa=pa, la=la, m_a=m_a: e.reduce_max(
                    m_a, c.PS[pa][:, 0:la], axis=AX), inc=(sDVE, 1))
                if span > 1024:
                    lb = span - 1024
                    v2 = P.op("vector", lambda e, pb=pb, lb=lb, m_b=m_b: e.reduce_max(
                        m_b, c.PS[pb][:, 0:lb], axis=AX), inc=(sDVE, 1))
                    P.wait("vector", sDVE, v2)
                    v1 = P.op("vector", lambda e, m_a=m_a, m_b=m_b: e.tensor_max(
                        m_a, m_a, m_b), inc=(sDVE, 1))
                P.wait("vector", sDVE, v1)
                v3 = P.op("vector", lambda e, m_a=m_a, negm=negm: e.tensor_scalar(
                    negm, m_a, -SCALE_L1, None, ALU.mult), inc=(sDVE, 1))
                P.wait("scalar", sDVE, v3)
                if c.exp_guard2[qc2] is not None:
                    P.wait("scalar", c.exp_guard2[qc2][0], c.exp_guard2[qc2][1])
                vp = P.op("scalar", lambda e, pa=pa, la=la, negm=negm, d1=d1,
                          exp_sb=exp_sb: e.activation(
                    exp_sb[:, 0:la], c.PS[pa][:, 0:la], AF.Exp, bias=negm, scale=SCALE_L1,
                    accum_out=d1), inc=(sACT, 1))
                if span > 1024:
                    vp = P.op("scalar", lambda e, pb=pb, lb=lb, negm=negm, d2=d2,
                              exp_sb=exp_sb: e.activation(
                        exp_sb[:, 1024:1024 + lb], c.PS[pb][:, 0:lb], AF.Exp, bias=negm,
                        scale=SCALE_L1, accum_out=d2), inc=(sACT, 1))
                ps_set(c, pa, sACT, vp)
                ps_set(c, pb, sACT, vp)
                c.stat_guard2[qc2] = (sACT, vp)
                P.wait("vector", sACT, vp)
                if span > 1024:
                    v4 = P.op("vector", lambda e, d1=d1, d2=d2: e.tensor_add(
                        d1, d1, d2), inc=(sDVE, 1))
                    P.wait("vector", sDVE, v4)
                v5 = P.op("vector", lambda e, d1=d1, rcp=rcp: e.reciprocal(rcp, d1),
                          inc=(sDVE, 1))
                P.wait("vector", sDVE, v5)
                if h == 0:
                    vps = P.op("vector", lambda e, qc2=qc2, span=span, rcp=rcp,
                               exp_sb=exp_sb: e.tensor_scalar(
                        psum_blk[qc2][:, 0:span], exp_sb[:, 0:span], rcp, None, ALU.mult),
                        inc=(sDVE, 1))
                else:
                    vps = P.op("vector", lambda e, qc2=qc2, span=span, rcp=rcp,
                               exp_sb=exp_sb: e.scalar_tensor_tensor(
                        psum_blk[qc2][:, 0:span], exp_sb[:, 0:span], rcp,
                        psum_blk[qc2][:, 0:span], ALU.mult, ALU.add), inc=(sDVE, 1))
                c.exp_guard2[qc2] = (sDVE, vps)
            c.ring_q.consumed(qb * 16 + h, sPE, vsc_last)
            for j in range(nq512):
                c.ring_k.consumed(h * 4 + j, sPE, vsc_last)

        for qc2 in range(2):
            qi = qb * 2 + qc2
            span = (qi + 1) * 128
            pa = 0 if qc2 == 0 else 2
            if c.pbf_guard is not None:
                P.wait("vector", c.pbf_guard[0], c.pbf_guard[1])
            P.wait("vector", sDVE, P.sems["sDVE"].n)
            vcast = P.op("vector", lambda e, qc2=qc2, span=span: e.tensor_copy(
                probs_bf[:, 0:span], psum_blk[qc2][:, 0:span]), inc=(sDVE, 1))
            P.wait("tensor", sDVE, vcast)
            nk = qi + 1
            ps_wait(c, P, "tensor", pa)
            vtr = None
            for j in range(nk):
                vtr = P.op("tensor", lambda e, j=j, pa=pa: e.transpose(
                    c.PS[pa].bitcast(BF16)[:, (j % 8) * 128:((j % 8) + 1) * 128],
                    probs_bf[:, j * 128:(j + 1) * 128], c.idb),
                    inc=(sPE, 1) if (j % 8 == 7 or j == nk - 1) else None)
                if j % 8 == 7 or j == nk - 1:
                    base = (j // 8) * 8
                    cnt = j - base + 1
                    P.wait("vector", sPE, vtr)
                    if qc2 == 0 and base == 0 and c.pT_war is not None:
                        P.wait("vector", c.pT_war[0], c.pT_war[1])
                    vmv = P.op("vector", lambda e, base=base, cnt=cnt, qc2=qc2, pa=pa: e.tensor_copy(
                        probsT3[:, base:base + cnt, qc2 * 128:(qc2 + 1) * 128],
                        c.PS[pa].bitcast(BF16).rearrange("p (j n) -> p j n", j=16)[:, 0:cnt, :]),
                        inc=(sDVE, 1))
                    ps_set(c, pa, sDVE, vmv)
                    c.pbf_guard = (sPE, vtr)
                    if j < nk - 1:
                        ps_wait(c, P, "tensor", pa)
        vz = P.op("vector", lambda e, nvk=nvk: e.memset(
            probsT3[:, nvk - 1, 0:128], 0), inc=(sDVE, 1))

        v_pT = P.sems["sDVE"].n
        P.wait("tensor", sDVE, v_pT)
        pvtile = lambda d8: c.PS[d8 // 2][:, (d8 % 2) * 512:(d8 % 2) * 512 + 256]
        for dhalf in range(2):
            for i in range(4):
                ps_wait(c, P, "tensor", i)
            vpv = None
            for kc in range(nvk):
                tok_v = c.ring_v.load("sync", dhalf * 16 + kc, lambda e, kc=kc: e.dma_start(
                    out=vslot[kc % 2], in_=spill["v_sp"][kc]))
                P.wait("tensor", *tok_v)
                for d8 in range(8):
                    dc = dhalf * 8 + d8
                    vpv = P.op("tensor", lambda e, kc=kc, dc=dc, d8=d8: e.matmul(
                        pvtile(d8), vslot[kc % 2][:, dc * 128:(dc + 1) * 128],
                        probsT3[:, kc, :], start=(kc == 0), stop=(kc == nvk - 1),
                        skip_group_check=True),
                        inc=(sPE, 1) if d8 == 7 else None)
                c.ring_v.consumed(dhalf * 16 + kc, sPE, vpv)
            P.wait("scalar", sPE, vpv)
            for d8 in range(8):
                dc = dhalf * 8 + d8
                vout = P.op("scalar", lambda e, dc=dc, d8=d8, qb=qb: e.activation(
                    ao(dc)[:, qb * 256:(qb + 1) * 256], pvtile(d8), AF.Copy), inc=(sACT, 1))
            v_evac = P.sems["sACT"].n
            for i in range(4):
                ps_set(c, i, sACT, v_evac)
        c.pT_war = (sPE, P.sems["sPE"].n)
    v_attn = P.sems["sACT"].n

    if globals().get("DEBUG_L1A_STOP", False):
        sDBG = P.sem("sDBG")
        P.wait("gpsimd", sACT, v_attn)
        for dc in range(16):
            P.op("gpsimd", lambda e, dc=dc: e.dma_start(
                out=x_out_ap[dc * 128:(dc + 1) * 128, :], in_=ao(dc)), inc=(sDBG, 16))
        P.wait("gpsimd", sDBG, P.sems["sDBG"].n)
        P.barrier()
        c.reset_phase_guards()
        return

    xsl = [vf(c.C, 0, 2048), vf(c.C, 4096, 2048)]
    xosl = [vf(c.C, 2048, 2048), vf(c.C, 6144, 2048)]
    P.wait("sync", sDVE, P.sems["sDVE"].n)
    P.wait("sync", sPE, P.sems["sPE"].n)
    P.wait("tensor", sACT, v_attn)
    oslot = [whi, wlo]
    for oc in range(16):
        sl = oc % 2
        tok_w = c.ring_wo.load("gpsimd", oc, lambda e, oc=oc: e.dma_start(
            out=oslot[oc % 2], in_=a["wo"][oc]))
        tok_x = c.ring_xep.load("sync", oc, lambda e, oc=oc, sl=sl: e.dma_start(
            out=xsl[sl], in_=x_in_ap[oc * 128:(oc + 1) * 128, :]))
        P.wait("tensor", *tok_w)
        vdv = None
        for tt in range(4):
            hh = tt % 2
            ps_wait(c, P, "tensor", 3, halves=(hh,))
            vm = None
            for kc in range(16):
                vm = P.op("tensor", lambda e, oc=oc, tt=tt, kc=kc, hh=hh: e.matmul(
                    c.PS[3][:, hh * 512:(hh + 1) * 512],
                    oslot[oc % 2][:, kc * 128:(kc + 1) * 128],
                    ao(kc)[:, tt * 512:(tt + 1) * 512],
                    start=(kc == 0), stop=(kc == 15)), inc=(sPE, 1) if kc == 15 else None)
            if tt == 3:
                c.ring_wo.consumed(oc, sPE, vm)
            P.wait("vector", sPE, vm)
            if tt == 0:
                P.wait("vector", *tok_x)
                c.ring_xo.acquire(P, "vector", oc)
            vdv = P.op("vector", lambda e, tt=tt, hh=hh, sl=sl: e.tensor_add(
                xosl[sl][:, tt * 512:(tt + 1) * 512], c.PS[3][:, hh * 512:(hh + 1) * 512],
                xsl[sl][:, tt * 512:(tt + 1) * 512]), inc=(sDVE, 1))
            ps_set(c, 3, sDVE, vdv, halves=(hh,))
        c.ring_xep.consumed(oc, sDVE, vdv)
        P.wait("sync", sDVE, vdv)
        stsem = c.st_sems[sl]
        vst = P.op("sync", lambda e, oc=oc, sl=sl: e.dma_start(
            out=x_out_ap[oc * 128:(oc + 1) * 128, :], in_=xosl[sl]), inc=(stsem, 16))
        c.ring_xo.release(oc, stsem, vst)
    for st in c.st_sems:
        P.wait("sync", st, st.n)
    P.barrier()
    c.reset_phase_guards()
    return


def prep_l1_attn(W_q, W_k, W_v, W_o):
    return {"l1a_wq": lhsT_tiles(W_q), "l1a_wk": lhsT_tiles(W_k),
            "l1a_wv": np.ascontiguousarray(W_v.reshape(16, 128, 2048)),
            "l1a_wo": lhsT_tiles(W_o)}


def declare_l1_attn(nc):
    a = {}
    for n in ("wq", "wk", "wv", "wo"):
        a[n] = nc.dram_tensor(f"l1a_{n}", [16, 128, 2048], F32, kind="ExternalInput").ap()
    return a



def init_guards(c):
    P = c.P
    c.ring_xw = DmaRing(P, "xw", 1)
    c.ring_sq = Ring(1)
    c.ring_bc = Ring(1)
    c.ring_g = DmaRing(P, "wg", 2)
    c.ring_u = DmaRing(P, "wu", 2)
    c.ring_d = DmaRing(P, "wd", 2)
    c.ring_xep = DmaRing(P, "xep", 2)
    c.ring_stg1 = DmaRing(P, "stg1", 1)
    c.ring_stg2 = DmaRing(P, "stg2", 1)
    c.ring_wv = DmaRing(P, "wv", 16)
    c.ring_w = DmaRing(P, "wkqo", 2)
    c.ring_q = DmaRing(P, "qld", 2)
    c.ring_k = DmaRing(P, "kld", 4)
    c.ring_wo = DmaRing(P, "wo", 2)
    c.ring_wf = DmaRing(P, "wf", 1)
    c.ring_v = DmaRing(P, "vld", 2)
    c.ring_vtmp = Ring(2)
    c.ring_ktmp = Ring(2)
    c.exp_guard = None
    c.pbf_guard = None
    c.pT_war = None
    c.stat_guard = None
    c.stat_guard2 = [None, None]
    c.pbf_guard2 = [None, None]
    c.exp_guard2 = [None, None]
    c.mbc_guard = None
    c.rbc_guard = None
    c.ylo_tmp_guard = None
    c.ring_xo = Ring(2)
    c.st_sems = [P.sem("stA"), P.sem("stB")]
    c.oa_g = [None] * 16
    c.y_guard = None
    c.l1_dh_v = 0
    c.l0_negsw_v = 0
    c.bc_pool_guard = None


def _reset_phase_guards(c):
    for r in (c.ring_xw, c.ring_g, c.ring_u, c.ring_d, c.ring_xep, c.ring_stg1,
              c.ring_stg2, c.ring_wv, c.ring_w, c.ring_wo, c.ring_wf, c.ring_q, c.ring_k, c.ring_v):
        r.guard = [None] * r.n
    c.ring_sq = Ring(1)
    c.ring_bc = Ring(1)
    c.ring_xo = Ring(2)
    c.ring_vtmp = Ring(2)
    c.ring_ktmp = Ring(2)
    c.oa_g = [None] * 16
    c.psg = {}
    c.y_guard = None
    c.exp_guard = None
    c.pbf_guard = None
    c.pT_war = None
    c.stat_guard = None
    c.stat_guard2 = [None, None]
    c.pbf_guard2 = [None, None]
    c.exp_guard2 = [None, None]
    c.mbc_guard = None
    c.rbc_guard = None
    c.ylo_tmp_guard = None


def attach_guard_methods(c):
    c.reset_phase_guards = lambda: _reset_phase_guards(c)

    def y_guard_wait(P, eng):
        if c.y_guard is not None:
            P.wait(eng, c.y_guard[0], c.y_guard[1])
    c.y_guard_wait = y_guard_wait

    def exp_guard_wait(P):
        if c.exp_guard is not None:
            P.wait("scalar", c.exp_guard[0], c.exp_guard[1])
    c.exp_guard_wait = exp_guard_wait

    def oa_guard_wait(P, eng, oc):
        if c.oa_g[oc] is not None:
            P.wait(eng, c.oa_g[oc][0], c.oa_g[oc][1])
    c.oa_guard_wait = oa_guard_wait

    def oa_guard_set(oc, sem, val):
        c.oa_g[oc] = (sem, val)
    c.oa_guard_set = oa_guard_set

    def oa_guard_clear(oc):
        c.oa_g[oc] = None
    c.oa_guard_clear = oa_guard_clear



def prep_l1_ffn(gate_w, up_w, down_w, gate_scale, up_scale, down_scale):
    d = {}
    d["l1_gw"] = lhsT_tiles(np.ascontiguousarray(gate_w.T))
    d["l1_uw"] = lhsT_tiles(np.ascontiguousarray(up_w.T))
    d["l1_dw"] = lhsT_tiles(np.ascontiguousarray(down_w.T))
    d["l1_gs"] = np.ascontiguousarray(gate_scale.reshape(64, 128).T)
    d["l1_us"] = np.ascontiguousarray(up_scale.reshape(64, 128).T)
    d["l1_ds"] = np.ascontiguousarray(down_scale.reshape(16, 128).T)
    return d


def declare_l1_ffn(nc):
    aps = {}
    aps["l1_gw"] = nc.dram_tensor("l1_gw", [64, 128, 2048], I32, kind="ExternalInput").ap()
    aps["l1_uw"] = nc.dram_tensor("l1_uw", [64, 128, 2048], I32, kind="ExternalInput").ap()
    aps["l1_dw"] = nc.dram_tensor("l1_dw", [16, 128, 8192], I32, kind="ExternalInput").ap()
    aps["l1_gs"] = nc.dram_tensor("l1_gs", [128, 64], F32, kind="ExternalInput").ap()
    aps["l1_us"] = nc.dram_tensor("l1_us", [128, 64], F32, kind="ExternalInput").ap()
    aps["l1_ds"] = nc.dram_tensor("l1_ds", [128, 16], F32, kind="ExternalInput").ap()
    return aps



def build_test_l1ffn():
    nc = bass.Bass("TRN2", target_bir_lowering=False)
    P = Prog(nc)
    xin = nc.dram_tensor("xin", [D, T], F32, kind="ExternalInput").ap()
    xout = nc.dram_tensor("xout", [D, T], F32, kind="ExternalOutput").ap()
    aps = declare_l1_ffn(nc)
    c = build_common(nc, P)
    init_guards(c)
    attach_guard_methods(c)
    emit_l1_ffn(c, xin, xout, aps["l1_gw"], aps["l1_uw"], aps["l1_dw"],
                aps["l1_gs"], aps["l1_us"], aps["l1_ds"], load_scales=True)
    for st in c.st_sems:
        P.wait("sync", st, st.n)
    return P.finalize(), P


def build_test_l0ffn():
    nc = bass.Bass("TRN2", target_bir_lowering=False)
    P = Prog(nc)
    xin = nc.dram_tensor("xin", [D, T], F32, kind="ExternalInput").ap()
    xout = nc.dram_tensor("xout", [D, T], F32, kind="ExternalOutput").ap()
    aps = declare_l0_ffn(nc)
    c = build_common(nc, P)
    init_guards(c)
    attach_guard_methods(c)
    emit_l0_ffn(c, xin, xout, aps, load_w=True)
    for st in c.st_sems:
        P.wait("sync", st, st.n)
    return P.finalize(), P


def np_l0_ffn_ref(x, rw, gu, dn, shg, shu, shd, sgw):
    def ln(v):
        m = v.mean(-1, keepdims=True)
        var = v.var(-1, keepdims=True)
        return (v - m) / np.sqrt(var + 1e-5)
    def silu(v):
        return v / (1 + np.exp(-v))
    y = ln(x)
    shared = (silu(y @ shg.T) * (y @ shu.T)) @ shd.T
    logits = y @ rw.T
    probs = np.exp(logits - logits.max(-1, keepdims=True))
    probs = probs / probs.sum(-1, keepdims=True)
    order = np.argsort(-probs, axis=-1)
    top2 = order[:, :2]
    tv = np.take_along_axis(probs, top2, axis=-1)
    tv = tv / tv.sum(-1, keepdims=True)
    comb = np.zeros_like(probs)
    np.put_along_axis(comb, top2, tv, axis=-1)
    out = np.zeros_like(x)
    for e in range(8):
        gu_ = y @ gu[e].T
        cur = (silu(gu_[:, :1408]) * gu_[:, 1408:]) @ dn[e].T
        out = out + cur * comb[:, e:e + 1]
    out = out + (1 / (1 + np.exp(-(y @ sgw.T)))) * shared
    return x + 0.5 * out


def _test_l0ffn(sim=False):
    rng = np.random.default_rng(11)
    x = (rng.standard_normal((T, D)) * 1.5).astype(np.float32)
    sc = 0.02
    rw = (rng.standard_normal((8, 2048)) * sc).astype(np.float32)
    gu = (rng.standard_normal((8, 2816, 2048)) * sc).astype(np.float32)
    dn = (rng.standard_normal((8, 2048, 1408)) * sc).astype(np.float32)
    shg = (rng.standard_normal((5632, 2048)) * sc).astype(np.float32)
    shu = (rng.standard_normal((5632, 2048)) * sc).astype(np.float32)
    shd = (rng.standard_normal((2048, 5632)) * sc).astype(np.float32)
    sgw = (rng.standard_normal((1, 2048)) * sc).astype(np.float32)

    nc, P = build_test_l0ffn()
    pre = prep_l0_ffn(rw, gu, dn, shg, shu, shd, sgw)
    inm = {"xin": np.ascontiguousarray(x.T), **host_consts(),
           **{k[3:] if False else k: v for k, v in pre.items()}}
    if sim:
        from concourse.bass_interp import CoreSim
        import time as _t
        sm = CoreSim(nc)
        for k, v in inm.items():
            sm.tensor(k)[:] = v
        t0 = _t.time()
        sm.simulate()
        print("sim took", _t.time() - t0)
        got = np.array(sm.tensor("xout")).T
    else:
        res = run_bass_kernel_spmd(nc, [inm], core_ids=[0])
        got = res.results[0]["xout"].T
    want = np_l0_ffn_ref(x, rw, gu, dn, shg, shu, shd, sgw)
    err = got - want
    rel = np.sqrt((err ** 2).mean() / (want ** 2).mean())
    print("rel l2:", rel, "max abs:", np.abs(err).max(), "scale:", np.abs(want).max())


def build_test_l0attn():
    nc = bass.Bass("TRN2", target_bir_lowering=False)
    P = Prog(nc)
    xin = nc.dram_tensor("xin", [D, T], F32, kind="ExternalInput").ap()
    xout = nc.dram_tensor("xout", [D, T], F32, kind="ExternalOutput").ap()
    aps = declare_l0_attn(nc)
    c = build_common(nc, P)
    init_guards(c)
    attach_guard_methods(c)
    emit_l0_attn(c, xin, xout, aps, load_w=True)
    for st in c.st_sems:
        P.wait("sync", st, st.n)
    return P.finalize(), P


def np_l0_attn_ref(x, basis, phase, amp):
    def ln(v):
        m = v.mean(-1, keepdims=True)
        return (v - m) / np.sqrt(v.var(-1, keepdims=True) + 1e-5)
    def holo(v, b, p, a):
        resn = v @ b.T
        w = a * np.cos(p)
        return resn @ w.T
    y = ln(x)
    q = holo(y, basis[0], phase[0], amp[0]).reshape(T, 16, 128).transpose(1, 0, 2)
    k = holo(y, basis[1], phase[1], amp[1]).reshape(T, 16, 128).transpose(1, 0, 2)
    v = holo(y, basis[2], phase[2], amp[2]).reshape(T, 16, 128).transpose(1, 0, 2)
    out = np.zeros((16, T, 128), dtype=np.float32)
    sc = 1.0 / np.sqrt(128.0)
    for h in range(16):
        s_ = (q[h] @ k[h].T) * sc
        s_ = s_ - s_.max(-1, keepdims=True)
        p_ = np.exp(s_)
        p_ = p_ / p_.sum(-1, keepdims=True)
        out[h] = p_ @ v[h]
    o = out.transpose(1, 0, 2).reshape(T, 2048)
    return x + holo(o, basis[3], phase[3], amp[3])


def _test_l0attn(sim=False):
    rng = np.random.default_rng(13)
    x = (rng.standard_normal((T, D)) * 1.5).astype(np.float32)
    basis = rng.standard_normal((4, 64, 2048)).astype(np.float32)
    basis = basis / np.linalg.norm(basis, axis=2, keepdims=True)
    phase = (rng.standard_normal((4, 2048, 64)) * 0.1).astype(np.float32)
    amp = (np.ones((4, 2048, 64)) / 8.0).astype(np.float32) * (
        1 + 0.3 * rng.standard_normal((4, 2048, 64))).astype(np.float32)

    nc, P = build_test_l0attn()
    pre = prep_l0_attn(basis[0], phase[0], amp[0], basis[1], phase[1], amp[1],
                       basis[2], phase[2], amp[2], basis[3], phase[3], amp[3])
    inm = {"xin": np.ascontiguousarray(x.T), **host_consts(), **pre}
    if sim:
        from concourse.bass_interp import CoreSim
        import time as _t
        sm = CoreSim(nc)
        for kk, vv in inm.items():
            sm.tensor(kk)[:] = vv
        t0 = _t.time()
        sm.simulate()
        print("sim took", _t.time() - t0)
        got = np.array(sm.tensor("xout")).T
    else:
        res = run_bass_kernel_spmd(nc, [inm], core_ids=[0])
        got = res.results[0]["xout"].T
    want = np_l0_attn_ref(x, basis, phase, amp)
    err = got - want
    rel = np.sqrt((err ** 2).mean() / (want ** 2).mean())
    print("rel l2:", rel, "max abs:", np.abs(err).max(), "scale:", np.abs(want).max())


def build_test_l1attn():
    nc = bass.Bass("TRN2", target_bir_lowering=False)
    P = Prog(nc)
    xin = nc.dram_tensor("xin", [D, T], F32, kind="ExternalInput").ap()
    xout = nc.dram_tensor("xout", [D, T], F32, kind="ExternalOutput").ap()
    aps = declare_l1_attn(nc)
    spill = {"v_sp": nc.dram_tensor("v_sp", [16, 128, 2048], BF16).ap(),
             "q_sp": nc.dram_tensor("q_sp", [16, 128, 2048], F32).ap(),
             "k_sp": nc.dram_tensor("k_sp", [16, 128, 2048], F32).ap()}
    c = build_common(nc, P)
    init_guards(c)
    attach_guard_methods(c)
    emit_l1_attn(c, xin, xout, aps, spill)
    for st in c.st_sems:
        P.wait("sync", st, st.n)
    return P.finalize(), P


def np_l1_attn_ref(x, W_q, W_k, W_v, W_o):
    def ln(v):
        m = v.mean(-1, keepdims=True)
        return (v - m) / np.sqrt(v.var(-1, keepdims=True) + 1e-5)
    y = ln(x)
    q = (y @ W_q).reshape(T, 16, 128).transpose(1, 0, 2)
    k = (y @ W_k).reshape(T, 16, 128).transpose(1, 0, 2)
    v = y @ W_v
    sc = 1.0 / np.sqrt(128.0)
    mask = np.triu(np.full((T, T), -1e9, dtype=np.float32), k=1)
    psum = np.zeros((T, T), dtype=np.float32)
    for h in range(16):
        s_ = (q[h] @ k[h].T + mask) * sc
        s_ = s_ - s_.max(-1, keepdims=True)
        p_ = np.exp(s_)
        psum += p_ / p_.sum(-1, keepdims=True)
    return x + (psum @ v) @ W_o


def _test_l1attn(sim=False):
    rng = np.random.default_rng(17)
    x = (rng.standard_normal((T, D)) * 1.5).astype(np.float32)
    W_q = rng.standard_normal((D, D)).astype(np.float32)
    W_k = rng.standard_normal((D, D)).astype(np.float32)
    W_v = (rng.standard_normal((D, D)) * np.sqrt(2.0 / D)).astype(np.float32)
    W_o = (rng.standard_normal((D, D)) * np.sqrt(2.0 / D)).astype(np.float32)
    nc, P = build_test_l1attn()
    inm = {"xin": np.ascontiguousarray(x.T), **host_consts(),
           **prep_l1_attn(W_q, W_k, W_v, W_o)}
    if sim:
        from concourse.bass_interp import CoreSim
        import time as _t
        sm = CoreSim(nc)
        for kk, vv in inm.items():
            sm.tensor(kk)[:] = vv
        t0 = _t.time()
        sm.simulate()
        print("sim took", _t.time() - t0)
        got = np.array(sm.tensor("xout")).T
    else:
        res = run_bass_kernel_spmd(nc, [inm], core_ids=[0])
        got = res.results[0]["xout"].T
    want = np_l1_attn_ref(x, W_q, W_k, W_v, W_o)
    err = got - want
    rel = np.sqrt((err ** 2).mean() / (want ** 2).mean())
    print("rel l2:", rel, "max abs:", np.abs(err).max(), "scale:", np.abs(want).max())



_PROGS = {}

FUSED = True


def build_fused():
    nc = bass.Bass("TRN2", target_bir_lowering=False)
    P = Prog(nc)
    xin = nc.dram_tensor("xin", [D, T], F32, kind="ExternalInput").ap()
    xout = nc.dram_tensor("xout", [D, T], F32, kind="ExternalOutput").ap()
    xs0 = nc.dram_tensor("xs0", [D, T], F32).ap()
    xs1 = nc.dram_tensor("xs1", [D, T], F32).ap()
    spill = {"v_sp": nc.dram_tensor("v_sp", [16, 128, 2048], BF16).ap(),
             "q_sp": nc.dram_tensor("q_sp", [16, 128, 2048], F32).ap(),
             "k_sp": nc.dram_tensor("k_sp", [16, 128, 2048], F32).ap()}
    a_l0f = declare_l0_ffn(nc)
    a_l0a = declare_l0_attn(nc)
    a_l1f = declare_l1_ffn(nc)
    a_l1a = declare_l1_attn(nc)
    c = build_common(nc, P)
    init_guards(c)
    attach_guard_methods(c)
    emit_l0_ffn(c, xin, xs0, a_l0f, load_w=True)
    emit_l0_attn(c, xs0, xs1, a_l0a, load_w=True)
    emit_l0_ffn(c, xs1, xs0, a_l0f, load_w=False)
    emit_l1_ffn(c, xs0, xs1, a_l1f["l1_gw"], a_l1f["l1_uw"], a_l1f["l1_dw"],
                a_l1f["l1_gs"], a_l1f["l1_us"], a_l1f["l1_ds"], load_scales=True)
    emit_l1_attn(c, xs1, xs0, a_l1a, spill)
    emit_l1_ffn(c, xs0, xout, a_l1f["l1_gw"], a_l1f["l1_uw"], a_l1f["l1_dw"],
                a_l1f["l1_gs"], a_l1f["l1_us"], a_l1f["l1_ds"], load_scales=False)
    for st in c.st_sems:
        P.wait("sync", st, st.n)
    return P.finalize(), P


def _get_stage_prog(kind):
    if kind in _PROGS:
        return _PROGS[kind]
    if kind == "l0ffn":
        nc, P = build_test_l0ffn()
    elif kind == "l0attn":
        nc, P = build_test_l0attn()
    elif kind == "l1ffn":
        nc, P = build_test_l1ffn()
    elif kind == "l1attn":
        nc, P = build_test_l1attn()
    _PROGS[kind] = nc
    return nc


def kernel(**inputs):
    inp = {k: np.asarray(v) for k, v in inputs.items()}
    x = np.asarray(inp["L0_attn_hidden_states"], dtype=np.float32)
    consts = host_consts()
    w_l0f = prep_l0_ffn(inp["L0_ffn_router_weight"], inp["L0_ffn_gate_up_proj"],
                        inp["L0_ffn_down_proj"], inp["L0_ffn_sh_gate_w"],
                        inp["L0_ffn_sh_up_w"], inp["L0_ffn_sh_down_w"],
                        inp["L0_ffn_shared_gate_w"])
    w_l0a = prep_l0_attn(inp["L0_attn_basis_q"], inp["L0_attn_phase_q"], inp["L0_attn_amp_q"],
                         inp["L0_attn_basis_k"], inp["L0_attn_phase_k"], inp["L0_attn_amp_k"],
                         inp["L0_attn_basis_v"], inp["L0_attn_phase_v"], inp["L0_attn_amp_v"],
                         inp["L0_attn_basis_o"], inp["L0_attn_phase_o"], inp["L0_attn_amp_o"])
    w_l1a = prep_l1_attn(np.asarray(inp["L1_attn_W_q"], np.float32),
                         np.asarray(inp["L1_attn_W_k"], np.float32),
                         np.asarray(inp["L1_attn_W_v"], np.float32),
                         np.asarray(inp["L1_attn_W_o"], np.float32))
    w_l1f = prep_l1_ffn(np.asarray(inp["L1_ffn_gate_w"], np.int32),
                        np.asarray(inp["L1_ffn_up_w"], np.int32),
                        np.asarray(inp["L1_ffn_down_w"], np.int32),
                        np.asarray(inp["L1_ffn_gate_scale"], np.float32),
                        np.asarray(inp["L1_ffn_up_scale"], np.float32),
                        np.asarray(inp["L1_ffn_down_scale"], np.float32))

    xs = [np.ascontiguousarray(x[b].T).astype(np.float32) for b in range(8)]
    if FUSED:
        if "fused" not in _PROGS:
            _PROGS["fused"] = build_fused()[0]
        nc = _PROGS["fused"]
        wd = {**consts, **w_l0f, **w_l0a, **w_l1f, **w_l1a}
        in_maps = [{"xin": xs[b], **wd} for b in range(8)]
        res = run_bass_kernel_spmd(nc, in_maps, core_ids=list(range(8)))
        xs = [np.asarray(res.results[b]["xout"]) for b in range(8)]
    else:
        stages = [("l0ffn", w_l0f), ("l0attn", w_l0a), ("l0ffn", w_l0f),
                  ("l1ffn", w_l1f), ("l1attn", w_l1a), ("l1ffn", w_l1f)]
        for kind, wd in stages:
            nc = _get_stage_prog(kind)
            in_maps = [{"xin": xs[b], **consts, **wd} for b in range(8)]
            res = run_bass_kernel_spmd(nc, in_maps, core_ids=list(range(8)))
            xs = [np.asarray(res.results[b]["xout"]) for b in range(8)]
    out = np.stack([xs[b].T for b in range(8)]).astype(np.float32)
    return np.ascontiguousarray(out)


def _test_l1ffn():
    import jax
    import jax.numpy as jnp
    rng = np.random.default_rng(7)
    x = (rng.standard_normal((T, D)) * 1.5).astype(np.float32)
    gate_w = rng.integers(0, 256, (8192, 2048)).astype(np.int32)
    up_w = rng.integers(0, 256, (8192, 2048)).astype(np.int32)
    down_w = rng.integers(0, 256, (2048, 8192)).astype(np.int32)
    gs = (rng.random(8192) * 0.02).astype(np.float32)
    us = (rng.random(8192) * 0.02).astype(np.float32)
    ds = (rng.random(2048) * 0.02).astype(np.float32)

    nc, P = build_test_l1ffn()
    inm = {"xin": np.ascontiguousarray(x.T), **host_consts(),
           **prep_l1_ffn(gate_w, up_w, down_w, gs, us, ds)}
    res = run_bass_kernel_spmd(nc, [inm], core_ids=[0])
    got = res.results[0]["xout"].T

    def ln(v):
        m = v.mean(-1, keepdims=True)
        var = v.var(-1, keepdims=True)
        return (v - m) / np.sqrt(var + 1e-5)
    y = ln(x)
    Wg = (gate_w.astype(np.float32) - 128.0) * gs[:, None]
    Wu = (up_w.astype(np.float32) - 128.0) * us[:, None]
    Wd = (down_w.astype(np.float32) - 128.0) * ds[:, None]
    g = y @ Wg.T
    u = y @ Wu.T
    h = (g / (1 + np.exp(-g))) * u
    out = h @ Wd.T
    want = x + 0.5 * out
    err = np.abs(got - want)
    rel = np.sqrt((err ** 2).mean() / (want ** 2).mean())
    print("rel l2:", rel, "max abs:", err.max(), "want scale:", np.abs(want).max())


if __name__ == "__main__":
    import sys
    if len(sys.argv) > 1 and sys.argv[1] == "l1ffn":
        _test_l1ffn()
    elif len(sys.argv) > 1 and sys.argv[1] == "l0ffn":
        _test_l0ffn(sim=("sim" in sys.argv))
    elif len(sys.argv) > 1 and sys.argv[1] == "l0attn":
        _test_l0attn(sim=("sim" in sys.argv))
    elif len(sys.argv) > 1 and sys.argv[1] == "l1attn":
        _test_l1attn(sim=("sim" in sys.argv))


# version 09 (final)
# speedup vs baseline: 1.0345x; 1.0345x over previous
import numpy as np
import ml_dtypes
import concourse.bass as bass
import concourse.mybir as mybir
from concourse.bass_utils import run_bass_kernel_spmd

F32 = mybir.dt.float32
F32R = mybir.dt.float32r
BF16 = mybir.dt.bfloat16
I32 = mybir.dt.int32
AF = mybir.ActivationFunctionType
ALU = mybir.AluOpType
AX = mybir.AxisListType.X

D = 2048
T = 2048
KC = 16
NB = 512
TH = 1024


def lhsT_tiles(w_km):
    K, M = w_km.shape
    kc, mc = K // 128, M // 128
    t = w_km.reshape(kc, 128, mc, 128).transpose(2, 1, 0, 3)
    return np.ascontiguousarray(t.reshape(mc, 128, kc * 128))


def skinny_lhsT(w_km):
    K, M = w_km.shape
    kc = K // 128
    return np.ascontiguousarray(w_km.reshape(kc, 128, M).transpose(1, 0, 2))



ENGS = ("tensor", "vector", "scalar", "gpsimd", "sync")


class Sem:
    def __init__(self, nc, name):
        self.h = nc.alloc_semaphore(name)
        self.n = 0
        self.name = name

    def tick(self, k=1):
        self.n += k
        return self.n


class Prog:
    def __init__(self, nc):
        self.nc = nc
        self.ops = {e: [] for e in ENGS}
        self.sems = {}

    def sem(self, name):
        if name not in self.sems:
            self.sems[name] = Sem(self.nc, name)
        return self.sems[name]

    def wait(self, eng, sem, val):
        assert val <= sem.n, f"wait {sem.name}>={val} but only {sem.n} emitted"
        if val > 0:
            self.ops[eng].append(lambda e: e.wait_ge(sem.h, val))

    def op(self, eng, fn, inc=None):
        if inc is None:
            self.ops[eng].append(fn)
            return None
        sem, k = inc
        v = sem.tick(k)
        self.ops[eng].append(lambda e: fn(e).then_inc(sem.h, k))
        return v

    def barrier(self):
        vals = {}
        for eng in ENGS:
            s = self.sem("bar_" + eng)
            vals[eng] = self.op(eng, lambda e: e.nop(), inc=(s, 1))
        for eng in ENGS:
            for other in ENGS:
                if other != eng:
                    self.wait(eng, self.sem("bar_" + other), vals[other])

    def finalize(self):
        nc = self.nc
        with nc.Block() as block:
            @block.tensor
            def _(t):
                for f in self.ops["tensor"]:
                    f(t)

            @block.vector
            def _(v):
                for f in self.ops["vector"]:
                    f(v)

            @block.scalar
            def _(s):
                for f in self.ops["scalar"]:
                    f(s)

            @block.gpsimd
            def _(g):
                for f in self.ops["gpsimd"]:
                    f(g)

            @block.sync
            def _(sy):
                for f in self.ops["sync"]:
                    f(sy)
        return nc


class Ring:
    def __init__(self, n):
        self.n = n
        self.guard = [None] * n

    def acquire(self, P, eng, i):
        g = self.guard[i % self.n]
        if g is not None:
            P.wait(eng, g[0], g[1])

    def release(self, i, sem, val):
        self.guard[i % self.n] = (sem, val)


class DmaRing:
    _uid = 0

    def __init__(self, P, name, n):
        DmaRing._uid += 1
        self.P, self.n = P, n
        self.sems = [P.sem(f"dr{DmaRing._uid}_{name}_{i}") for i in range(n)]
        self.guard = [None] * n

    def load(self, eng, i, fn):
        g = self.guard[i % self.n]
        if g:
            for gs, gv in g:
                self.P.wait(eng, gs, gv)
        s = self.sems[i % self.n]
        v = self.P.op(eng, fn, inc=(s, 16))
        return (s, v)

    def consumed(self, i, sem, val):
        self.guard[i % self.n] = [(sem, val)]

    def also_consumed(self, i, sem, val):
        g = self.guard[i % self.n]
        if g is None:
            g = self.guard[i % self.n] = []
        g.append((sem, val))


def vbf(ar_ap, off_f32, n):
    return ar_ap.bitcast(BF16)[:, 2 * off_f32: 2 * off_f32 + n]


def vf(ar_ap, off_f32, n):
    return ar_ap[:, off_f32: off_f32 + n]


class Ctx:
    pass


def build_common(nc, P):
    c = Ctx()
    c.nc, c.P = nc, P
    c.A_r = nc.alloc_sbuf_tensor("arA", [128, 16384], F32R).ap()
    c.A = c.A_r.bitcast(F32)
    c.B = nc.alloc_sbuf_tensor("arB", [128, 16384], F32).ap()
    c.C_r = nc.alloc_sbuf_tensor("arC", [128, 8192], F32R).ap()
    c.C = c.C_r.bitcast(F32)
    c.W = nc.alloc_sbuf_tensor("arW", [128, 4280], F32).ap()
    c.FR = nc.alloc_sbuf_tensor("arFR", [128, 2560], F32R).ap()
    c.S = nc.alloc_sbuf_tensor("arS", [128, 2048], F32).ap()
    c.L = nc.alloc_sbuf_tensor("arL", [128, 1536], F32).ap()
    c.R = nc.alloc_sbuf_tensor("arR", [128, 1024], F32).ap()
    c.PS = [nc.alloc_psum_tensor(f"ps{i}", [128, 1024], F32).ap() for i in range(4)]
    c.psg = {}

    c.idf = nc.alloc_sbuf_tensor("idf", [128, 128], F32).ap()
    c.idb = nc.alloc_sbuf_tensor("idb", [128, 128], BF16).ap()
    c.ones_col_f = nc.alloc_sbuf_tensor("ones_col_f", [128, 1], F32).ap()
    c.ones_col_b = nc.alloc_sbuf_tensor("ones_col_b", [128, 1], BF16).ap()
    c.ones_row_f = nc.alloc_sbuf_tensor("ones_row_f", [1, 128], F32).ap()
    c.ones_row_b = nc.alloc_sbuf_tensor("ones_row_b", [1, 128], BF16).ap()
    c.m128_row_b = nc.alloc_sbuf_tensor("m128_row_b", [1, 128], BF16).ap()
    c.eps_col = nc.alloc_sbuf_tensor("eps_col", [1, 1], F32).ap()
    c.pi2_col = nc.alloc_sbuf_tensor("pi2_col", [128, 1], F32).ap()
    c.maskd = nc.alloc_sbuf_tensor("maskd", [128, 128], BF16).ap()

    ident = nc.dram_tensor("c_ident", [128, 128], F32, kind="ExternalInput").ap()
    maskd_d = nc.dram_tensor("c_maskd", [128, 128], F32, kind="ExternalInput").ap()
    crow = nc.dram_tensor("c_rows", [4, 128], F32, kind="ExternalInput").ap()
    sLDG = P.sem("sLDG")
    for out, in_ in [(c.idf, ident), (c.idb, ident),
                     (c.ones_row_f, crow[0:1, :]), (c.ones_row_b, crow[0:1, :]),
                     (c.m128_row_b, crow[1:2, :]), (c.eps_col, crow[2:3, 0:1]),
                     (c.ones_col_f, crow[0:1, 0:1].broadcast_to([128, 1])),
                     (c.ones_col_b, crow[0:1, 0:1].broadcast_to([128, 1])),
                     (c.pi2_col, crow[3:4, 0:1].broadcast_to([128, 1])),
                     (c.maskd, maskd_d)]:
        v = P.op("gpsimd", lambda e, out=out, in_=in_: e.dma_start(out=out, in_=in_),
                 inc=(sLDG, 16))
    for eng in ENGS:
        P.wait(eng, sLDG, v)
    return c


def ps_wait(c, P, eng, i, halves=(0, 1)):
    for h in halves:
        g = c.psg.get((i, h))
        if g is not None:
            P.wait(eng, g[0], g[1])


def ps_set(c, i, sem, val, halves=(0, 1)):
    for h in halves:
        c.psg[(i, h)] = (sem, val)


def host_consts():
    crow = np.zeros((4, 128), dtype=np.float32)
    crow[0, :] = 1.0
    crow[1, :] = -128.0
    crow[2, :] = 1e-5
    crow[3, :] = np.pi / 2
    maskd = np.triu(np.full((128, 128), -1e9, dtype=np.float32), k=1)
    return {"c_ident": np.eye(128, dtype=np.float32), "c_rows": crow,
            "c_maskd": maskd}



def emit_ln(c, x_in_ap, t0, width, y_ar, y_off, router=None, y_row=None, y_col=0,
            ylo_ar=None, ylo_off=0, split4=False):
    if y_row is None:
        y_row = width // 2
    P = c.P
    sPE = P.sem("sPE"); sACT = P.sem("sACT")
    sDVE = P.sem("sDVE"); sPOOL = P.sem("sPOOL")
    xw = c.C[:, 0:8192].rearrange("p (kc t) -> p kc t", kc=16)
    x_src = x_in_ap.rearrange("(kc p) t -> p kc t", p=128)
    sq = [vbf(c.L, 0, 512), vbf(c.L, 0, 512)]
    t1s = vf(c.L, 256, 256) if False else vf(c.L, 0, 512)
    bcm, bcr = vf(c.L, 512, 512), vf(c.L, 1024, 512)
    row_m = c.R[0:1, 0:512]
    row_v = c.R[0:1, 512:1024]

    for sub in range(width // NB):
        tt0 = t0 + sub * NB
        if split4 and sub == 0:
            s_xw = c.ring_xw.sems[0]
            s_xwg = P.sem("sXWG")
            g_xw = c.ring_xw.guard[0]
            vtot = None
            vtot_g = None
            for (k0, k1), eng in zip(((0, 6), (6, 11), (11, 16)),
                                     ("scalar", "sync", "gpsimd")):
                if g_xw:
                    for gs_, gv_ in g_xw:
                        P.wait(eng, gs_, gv_)
                sm = s_xwg if eng == "gpsimd" else s_xw
                v_ = P.op(eng, lambda e, k0=k0, k1=k1, tt0=tt0: e.dma_start(
                    out=xw[:, k0:k1, :],
                    in_=x_src[:, k0:k1, tt0:tt0 + NB]), inc=(sm, 16))
                if eng == "gpsimd":
                    vtot_g = v_
                else:
                    vtot = v_
            c.ring_xw.guard[0] = None
            tok_x = (s_xw, vtot)
            P.wait("tensor", s_xwg, vtot_g)
            P.wait("scalar", s_xwg, vtot_g)
        else:
            tok_x = c.ring_xw.load("scalar", 0, lambda e, tt0=tt0: e.dma_start(
                out=xw, in_=x_src[:, :, tt0:tt0 + NB]))
        bk_s = c.PS[3][0:1, 0:512]
        bk_q = c.PS[3][0:1, 512:1024]
        P.wait("tensor", *tok_x)
        P.wait("scalar", *tok_x)
        ps_wait(c, P, "tensor", 3)
        sq_pe = {}
        for kc in range(16):
            if kc >= 1:
                P.wait("scalar", sPE, sq_pe[kc - 1])
            if kc == 0:
                c.ring_sq.acquire(P, "scalar", 0)
            vv = P.op("scalar", lambda e, kc=kc: e.activation(
                sq[kc % 2], xw[:, kc, :], AF.Square), inc=(sACT, 1))
            P.op("tensor", lambda e, kc=kc: e.matmul(
                bk_s, c.ones_col_f, xw[:, kc, :], start=(kc == 0), stop=(kc == 15)))
            P.wait("tensor", sACT, vv)
            sq_pe[kc] = P.op("tensor", lambda e, kc=kc: e.matmul(
                bk_q, c.ones_col_b, sq[kc % 2], start=(kc == 0), stop=(kc == 15)),
                inc=(sPE, 1))
        v_stats = sq_pe[15]

        if router is not None:
            ps_wait(c, P, "tensor", 0, halves=(sub,))
            for kc in range(16):
                P.op("tensor", lambda e, kc=kc, sub=sub: e.matmul(
                    c.PS[0][0:8, sub * 512:(sub + 1) * 512],
                    router["rw_r"][:, kc * 8:(kc + 1) * 8],
                    xw[:, kc, :], start=(kc == 0), stop=False))

        P.wait("scalar", sPE, v_stats)
        va = P.op("scalar", lambda e: e.activation(row_m, bk_s, AF.Copy, scale=1.0 / 2048.0),
                  inc=(sACT, 1))
        P.wait("scalar", sACT, va)
        vb = P.op("scalar", lambda e: e.activation(row_v, row_m, AF.Square), inc=(sACT, 1))
        P.wait("vector", sACT, vb)
        vc_ = P.op("vector", lambda e: e.scalar_tensor_tensor(
            row_v, bk_q, 1.0 / 2048.0, row_v, ALU.mult, ALU.subtract), inc=(sDVE, 1))
        ps_set(c, 3, sDVE, vc_)
        P.wait("scalar", sDVE, vc_)
        vd = P.op("scalar", lambda e: e.activation(row_v, row_v, AF.Sqrt, bias=c.eps_col),
                  inc=(sACT, 1))
        P.wait("vector", sACT, vd)
        ve = P.op("vector", lambda e: e.reciprocal(row_v, row_v), inc=(sDVE, 1))

        P.wait("tensor", sACT, va)
        P.wait("tensor", sDVE, ve)
        ps_wait(c, P, "tensor", 2)
        P.op("tensor", lambda e: e.matmul(c.PS[2][:, 0:512], c.ones_row_f, row_m,
                                          start=True, stop=True))
        vg = P.op("tensor", lambda e: e.matmul(c.PS[2][:, 512:1024], c.ones_row_f, row_v,
                                               start=True, stop=True), inc=(sPE, 1))
        P.wait("scalar", sPE, vg)
        c.ring_bc.acquire(P, "scalar", 0)
        P.op("scalar", lambda e: e.activation(bcm, c.PS[2][:, 0:512], AF.Copy))
        vh = P.op("scalar", lambda e: e.activation(bcr, c.PS[2][:, 512:1024], AF.Copy),
                  inc=(sACT, 1))
        ps_set(c, 2, sACT, vh)

        vr1_sub = None
        if router is not None:
            vr1 = P.op("tensor", lambda e, sub=sub: e.matmul(
                c.PS[0][0:8, sub * 512:(sub + 1) * 512], router["negsw"],
                row_m, start=False, stop=True), inc=(sPE, 1))
            vr1_sub = vr1
            P.wait("vector", sPE, vr1)
            P.wait("vector", sACT, vh)
            vlg = P.op("vector", lambda e, sub=sub: e.tensor_tensor(
                router["out"][0:8, sub * 512:(sub + 1) * 512],
                c.PS[0][0:8, sub * 512:(sub + 1) * 512], bcr[0:8, :], ALU.mult),
                inc=(sDVE, 1))
            ps_set(c, 0, sDVE, vlg, halves=(sub,))

        P.wait("gpsimd", sACT, vh)
        P.wait("gpsimd", sPE, v_stats)
        t1_dve = {}
        pool_last = None
        for kc in range(16):
            if kc >= 1:
                P.wait("gpsimd", sDVE, t1_dve[kc - 1])
            vp = P.op("gpsimd", lambda e, kc=kc: e.tensor_tensor(
                t1s, xw[:, kc, :], bcm, ALU.subtract), inc=(sPOOL, 1))
            pool_last = vp
            P.wait("vector", sPOOL, vp)
            ytile = vbf(y_ar, y_off + kc * y_row, 2 * y_row)[:, y_col + sub * NB:y_col + (sub + 1) * NB]
            c.y_guard_wait(P, "vector")
            t1_dve[kc] = P.op("vector", lambda e, kc=kc, ytile=ytile: e.tensor_tensor(
                ytile, t1s, bcr, ALU.mult), inc=(sDVE, 1))
            if ylo_ar is not None:
                tmp = c.S[:, 0:512]
                if c.ylo_tmp_guard is not None:
                    P.wait("gpsimd", c.ylo_tmp_guard[0], c.ylo_tmp_guard[1])
                P.wait("gpsimd", sPOOL, vp)
                vtp = P.op("gpsimd", lambda e, kc=kc: e.tensor_tensor(
                    tmp, t1s, bcr, ALU.mult), inc=(sPOOL, 1))
                pool_last = vtp
                ylotile = vbf(ylo_ar, ylo_off + kc * y_row, 2 * y_row)[
                    :, y_col + sub * NB:y_col + (sub + 1) * NB]
                P.wait("vector", sPOOL, vtp)
                P.wait("vector", sDVE, t1_dve[kc])
                vlo = P.op("vector", lambda e, kc=kc, ylotile=ylotile, ytile=ytile: e.tensor_tensor(
                    ylotile, tmp, ytile, ALU.subtract), inc=(sDVE, 1))
                c.ylo_tmp_guard = (sDVE, vlo)
                t1_dve[kc] = vlo
        c.ring_xw.consumed(0, sPOOL, pool_last)
        if vr1_sub is not None:
            c.ring_xw.also_consumed(0, sPE, vr1_sub)
        c.ring_sq = Ring(1)
        c.ring_sq.release(0, sDVE, t1_dve[15])
        c.ring_bc.release(0, sDVE, t1_dve[15])
    return P.sems["sDVE"].n



def emit_l1_ffn(c, x_in_ap, x_out_ap, wg_ap, wu_ap, wd_ap, gs_ap, us_ap, ds_ap,
                load_scales):
    P = c.P
    sPE = P.sem("sPE"); sACT = P.sem("sACT"); sDVE = P.sem("sDVE")
    sSCL = P.sem("sSCL")

    y_off = 0
    act_off = 8192
    yv = lambda kc: vbf(c.B, y_off + kc * 512, 1024)
    actv = lambda ic: vbf(c.B, act_off + ic * 512, 1024)

    gs = vf(c.W, 4096, 64)
    us = vf(c.W, 4160, 64)
    dh = vf(c.W, 4224, 16)
    if load_scales:
        P.op("sync", lambda e: e.dma_start(out=gs, in_=gs_ap), inc=(sSCL, 16))
        P.op("sync", lambda e: e.dma_start(out=us, in_=us_ap), inc=(sSCL, 16))
        ds_tmp = vf(c.W, 4240, 16)
        v3 = P.op("sync", lambda e: e.dma_start(out=ds_tmp, in_=ds_ap), inc=(sSCL, 16))
        P.wait("vector", sSCL, v3)
        c.l1_dh_v = P.op("vector", lambda e: e.tensor_scalar(
            dh, ds_tmp, 0.5, None, ALU.mult), inc=(sDVE, 1))
        P.wait("scalar", sSCL, v3)
    v_scales = c.l1_dh_v

    gslot = [vbf(c.W, 0, 2048), vbf(c.W, 1024, 2048)]
    uslot = [vbf(c.W, 2048, 2048), vbf(c.W, 3072, 2048)]
    dslot = gslot
    ring_g, ring_u, ring_d = c.ring_g, c.ring_u, c.ring_g

    ysum_b = c.R[0:1, 0:512].bitcast(BF16)
    actsum_b = c.R[0:1, 512:1024].bitcast(BF16)

    act_war = [None] * 16
    silu_sl = [vf(c.S, 0, 1024), vf(c.S, 1024, 1024)]
    silu_guard = [None, None]
    down_pe_last = None

    for th in range(2):
        t0 = th * TH
        v_y = emit_ln(c, x_in_ap, t0, TH, c.B, y_off, split4=(th == 0))

        P.wait("tensor", sDVE, v_y)
        ps_wait(c, P, "tensor", 3)
        vys = None
        for tt in range(2):
            for kc in range(16):
                vys = P.op("tensor", lambda e, kc=kc, tt=tt: e.matmul(
                    c.PS[3][0:1, tt * 512:(tt + 1) * 512], c.ones_col_b,
                    yv(kc)[:, tt * 512:(tt + 1) * 512], start=(kc == 0), stop=(kc == 15)),
                    inc=(sPE, 1) if (tt == 1 and kc == 15) else None)
        P.wait("scalar", sPE, vys)
        vysum = P.op("scalar", lambda e: e.activation(ysum_b, c.PS[3][0:1, 0:1024], AF.Copy),
                     inc=(sACT, 1))
        ps_set(c, 3, sACT, vysum)
        P.wait("tensor", sACT, vysum)
        if th == 0:
            P.wait("tensor", sDVE, v_scales)
            P.wait("vector", sDVE, v_scales)

        for ib in range(4):
            for ic in range(16):
                icg = ib * 16 + ic
                par = icg % 2
                tok_g = ring_g.load("gpsimd", icg, lambda e, icg=icg, par=par: e.dma_start(
                    out=gslot[par], in_=wg_ap[icg]))
                tok_u = ring_u.load("gpsimd", icg, lambda e, icg=icg, par=par: e.dma_start(
                    out=uslot[par], in_=wu_ap[icg]))

                pg, pu = c.PS[par], c.PS[2 + par]
                P.wait("tensor", *tok_g)
                ps_wait(c, P, "tensor", par)
                for kc in range(16):
                    for tt in range(2):
                        P.op("tensor", lambda e, kc=kc, tt=tt, par=par, pg=pg: e.matmul(
                            pg[:, tt * 512:(tt + 1) * 512],
                            gslot[par][:, kc * 128:(kc + 1) * 128],
                            yv(kc)[:, tt * 512:(tt + 1) * 512],
                            start=(kc == 0), stop=False))
                for tt in range(2):
                    v_g = P.op("tensor", lambda e, tt=tt, par=par, pg=pg: e.matmul(
                        pg[:, tt * 512:(tt + 1) * 512], c.m128_row_b,
                        ysum_b[0:1, tt * 512:(tt + 1) * 512], start=False, stop=True),
                        inc=(sPE, 1) if tt == 1 else None)
                ring_g.consumed(icg, sPE, v_g)

                P.wait("tensor", *tok_u)
                ps_wait(c, P, "tensor", 2 + par)
                for kc in range(16):
                    for tt in range(2):
                        P.op("tensor", lambda e, kc=kc, tt=tt, par=par, pu=pu: e.matmul(
                            pu[:, tt * 512:(tt + 1) * 512],
                            uslot[par][:, kc * 128:(kc + 1) * 128],
                            yv(kc)[:, tt * 512:(tt + 1) * 512],
                            start=(kc == 0), stop=False))
                for tt in range(2):
                    v_u = P.op("tensor", lambda e, tt=tt, par=par, pu=pu: e.matmul(
                        pu[:, tt * 512:(tt + 1) * 512], c.m128_row_b,
                        ysum_b[0:1, tt * 512:(tt + 1) * 512], start=False, stop=True),
                        inc=(sPE, 1) if tt == 1 else None)
                ring_u.consumed(icg, sPE, v_u)

                P.wait("scalar", sPE, v_g)
                if silu_guard[par] is not None:
                    P.wait("scalar", *silu_guard[par])
                vsil = P.op("scalar", lambda e, par=par, icg=icg, pg=pg: e.activation(
                    silu_sl[par], pg[:, :], AF.Silu, scale=gs[:, icg:icg + 1]),
                    inc=(sACT, 1))
                ps_set(c, par, sACT, vsil)

                P.wait("vector", sPE, v_u)
                P.wait("vector", sACT, vsil)
                if act_war[ic] is not None:
                    P.wait("vector", act_war[ic][0], act_war[ic][1])
                vact = P.op("vector", lambda e, ic=ic, par=par, icg=icg, pu=pu: e.scalar_tensor_tensor(
                    actv(ic), pu[:, :], us[:, icg:icg + 1], silu_sl[par],
                    ALU.mult, ALU.mult), inc=(sDVE, 1))
                ps_set(c, 2 + par, sDVE, vact)
                silu_guard[par] = (sDVE, vact)

            v_acts = P.sems["sDVE"].n
            P.wait("tensor", sDVE, v_acts)
            ps_wait(c, P, "tensor", 3)
            vas = None
            for tt in range(2):
                for ic in range(16):
                    vas = P.op("tensor", lambda e, ic=ic, tt=tt: e.matmul(
                        c.PS[3][0:1, tt * 512:(tt + 1) * 512], c.ones_col_b,
                        actv(ic)[:, tt * 512:(tt + 1) * 512], start=(ic == 0), stop=(ic == 15)),
                        inc=(sPE, 1) if (tt == 1 and ic == 15) else None)
            P.wait("scalar", sPE, vas)
            vas2 = P.op("scalar", lambda e: e.activation(actsum_b, c.PS[3][0:1, 0:1024], AF.Copy),
                        inc=(sACT, 1))
            ps_set(c, 3, sACT, vas2)
            P.wait("tensor", sACT, vas2)

            for oc in range(16):
                dpar = oc % 2
                tok_d = ring_d.load("gpsimd", oc, lambda e, oc=oc, ib=ib, dpar=dpar: e.dma_start(
                    out=dslot[dpar], in_=wd_ap[oc, :, ib * 2048:(ib + 1) * 2048]))
                P.wait("tensor", *tok_d)
                ps_wait(c, P, "tensor", dpar)
                for tt in range(2):
                    for ic in range(16):
                        P.op("tensor", lambda e, ic=ic, tt=tt, dpar=dpar: e.matmul(
                            c.PS[dpar][:, tt * 512:(tt + 1) * 512],
                            dslot[dpar][:, ic * 128:(ic + 1) * 128],
                            actv(ic)[:, tt * 512:(tt + 1) * 512],
                            start=(ic == 0), stop=False))
                    vdn = P.op("tensor", lambda e, tt=tt, dpar=dpar: e.matmul(
                        c.PS[dpar][:, tt * 512:(tt + 1) * 512], c.m128_row_b,
                        actsum_b[0:1, tt * 512:(tt + 1) * 512], start=False, stop=True),
                        inc=(sPE, 1) if tt == 1 else None)
                ring_d.consumed(oc, sPE, vdn)
                down_pe_last = vdn
                P.wait("vector", sPE, vdn)
                oa = vf(c.A, oc * 1024, 1024)
                c.oa_guard_wait(P, "vector", oc)
                if ib == 0:
                    vev = P.op("vector", lambda e, oa=oa, dpar=dpar: e.tensor_copy(
                        oa, c.PS[dpar][:, :]), inc=(sDVE, 1))
                else:
                    vev = P.op("vector", lambda e, oa=oa, dpar=dpar: e.tensor_add(
                        oa, oa, c.PS[dpar][:, :]), inc=(sDVE, 1))
                ps_set(c, dpar, sDVE, vev)
                c.oa_guard_clear(oc)
            for ic in range(16):
                act_war[ic] = (sPE, down_pe_last)

        xsl = [vf(c.B, 8192, 1024), vf(c.B, 9216, 1024)]
        xosl = [vf(c.B, 10240, 1024), vf(c.B, 11264, 1024)]
        P.wait("sync", sPE, P.sems["sPE"].n)
        vstl = [None, None]
        for oc in range(16):
            sl = oc % 2
            tok_x = c.ring_xep.load("sync", oc, lambda e, oc=oc, sl=sl, t0=t0: e.dma_start(
                out=xsl[sl], in_=x_in_ap[oc * 128:(oc + 1) * 128, t0:t0 + TH]))
            P.wait("vector", *tok_x)
            c.ring_xo.acquire(P, "vector", oc)
            vxo = P.op("vector", lambda e, oc=oc, sl=sl: e.scalar_tensor_tensor(
                xosl[sl], vf(c.A, oc * 1024, 1024), dh[:, oc:oc + 1], xsl[sl],
                ALU.mult, ALU.add), inc=(sDVE, 1))
            c.ring_xep.consumed(oc, sDVE, vxo)
            c.oa_guard_set(oc, sDVE, vxo)
            P.wait("sync", sDVE, vxo)
            stsem = c.st_sems[sl]
            vst = P.op("sync", lambda e, oc=oc, sl=sl, t0=t0: e.dma_start(
                out=x_out_ap[oc * 128:(oc + 1) * 128, t0:t0 + TH], in_=xosl[sl]),
                inc=(stsem, 16))
            vstl[sl] = vst
            c.ring_xo.release(oc, stsem, vst)
        act_war[4] = act_war[5] = (c.st_sems[0], vstl[0])
        act_war[6] = act_war[7] = (c.st_sems[1], vstl[1])
        for st in c.st_sems:
            P.wait("sync", st, st.n)
        if th == 1:
            P.barrier()
            c.reset_phase_guards()
    return



def emit_l0_ffn(c, x_in_ap, x_out_ap, w, load_w):
    P = c.P
    sPE = P.sem("sPE"); sACT = P.sem("sACT"); sDVE = P.sem("sDVE"); sPOOL = P.sem("sPOOL")
    sSCL = P.sem("sSCL")

    y_off = 0
    act_off = 8192
    yv = lambda kc: vbf(c.B, y_off + kc * 512, 1024)
    actv = lambda ic: vbf(c.B, act_off + ic * 512, 1024)

    rw_f = vf(c.W, 4096, 128)
    negsw = vf(c.W, 4224, 8)[0:1, :]
    sgw_b = vbf(c.W, 4232, 32)
    combW = vbf(c.W, 4248, 64)
    if load_w:
        P.op("sync", lambda e: e.dma_start(out=rw_f, in_=w["rw"]), inc=(sSCL, 16))
        sSGW = P.sem("sSGW")
        v2 = P.op("gpsimd", lambda e: e.dma_start(out=sgw_b[:, 0:16], in_=w["sg"]),
                  inc=(sSGW, 16))
        P.wait("tensor", sSCL, P.sems["sSCL"].n)
        P.wait("tensor", sSGW, v2)
        ps_wait(c, P, "tensor", 3)
        vsw = None
        for kc in range(16):
            vsw = P.op("tensor", lambda e, kc=kc: e.matmul(
                c.PS[3][0:1, 0:8], c.ones_col_f,
                rw_f[:, kc * 8:(kc + 1) * 8],
                start=(kc == 0), stop=(kc == 15)),
                inc=(sPE, 1) if kc == 15 else None)
        P.wait("scalar", sPE, vsw)
        vnw = P.op("scalar", lambda e: e.activation(negsw, c.PS[3][0:1, 0:8], AF.Copy,
                                                    scale=-1.0), inc=(sACT, 1))
        ps_set(c, 3, sACT, vnw)
        c.l0_negsw_v = vnw
    P.wait("tensor", sACT, c.l0_negsw_v)

    logitsT = c.S[0:8, 0:1024]
    router = {"rw_r": rw_f, "negsw": negsw, "out": logitsT}

    gslot = [vbf(c.W, 0, 2048), vbf(c.W, 1024, 2048)]
    uslot = [vbf(c.W, 2048, 2048), vbf(c.W, 3072, 2048)]
    dslot = gslot
    ring_g, ring_u, ring_d = c.ring_g, c.ring_u, c.ring_g

    sgrow_b = c.R[0:1, 512:1024].bitcast(BF16)
    crow_b = c.R[0:1, 0:512].bitcast(BF16)
    bc = vf(c.B, 13824, 1024)
    sbc = vf(c.B, 14848, 1024)

    for th in range(2):
        t0 = th * TH
        v_y = emit_ln(c, x_in_ap, t0, TH, c.B, y_off, router=router, split4=(th == 0))
        v_logits = P.sems["sDVE"].n

        P.wait("tensor", sDVE, v_y)
        ps_wait(c, P, "tensor", 3)
        vsg = None
        for tt in range(2):
            for kc in range(16):
                vsg = P.op("tensor", lambda e, kc=kc, tt=tt: e.matmul(
                    c.PS[3][0:1, tt * 512:(tt + 1) * 512], sgw_b[:, kc:kc + 1],
                    yv(kc)[:, tt * 512:(tt + 1) * 512], start=(kc == 0), stop=(kc == 15)),
                    inc=(sPE, 1) if (tt == 1 and kc == 15) else None)
        P.wait("scalar", sPE, vsg)
        vsgr = P.op("scalar", lambda e: e.activation(sgrow_b, c.PS[3][0:1, 0:1024],
                                                     AF.Sigmoid), inc=(sACT, 1))
        ps_set(c, 3, sACT, vsgr)

        ltm = vf(c.C, 4096, 64)
        P.wait("tensor", sDVE, v_logits)
        ps_wait(c, P, "tensor", 3)
        vtr = None
        for i in range(8):
            vtr = P.op("tensor", lambda e, i=i: e.transpose(
                c.PS[3][:, i * 8:(i + 1) * 8].bitcast(F32),
                logitsT[:, i * 128:(i + 1) * 128], c.idf[0:8, 0:8]),
                inc=(sPE, 1) if i == 7 else None)
        P.wait("vector", sPE, vtr)
        P.wait("vector", sPOOL, P.sems["sPOOL"].n)
        vltm = P.op("vector", lambda e: e.tensor_copy(ltm, c.PS[3][:, 0:64]), inc=(sDVE, 1))
        ps_set(c, 3, sDVE, vltm)

        m1 = vf(c.C, 4160, 1)
        nm1 = vf(c.C, 4161, 1)
        m2 = vf(c.C, 4162, 1)
        ssum = vf(c.C, 4163, 1)
        rsum = vf(c.C, 4164, 1)
        eq1 = vf(c.C, 4168, 8)
        msk = vf(c.C, 4176, 8)
        e1t = vf(c.C, 4184, 8)
        wv = vf(c.C, 4192, 8)
        for i in range(8):
            lt = ltm[:, i * 8:(i + 1) * 8]
            P.wait("vector", sDVE, P.sems["sDVE"].n)
            v1 = P.op("vector", lambda e, lt=lt: e.reduce_max(m1, lt, axis=AX), inc=(sDVE, 1))
            v1b = P.op("vector", lambda e, lt=lt: e.reduce_max(nm1, lt, axis=AX, negate=True),
                       inc=(sDVE, 1))
            P.wait("vector", sDVE, v1b)
            v2_ = P.op("vector", lambda e, lt=lt: e.tensor_scalar(
                eq1, lt, m1, None, ALU.is_ge), inc=(sDVE, 1))
            P.wait("vector", sDVE, v2_)
            v3_ = P.op("vector", lambda e, lt=lt: e.scalar_tensor_tensor(
                msk, eq1, -1e30, lt, ALU.mult, ALU.add), inc=(sDVE, 1))
            P.wait("vector", sDVE, v3_)
            v4_ = P.op("vector", lambda e: e.reduce_max(m2, msk, axis=AX), inc=(sDVE, 1))
            P.wait("scalar", sDVE, v1b)
            v5_ = P.op("scalar", lambda e, lt=lt: e.activation(
                e1t, lt, AF.Exp, bias=nm1), inc=(sACT, 1))
            P.wait("vector", sDVE, v4_)
            P.wait("vector", sACT, v5_)
            v6_ = P.op("vector", lambda e, lt=lt: e.tensor_scalar(
                wv, lt, m2, None, ALU.is_ge), inc=(sDVE, 1))
            P.wait("vector", sDVE, v6_)
            v7_ = P.op("vector", lambda e: e.tensor_mul(wv, wv, e1t), inc=(sDVE, 1))
            P.wait("vector", sDVE, v7_)
            v8_ = P.op("vector", lambda e: e.reduce_sum(ssum, wv, axis=AX), inc=(sDVE, 1))
            P.wait("vector", sDVE, v8_)
            v9_ = P.op("vector", lambda e: e.reciprocal(rsum, ssum), inc=(sDVE, 1))
            P.wait("vector", sDVE, v9_)
            va_ = P.op("vector", lambda e, i=i: e.tensor_scalar(
                combW[:, i * 8:(i + 1) * 8], wv, rsum, None, ALU.mult), inc=(sDVE, 1))
            v_e1t_act = v5_
        v_comb = P.sems["sDVE"].n
        c.ring_xw.also_consumed(0, sDVE, v_comb)
        c.ring_xw.also_consumed(0, sACT, v_e1t_act)

        def piece_weights(p):
            if p < 4:
                q = p
                return (lambda i: w["shg"][11 * q + i],
                        lambda i: w["shu"][11 * q + i],
                        lambda oc: w["shd"][oc, :, 1408 * q:1408 * (q + 1)])
            e_ = p - 4
            return (lambda i: w["gu"][e_, i],
                    lambda i: w["gu"][e_, 11 + i],
                    lambda oc: w["dn"][e_, oc])

        silu_sl = [vf(c.S, 0, 1024), vf(c.S, 1024, 1024)]
        silu_guard = [None, None]
        sbc_guard = None
        act_war = [None] * 11
        down_pe_last = None

        for p in range(12):
            gap, uap, dap = piece_weights(p)
            if p == 0:
                P.wait("tensor", sACT, vsgr)
                P.wait("tensor", sDVE, v_comb)
                ps_wait(c, P, "tensor", 2)
                vbcm = None
                for tt in range(2):
                    vbcm = P.op("tensor", lambda e, tt=tt: e.matmul(
                        c.PS[2][:, tt * 512:(tt + 1) * 512], c.ones_row_b,
                        sgrow_b[0:1, tt * 512:(tt + 1) * 512], start=True, stop=True),
                        inc=(sPE, 1) if tt == 1 else None)
                P.wait("scalar", sPE, vbcm)
                P.wait("scalar", sDVE, v_comb)
                vbc = P.op("scalar", lambda e: e.activation(bc, c.PS[2][:, :], AF.Copy,
                                                            scale=0.5), inc=(sACT, 1))
                ps_set(c, 2, sACT, vbc)
                bc_ready = vbc
            elif p >= 4:
                e_ = p - 4
                ps_wait(c, P, "tensor", 3)
                vcr = None
                for i in range(8):
                    vcr = P.op("tensor", lambda e, i=i, e_=e_: e.matmul(
                        c.PS[3][0:1, i * 128:(i + 1) * 128],
                        combW[:, i * 8 + e_:i * 8 + e_ + 1], c.idb,
                        start=True, stop=True), inc=(sPE, 1) if i == 7 else None)
                P.wait("scalar", sPE, vcr)
                vcw = P.op("scalar", lambda e: e.activation(crow_b, c.PS[3][0:1, 0:1024],
                                                            AF.Copy), inc=(sACT, 1))
                ps_set(c, 3, sACT, vcw)
                P.wait("tensor", sACT, vcw)
                ps_wait(c, P, "tensor", 2)
                vbcm = None
                for tt in range(2):
                    vbcm = P.op("tensor", lambda e, tt=tt: e.matmul(
                        c.PS[2][:, tt * 512:(tt + 1) * 512], c.ones_row_b,
                        crow_b[0:1, tt * 512:(tt + 1) * 512], start=True, stop=True),
                        inc=(sPE, 1) if tt == 1 else None)
                P.wait("scalar", sPE, vbcm)
                if c.bc_pool_guard is not None:
                    P.wait("scalar", c.bc_pool_guard[0], c.bc_pool_guard[1])
                vbc = P.op("scalar", lambda e: e.activation(bc, c.PS[2][:, :], AF.Copy,
                                                            scale=0.5), inc=(sACT, 1))
                ps_set(c, 2, sACT, vbc)
                bc_ready = vbc

            for i in range(11):
                icg = p * 11 + i
                par = icg % 2
                tok_g = ring_g.load("gpsimd", icg, lambda e, par=par, a=gap(i): e.dma_start(
                    out=gslot[par], in_=a))
                tok_u = ring_u.load("gpsimd", icg, lambda e, par=par, a=uap(i): e.dma_start(
                    out=uslot[par], in_=a))
                pg, pu = c.PS[par], c.PS[2 + par]
                P.wait("tensor", *tok_g)
                ps_wait(c, P, "tensor", par)
                vgl = None
                for kc in range(16):
                    for tt in range(2):
                        vgl = P.op("tensor", lambda e, kc=kc, tt=tt, par=par, pg=pg: e.matmul(
                            pg[:, tt * 512:(tt + 1) * 512],
                            gslot[par][:, kc * 128:(kc + 1) * 128],
                            yv(kc)[:, tt * 512:(tt + 1) * 512],
                            start=(kc == 0), stop=(kc == 15)),
                            inc=(sPE, 1) if (kc == 15 and tt == 1) else None)
                v_g = vgl
                ring_g.consumed(icg, sPE, v_g)
                P.wait("tensor", *tok_u)
                ps_wait(c, P, "tensor", 2 + par)
                vul = None
                for kc in range(16):
                    for tt in range(2):
                        vul = P.op("tensor", lambda e, kc=kc, tt=tt, par=par, pu=pu: e.matmul(
                            pu[:, tt * 512:(tt + 1) * 512],
                            uslot[par][:, kc * 128:(kc + 1) * 128],
                            yv(kc)[:, tt * 512:(tt + 1) * 512],
                            start=(kc == 0), stop=(kc == 15)),
                            inc=(sPE, 1) if (kc == 15 and tt == 1) else None)
                v_u = vul
                ring_u.consumed(icg, sPE, v_u)

                P.wait("scalar", sPE, v_g)
                if silu_guard[par] is not None:
                    P.wait("scalar", *silu_guard[par])
                vsil = P.op("scalar", lambda e, par=par, pg=pg: e.activation(
                    silu_sl[par], pg[:, :], AF.Silu), inc=(sACT, 1))
                ps_set(c, par, sACT, vsil)

                P.wait("gpsimd", sACT, vsil)
                P.wait("gpsimd", sACT, bc_ready)
                if sbc_guard is not None:
                    P.wait("gpsimd", sbc_guard[0], sbc_guard[1])
                vsbc = P.op("gpsimd", lambda e, par=par: e.tensor_tensor(
                    sbc, silu_sl[par], bc, ALU.mult), inc=(sPOOL, 1))
                c.bc_pool_guard = (sPOOL, vsbc)
                silu_guard[par] = (sPOOL, vsbc)

                P.wait("vector", sPE, v_u)
                P.wait("vector", sPOOL, vsbc)
                if act_war[i] is not None:
                    P.wait("vector", act_war[i][0], act_war[i][1])
                vact = P.op("vector", lambda e, i=i, pu=pu: e.tensor_tensor(
                    actv(i), pu[:, :], sbc, ALU.mult), inc=(sDVE, 1))
                ps_set(c, 2 + par, sDVE, vact)
                sbc_guard = (sDVE, vact)
            v_acts = P.sems["sDVE"].n

            P.wait("tensor", sDVE, v_acts)
            for oc in range(16):
                dpar = oc % 2
                tok_d = ring_d.load("gpsimd", oc, lambda e, a=dap(oc), dpar=dpar: e.dma_start(
                    out=dslot[dpar][:, 0:1408], in_=a))
                P.wait("tensor", *tok_d)
                ps_wait(c, P, "tensor", dpar)
                vdn = None
                for tt in range(2):
                    for ic in range(11):
                        vdn = P.op("tensor", lambda e, ic=ic, tt=tt, dpar=dpar: e.matmul(
                            c.PS[dpar][:, tt * 512:(tt + 1) * 512],
                            dslot[dpar][:, ic * 128:(ic + 1) * 128],
                            actv(ic)[:, tt * 512:(tt + 1) * 512],
                            start=(ic == 0), stop=(ic == 10)),
                            inc=(sPE, 1) if (tt == 1 and ic == 10) else None)
                ring_d.consumed(oc, sPE, vdn)
                down_pe_last = vdn
                P.wait("vector", sPE, vdn)
                oa = vf(c.A, oc * 1024, 1024)
                c.oa_guard_wait(P, "vector", oc)
                if p == 0:
                    vev = P.op("vector", lambda e, oa=oa, dpar=dpar: e.tensor_copy(
                        oa, c.PS[dpar][:, :]), inc=(sDVE, 1))
                else:
                    vev = P.op("vector", lambda e, oa=oa, dpar=dpar: e.tensor_add(
                        oa, oa, c.PS[dpar][:, :]), inc=(sDVE, 1))
                ps_set(c, dpar, sDVE, vev)
                c.oa_guard_clear(oc)
            for i in range(11):
                act_war[i] = (sPE, down_pe_last)

        xsl = [vf(c.B, 0, 1024), vf(c.B, 1024, 1024)]
        xosl = [vf(c.B, 2048, 1024), vf(c.B, 3072, 1024)]
        P.wait("sync", sPE, P.sems["sPE"].n)
        for oc in range(16):
            sl = oc % 2
            tok_x = c.ring_xep.load("sync", oc, lambda e, oc=oc, sl=sl, t0=t0: e.dma_start(
                out=xsl[sl], in_=x_in_ap[oc * 128:(oc + 1) * 128, t0:t0 + TH]))
            P.wait("vector", *tok_x)
            c.ring_xo.acquire(P, "vector", oc)
            vxo = P.op("vector", lambda e, oc=oc, sl=sl: e.tensor_add(
                xosl[sl], vf(c.A, oc * 1024, 1024), xsl[sl]), inc=(sDVE, 1))
            c.ring_xep.consumed(oc, sDVE, vxo)
            c.oa_guard_set(oc, sDVE, vxo)
            P.wait("sync", sDVE, vxo)
            stsem = c.st_sems[sl]
            vst = P.op("sync", lambda e, oc=oc, sl=sl, t0=t0: e.dma_start(
                out=x_out_ap[oc * 128:(oc + 1) * 128, t0:t0 + TH], in_=xosl[sl]),
                inc=(stsem, 16))
            c.ring_xo.release(oc, stsem, vst)
        for st in c.st_sems:
            P.wait("sync", st, st.n)
        if th == 1:
            P.barrier()
            c.reset_phase_guards()
        else:
            P.wait("sync", P.sem("sDVE"), P.sems["sDVE"].n)
            for st in c.st_sems:
                P.wait("vector", st, st.n)
        c.bc_pool_guard = None
    return


def prep_l0_ffn(router_weight, gate_up_proj, down_proj, sh_gate_w, sh_up_w,
                sh_down_w, shared_gate_w):
    d = {}
    d["l0_shg"] = lhsT_tiles(np.ascontiguousarray(sh_gate_w.T))
    d["l0_shu"] = lhsT_tiles(np.ascontiguousarray(sh_up_w.T))
    d["l0_shd"] = lhsT_tiles(np.ascontiguousarray(sh_down_w.T))
    d["l0_gu"] = np.stack([lhsT_tiles(np.ascontiguousarray(gate_up_proj[e].T))
                           for e in range(8)])
    d["l0_dn"] = np.stack([lhsT_tiles(np.ascontiguousarray(down_proj[e].T))
                           for e in range(8)])
    d["l0_rw"] = skinny_lhsT(np.ascontiguousarray(router_weight.T)).reshape(128, 128)
    d["l0_sg"] = skinny_lhsT(np.ascontiguousarray(shared_gate_w.T)).reshape(128, 16)
    return d


def declare_l0_ffn(nc):
    a = {}
    a["shg"] = nc.dram_tensor("l0_shg", [44, 128, 2048], F32, kind="ExternalInput").ap()
    a["shu"] = nc.dram_tensor("l0_shu", [44, 128, 2048], F32, kind="ExternalInput").ap()
    a["shd"] = nc.dram_tensor("l0_shd", [16, 128, 5632], F32, kind="ExternalInput").ap()
    a["gu"] = nc.dram_tensor("l0_gu", [8, 22, 128, 2048], F32, kind="ExternalInput").ap()
    a["dn"] = nc.dram_tensor("l0_dn", [8, 16, 128, 1408], F32, kind="ExternalInput").ap()
    a["rw"] = nc.dram_tensor("l0_rw", [128, 128], F32, kind="ExternalInput").ap()
    a["sg"] = nc.dram_tensor("l0_sg", [128, 16], F32, kind="ExternalInput").ap()
    return a



SCALE_L0 = 1.0 / float(np.sqrt(128.0))


def emit_l0_attn(c, x_in_ap, x_out_ap, a, load_w):
    P = c.P
    sPE = P.sem("sPE"); sACT = P.sem("sACT"); sDVE = P.sem("sDVE"); sPOOL = P.sem("sPOOL")
    sLA = P.sem("sLA")
    sLS = P.sem("sLS")

    basis_b = [vbf(c.A, x * 512, 1024) for x in range(4)]
    wT = [vbf(c.A, 2048 + x * 1024, 2048)[0:64, :] for x in range(4)]
    res = [vbf(c.A, 6144 + x * 1024, 2048)[0:64, :] for x in range(3)]
    res_o = vbf(c.A, 6144, 2048)[0:64, :]
    stg1 = vf(c.A, 9216, 2048)[0:64, :]
    stg2 = vf(c.A, 11264, 2048)[0:64, :]
    stg3 = vf(c.A, 13312, 2048)[0:64, :]

    k_h = c.FR[:, 0:2048]
    v_h = vbf(c.C, 2048, 2048)
    q_blk = c.FR[:, 2048:2560]
    probsT = vbf(c.W, 0, 8192)
    probsT3 = probsT.rearrange("p (j n) -> p j n", j=16)
    m_a2 = [vf(c.C, 7424, 1), vf(c.C, 7426, 1)]
    m_b2 = [vf(c.C, 7425, 1), vf(c.C, 7427, 1)]
    mbc = vf(c.C, 0, 512)
    rbc = vf(c.C, 512, 512)
    tmpx = [vf(c.C, 1024, 512), vf(c.C, 1536, 512)]
    mrow = vf(c.C, 3072, 512)[0:1, :]
    drow = vf(c.C, 3584, 512)[0:1, :]
    rrow = vf(c.C, 4096, 512)[0:1, :]
    y_off = 0
    yv = lambda kc: vbf(c.B, y_off + kc * 1024, 2048)
    ao = lambda h: vbf(c.B, h * 1024, 2048)

    if load_w:
        for x in range(4):
            P.op("gpsimd", lambda e, x=x: e.dma_start(
                out=basis_b[x], in_=a["basis"][x].rearrange("p k h -> p (k h)")),
                inc=(sLA, 16))
    vprev_dve = None
    for x in range(4):
        tok1 = c.ring_stg1.load("sync", x, lambda e, x=x: e.dma_start(out=stg1, in_=a["phase"][x]))
        tok2 = c.ring_stg2.load("sync", x, lambda e, x=x: e.dma_start(out=stg2, in_=a["amp"][x]))
        P.wait("scalar", *tok1)
        if vprev_dve is not None:
            P.wait("scalar", sDVE, vprev_dve)
        vsin = P.op("scalar", lambda e: e.activation(stg3, stg1, AF.Sin,
                                                     bias=c.pi2_col[0:64, :]), inc=(sACT, 1))
        c.ring_stg1.consumed(x, sACT, vsin)
        P.wait("vector", *tok2)
        P.wait("vector", sACT, vsin)
        vw = P.op("vector", lambda e, x=x: e.tensor_tensor(wT[x], stg3, stg2, ALU.mult),
                  inc=(sDVE, 1))
        c.ring_stg2.consumed(x, sDVE, vw)
        vprev_dve = vw
    v_wT = vprev_dve
    P.wait("tensor", sLA, P.sems["sLA"].n)
    P.wait("tensor", sDVE, v_wT)

    v_y = None
    for half in range(2):
        emit_ln(c, x_in_ap, half * TH, TH, c.B, y_off, y_row=1024, y_col=half * TH,
                split4=(half == 0))
    v_y = P.sems["sDVE"].n

    P.wait("tensor", sDVE, v_y)
    for x in range(3):
        for tt in range(4):
            hh = tt % 2
            ps_wait(c, P, "tensor", 3, halves=(hh,))
            vm = None
            for kc in range(16):
                vm = P.op("tensor", lambda e, x=x, kc=kc, tt=tt, hh=hh: e.matmul(
                    c.PS[3][0:64, hh * 512:(hh + 1) * 512],
                    basis_b[x][:, kc * 64:(kc + 1) * 64],
                    yv(kc)[:, tt * 512:(tt + 1) * 512],
                    start=(kc == 0), stop=(kc == 15)),
                    inc=(sPE, 1) if kc == 15 else None)
            P.wait("scalar", sPE, vm)
            vev = P.op("scalar", lambda e, x=x, tt=tt, hh=hh: e.activation(
                res[x][:, tt * 512:(tt + 1) * 512], c.PS[3][0:64, hh * 512:(hh + 1) * 512],
                AF.Copy), inc=(sACT, 1))
            ps_set(c, 3, sACT, vev, halves=(hh,))
    v_res = P.sems["sACT"].n

    k_guard = None
    v_guard = None
    q_guard = None
    pbf_guard = None
    pT_guard = None
    for h in range(16):
        P.wait("tensor", sACT, v_res)
        for tt in range(4):
            hh = tt % 2
            ps_wait(c, P, "tensor", 3, halves=(hh,))
            if k_guard is not None and tt < 2:
                P.wait("tensor", k_guard[0], k_guard[1])
            vm = P.op("tensor", lambda e, h=h, tt=tt, hh=hh: e.matmul(
                c.PS[3][:, hh * 512:(hh + 1) * 512], wT[1][:, h * 128:(h + 1) * 128],
                res[1][:, tt * 512:(tt + 1) * 512], start=True, stop=True), inc=(sPE, 1))
            P.wait("scalar", sPE, vm)
            vev = P.op("scalar", lambda e, tt=tt, hh=hh: e.activation(
                k_h[:, tt * 512:(tt + 1) * 512], c.PS[3][:, hh * 512:(hh + 1) * 512],
                AF.Copy), inc=(sACT, 1))
            ps_set(c, 3, sACT, vev, halves=(hh,))
        v_kh = P.sems["sACT"].n
        for grp in range(4):
            hh = grp % 2
            ps_wait(c, P, "tensor", 3, halves=(hh,))
            if v_guard is not None and grp < 2:
                P.wait("tensor", v_guard[0], v_guard[1])
            vm = None
            for j in range(4):
                tc = grp * 4 + j
                vm = P.op("tensor", lambda e, h=h, tc=tc, j=j, hh=hh: e.matmul(
                    c.PS[3][:, hh * 512 + j * 128: hh * 512 + (j + 1) * 128],
                    res[2][:, tc * 128:(tc + 1) * 128],
                    wT[2][:, h * 128:(h + 1) * 128],
                    start=True, stop=True), inc=(sPE, 1) if j == 3 else None)
            P.wait("scalar", sPE, vm)
            vev = P.op("scalar", lambda e, grp=grp, hh=hh: e.activation(
                v_h[:, grp * 512:(grp + 1) * 512], c.PS[3][:, hh * 512:(hh + 1) * 512],
                AF.Copy), inc=(sACT, 1))
            ps_set(c, 3, sACT, vev, halves=(hh,))
        v_vh = P.sems["sACT"].n

        for qb in range(4):
            hh = 0
            ps_wait(c, P, "tensor", 3, halves=(hh,))
            vm = P.op("tensor", lambda e, h=h, qb=qb: e.matmul(
                c.PS[3][:, 0:512], wT[0][:, h * 128:(h + 1) * 128],
                res[0][:, qb * 512:(qb + 1) * 512], start=True, stop=True), inc=(sPE, 1))
            P.wait("scalar", sPE, vm)
            vq = P.op("scalar", lambda e: e.activation(q_blk, c.PS[3][:, 0:512], AF.Copy),
                      inc=(sACT, 1))
            ps_set(c, 3, sACT, vq, halves=(hh,))
            P.wait("tensor", sACT, vq)
            P.wait("tensor", sACT, v_kh)

            for qc in range(4):
                par = qc % 2
                pa, pb = 0, 1
                ma, mb = m_a2[par], m_b2[par]
                ps_wait(c, P, "tensor", pa)
                ps_wait(c, P, "tensor", pb)
                vsc = None
                for kt in range(4):
                    pst = pa if kt < 2 else pb
                    vsc = P.op("tensor", lambda e, qc=qc, kt=kt, pst=pst: e.matmul(
                        c.PS[pst][:, (kt % 2) * 512:(kt % 2 + 1) * 512],
                        q_blk[:, qc * 128:(qc + 1) * 128],
                        k_h[:, kt * 512:(kt + 1) * 512], start=True, stop=True),
                        inc=(sPE, 1) if kt == 3 else None)
                P.wait("vector", sPE, vsc)
                if c.stat_guard2[par] is not None:
                    P.wait("vector", c.stat_guard2[par][0], c.stat_guard2[par][1])
                vr1 = P.op("vector", lambda e, pa=pa, ma=ma: e.reduce_max(
                    ma, c.PS[pa][:, :], axis=AX), inc=(sDVE, 1))
                vr2 = P.op("vector", lambda e, pb=pb, mb=mb: e.reduce_max(
                    mb, c.PS[pb][:, :], axis=AX), inc=(sDVE, 1))
                ps_set(c, pa, sDVE, vr2)
                ps_set(c, pb, sDVE, vr2)
                P.wait("vector", sDVE, vr2)
                vr3 = P.op("vector", lambda e, ma=ma, mb=mb: e.tensor_max(ma, ma, mb),
                           inc=(sDVE, 1))
                c.stat_guard2[par] = (sDVE, vr3)
                P.wait("tensor", sDVE, vr3)
                if qc == 0:
                    ps_wait(c, P, "tensor", 3, halves=(0,))
                vmr = P.op("tensor", lambda e, qc=qc, ma=ma: e.matmul(
                    c.PS[3][0:1, qc * 128:(qc + 1) * 128], ma, c.idf,
                    start=(qc == 0), stop=(qc == 3)), inc=(sPE, 1))
                if qc == 3:
                    ps_set(c, 3, sPE, vmr, halves=(0,))
            P.wait("scalar", sPE, vmr)
            vmrow = P.op("scalar", lambda e: e.activation(mrow, c.PS[3][0:1, 0:512],
                                                          AF.Copy, scale=SCALE_L0),
                         inc=(sACT, 1))
            P.wait("tensor", sACT, vmrow)
            ps_wait(c, P, "tensor", 3, halves=(0,))
            vbcm_ = P.op("tensor", lambda e: e.matmul(
                c.PS[3][:, 0:512], c.ones_row_f, mrow, start=True, stop=True),
                inc=(sPE, 1))
            P.wait("scalar", sPE, vbcm_)
            if c.mbc_guard is not None:
                P.wait("scalar", c.mbc_guard[0], c.mbc_guard[1])
            vmbc = P.op("scalar", lambda e: e.activation(mbc, c.PS[3][:, 0:512], AF.Copy),
                        inc=(sACT, 1))
            ps_set(c, 3, sACT, vmbc, halves=(0,))

            P.wait("vector", sACT, vmbc)
            tx_guard = [None, None]
            vexp = None
            for kc in range(16):
                sl2 = kc % 2
                ps_wait(c, P, "tensor", 2, halves=(sl2,))
                vst_ = P.op("tensor", lambda e, kc=kc, sl2=sl2: e.matmul(
                    c.PS[2][:, sl2 * 512:(sl2 + 1) * 512],
                    k_h[:, kc * 128:(kc + 1) * 128], q_blk,
                    start=True, stop=True), inc=(sPE, 1))
                P.wait("vector", sPE, vst_)
                if tx_guard[sl2] is not None:
                    P.wait("vector", tx_guard[sl2][0], tx_guard[sl2][1])
                vtx = P.op("vector", lambda e, kc=kc, sl2=sl2: e.scalar_tensor_tensor(
                    tmpx[sl2], c.PS[2][:, sl2 * 512:(sl2 + 1) * 512], SCALE_L0,
                    mbc, ALU.mult, ALU.subtract), inc=(sDVE, 1))
                ps_set(c, 2, sDVE, vtx, halves=(sl2,))
                P.wait("scalar", sDVE, vtx)
                if kc == 0 and c.pT_war is not None:
                    P.wait("scalar", c.pT_war[0], c.pT_war[1])
                vexp = P.op("scalar", lambda e, kc=kc, sl2=sl2: e.activation(
                    probsT3[:, kc, :], tmpx[sl2], AF.Exp), inc=(sACT, 1))
                tx_guard[sl2] = (sACT, vexp)
            v_probs = P.sems["sACT"].n

            P.wait("tensor", sACT, v_probs)
            ps_wait(c, P, "tensor", 3, halves=(0,))
            vdn = None
            for kc in range(16):
                vdn = P.op("tensor", lambda e, kc=kc: e.matmul(
                    c.PS[3][0:1, 0:512], c.ones_col_b, probsT3[:, kc, :],
                    start=(kc == 0), stop=(kc == 15)), inc=(sPE, 1) if kc == 15 else None)
            P.wait("scalar", sPE, vdn)
            vdr = P.op("scalar", lambda e: e.activation(drow, c.PS[3][0:1, 0:512], AF.Copy),
                       inc=(sACT, 1))
            P.wait("vector", sACT, vdr)
            vrr = P.op("vector", lambda e: e.reciprocal(rrow, drow), inc=(sDVE, 1))
            P.wait("tensor", sDVE, vrr)
            vbr = P.op("tensor", lambda e: e.matmul(
                c.PS[3][:, 0:512], c.ones_row_f, rrow, start=True, stop=True),
                inc=(sPE, 1))
            P.wait("scalar", sPE, vbr)
            if c.rbc_guard is not None:
                P.wait("scalar", c.rbc_guard[0], c.rbc_guard[1])
            vrbc = P.op("scalar", lambda e: e.activation(rbc, c.PS[3][:, 0:512], AF.Copy),
                        inc=(sACT, 1))
            ps_set(c, 3, sACT, vrbc, halves=(0,))
            c.mbc_guard = None

            P.wait("tensor", sACT, v_vh)
            ps_wait(c, P, "tensor", 3, halves=(1,))
            vpv = None
            for kc in range(16):
                vpv = P.op("tensor", lambda e, kc=kc: e.matmul(
                    c.PS[3][:, 512:1024], v_h[:, kc * 128:(kc + 1) * 128],
                    probsT3[:, kc, :], start=(kc == 0), stop=(kc == 15)),
                    inc=(sPE, 1) if kc == 15 else None)
            c.pT_war = (sPE, vpv)
            P.wait("vector", sPE, vpv)
            P.wait("vector", sACT, vrbc)
            vout = P.op("vector", lambda e, h=h, qb=qb: e.tensor_mul(
                ao(h)[:, qb * 512:(qb + 1) * 512], c.PS[3][:, 512:1024], rbc),
                inc=(sDVE, 1))
            ps_set(c, 3, sDVE, vout, halves=(1,))
            c.rbc_guard = (sDVE, vout)
    v_heads = P.sems["sDVE"].n

    P.wait("tensor", sDVE, v_heads)
    for tt in range(4):
        hh = tt % 2
        ps_wait(c, P, "tensor", 3, halves=(hh,))
        vm = None
        for kc in range(16):
            vm = P.op("tensor", lambda e, kc=kc, tt=tt, hh=hh: e.matmul(
                c.PS[3][0:64, hh * 512:(hh + 1) * 512],
                basis_b[3][:, kc * 64:(kc + 1) * 64],
                ao(kc)[:, tt * 512:(tt + 1) * 512],
                start=(kc == 0), stop=(kc == 15)), inc=(sPE, 1) if kc == 15 else None)
        P.wait("scalar", sPE, vm)
        vev = P.op("scalar", lambda e, tt=tt, hh=hh: e.activation(
            res_o[:, tt * 512:(tt + 1) * 512], c.PS[3][0:64, hh * 512:(hh + 1) * 512],
            AF.Copy), inc=(sACT, 1))
        ps_set(c, 3, sACT, vev, halves=(hh,))
    v_reso = P.sems["sACT"].n
    P.wait("tensor", sACT, v_reso)

    xsl = [vf(c.C, 0, 2048), vf(c.C, 4096, 2048)]
    xosl = [vf(c.C, 2048, 2048), vf(c.C, 6144, 2048)]
    P.wait("sync", sDVE, P.sems["sDVE"].n)
    P.wait("sync", sPE, P.sems["sPE"].n)
    for oc in range(16):
        sl = oc % 2
        tok_x = c.ring_xep.load("sync", oc, lambda e, oc=oc, sl=sl: e.dma_start(
            out=xsl[sl], in_=x_in_ap[oc * 128:(oc + 1) * 128, :]))
        vdv = None
        for tt in range(4):
            hh = tt % 2
            ps_wait(c, P, "tensor", 3, halves=(hh,))
            vm = P.op("tensor", lambda e, oc=oc, tt=tt, hh=hh: e.matmul(
                c.PS[3][:, hh * 512:(hh + 1) * 512], wT[3][:, oc * 128:(oc + 1) * 128],
                res_o[:, tt * 512:(tt + 1) * 512], start=True, stop=True), inc=(sPE, 1))
            P.wait("vector", sPE, vm)
            if tt == 0:
                P.wait("vector", *tok_x)
                c.ring_xo.acquire(P, "vector", oc)
            vdv = P.op("vector", lambda e, tt=tt, hh=hh, sl=sl: e.tensor_add(
                xosl[sl][:, tt * 512:(tt + 1) * 512], c.PS[3][:, hh * 512:(hh + 1) * 512],
                xsl[sl][:, tt * 512:(tt + 1) * 512]), inc=(sDVE, 1))
            ps_set(c, 3, sDVE, vdv, halves=(hh,))
        c.ring_xep.consumed(oc, sDVE, vdv)
        P.wait("sync", sDVE, vdv)
        stsem = c.st_sems[sl]
        vst = P.op("sync", lambda e, oc=oc, sl=sl: e.dma_start(
            out=x_out_ap[oc * 128:(oc + 1) * 128, :], in_=xosl[sl]), inc=(stsem, 16))
        c.ring_xo.release(oc, stsem, vst)
    for st in c.st_sems:
        P.wait("sync", st, st.n)
    P.barrier()
    c.reset_phase_guards()
    return


def prep_l0_attn(basis_q, phase_q, amp_q, basis_k, phase_k, amp_k,
                 basis_v, phase_v, amp_v, basis_o, phase_o, amp_o):
    d = {}
    bs = [basis_q, basis_k, basis_v, basis_o]
    ph = [phase_q, phase_k, phase_v, phase_o]
    am = [amp_q, amp_k, amp_v, amp_o]
    d["l0a_basis"] = np.stack([skinny_lhsT(np.ascontiguousarray(b.T)) for b in bs])
    d["l0a_phase"] = np.stack([np.ascontiguousarray(p.T) for p in ph])
    d["l0a_amp"] = np.stack([np.ascontiguousarray(a.T) for a in am])
    return d


def declare_l0_attn(nc):
    a = {}
    a["basis"] = nc.dram_tensor("l0a_basis", [4, 128, 16, 64], F32, kind="ExternalInput").ap()
    a["phase"] = nc.dram_tensor("l0a_phase", [4, 64, 2048], F32, kind="ExternalInput").ap()
    a["amp"] = nc.dram_tensor("l0a_amp", [4, 64, 2048], F32, kind="ExternalInput").ap()
    return a



SCALE_L1 = 1.0 / float(np.sqrt(128.0))


def emit_l1_attn(c, x_in_ap, x_out_ap, a, spill):
    P = c.P
    sPE = P.sem("sPE"); sACT = P.sem("sACT"); sDVE = P.sem("sDVE")
    sVSP = [P.sem("sVSPa"), P.sem("sVSPb")]
    sQSP = [P.sem("sQSPa"), P.sem("sQSPb")]
    sKSP = [P.sem("sKSPa"), P.sem("sKSPb")]

    y_off = 0
    yv = lambda kc: vbf(c.B, y_off + kc * 1024, 2048)
    ao = lambda dc: vbf(c.B, dc * 1024, 2048)

    wf32 = [vf(c.W, 0, 2048)]
    whi = vbf(c.W, 2048, 2048)
    wlo = vbf(c.W, 3072, 2048)
    probsT = vbf(c.A, 4096, 4096)
    probsT3 = probsT.rearrange("p (j n) -> p j n", j=16)
    ylo = lambda kc: vbf(c.A, kc * 1024, 2048)

    kslot4 = [c.FR[:, j * 512:(j + 1) * 512] for j in range(4)]
    qslot = [c.FR[:, 2048:2304], c.FR[:, 2304:2560]]
    vslot = [vbf(c.C, 512, 2048), vbf(c.C, 1536, 2048)]
    vslot6 = vslot + [vbf(c.A, 12288 + i * 1024, 2048) for i in range(4)]
    c.vslot_kc = [None] * 6
    c.v_toks = {}
    psum_blk = [vf(c.C, 2560, 2048), vf(c.C, 4608, 2048)]
    probs_bf = vbf(c.C, 6656, 2048)
    m_a2 = [vf(c.C, 7680, 1), vf(c.C, 7688, 1)]
    m_b2 = [vf(c.C, 7681, 1), vf(c.C, 7689, 1)]
    negm2 = [vf(c.C, 7682, 1), vf(c.C, 7690, 1)]
    d1_2 = [vf(c.C, 7683, 1), vf(c.C, 7691, 1)]
    d2_2 = [vf(c.C, 7684, 1), vf(c.C, 7692, 1)]
    rcp2 = [vf(c.C, 7686, 1), vf(c.C, 7694, 1)]
    exp_sb2 = [vf(c.A, 8192, 2048), vf(c.A, 10240, 2048)]

    for half in range(2):
        emit_ln(c, x_in_ap, half * TH, TH, c.B, y_off, y_row=1024, y_col=half * TH,
                ylo_ar=c.A, ylo_off=0, split4=(half == 0))
    v_y = P.sems["sDVE"].n
    P.wait("tensor", sDVE, v_y)

    wvs = vbf(c.C, 4096, 8192).rearrange("p (kc n) -> p kc n", kc=16)
    vtmp = [vbf(c.C, 0, 1024), vbf(c.C, 512, 1024)]
    for dr in range(4):
        toks = []
        for kc in range(16):
            toks.append(c.ring_wv.load("gpsimd", dr * 16 + kc, lambda e, kc=kc, dr=dr: e.dma_start(
                out=wvs[:, kc, :], in_=a["wv"][kc, :, dr * 512:(dr + 1) * 512])))
        for t_ in toks:
            P.wait("tensor", *t_)
        for tc in range(16):
            sl = tc % 2
            hh = tc % 2
            ps_wait(c, P, "tensor", 3, halves=(hh,))
            vm = None
            for kc in range(16):
                vm = P.op("tensor", lambda e, tc=tc, kc=kc, hh=hh: e.matmul(
                    c.PS[3][:, hh * 512:(hh + 1) * 512],
                    yv(kc)[:, tc * 128:(tc + 1) * 128],
                    wvs[:, kc, :],
                    start=(kc == 0), stop=(kc == 15)), inc=(sPE, 1) if kc == 15 else None)
            vlast_wv = vm
            P.wait("scalar", sPE, vm)
            c.ring_vtmp.acquire(P, "scalar", tc)
            vevl = P.op("scalar", lambda e, sl=sl, hh=hh: e.activation(
                vtmp[sl][:, 0:512], c.PS[3][:, hh * 512:(hh + 1) * 512],
                AF.Copy), inc=(sACT, 1))
            ps_set(c, 3, sACT, vevl, halves=(hh,))
            P.wait("sync", sACT, vevl)
            vst = P.op("sync", lambda e, tc=tc, dr=dr, sl=sl: e.dma_start(
                out=spill["v_sp"][tc, :, dr * 512:(dr + 1) * 512], in_=vtmp[sl][:, 0:512]),
                inc=(sVSP[sl], 16))
            c.ring_vtmp.release(tc, sVSP[sl], vst)
        for kc in range(16):
            c.ring_wv.consumed(dr * 16 + kc, sPE, vlast_wv)

    ktmp = [c.FR[:, 0:512], c.FR[:, 512:1024]]
    P.wait("scalar", sVSP[0], sVSP[0].n)
    P.wait("scalar", sVSP[1], sVSP[1].n)
    whilo_guard = None
    for proj in range(2):
        wap = a["wk"] if proj == 0 else a["wq"]
        sp = spill["k_sp"] if proj == 0 else spill["q_sp"]
        ssp = sKSP if proj == 0 else sQSP
        for h in range(16):
            sl = h % 2
            idx = proj * 16 + h
            tok_w = c.ring_wf.load("sync", idx, lambda e, wap=wap, h=h: e.dma_start(
                out=wf32[0], in_=wap[h]))
            P.wait("scalar", *tok_w)
            if whilo_guard is not None:
                P.wait("scalar", whilo_guard[0], whilo_guard[1])
            vhi = P.op("scalar", lambda e: e.activation(whi, wf32[0], AF.Copy),
                       inc=(sACT, 1))
            P.wait("vector", sACT, vhi)
            vlo_ = P.op("vector", lambda e: e.tensor_sub(wlo, wf32[0], whi),
                        inc=(sDVE, 1))
            P.wait("tensor", sDVE, vlo_)
            vevl = None
            vmlast = None
            for tt in range(4):
                hh = tt % 2
                ps_wait(c, P, "tensor", 3, halves=(hh,))
                vm = None
                for term in range(3):
                    ya = yv if term != 1 else ylo
                    wa = whi if term != 2 else wlo
                    for kc in range(16):
                        vm = P.op("tensor", lambda e, tt=tt, kc=kc, hh=hh, ya=ya, wa=wa,
                                  term=term: e.matmul(
                            c.PS[3][:, hh * 512:(hh + 1) * 512],
                            wa[:, kc * 128:(kc + 1) * 128],
                            ya(kc)[:, tt * 512:(tt + 1) * 512],
                            start=(term == 0 and kc == 0), stop=(term == 2 and kc == 15)),
                            inc=(sPE, 1) if (term == 2 and kc == 15) else None)
                vmlast = vm
                P.wait("scalar", sPE, vm)
                ksl = tt % 2
                c.ring_ktmp.acquire(P, "scalar", tt)
                vevl = P.op("scalar", lambda e, ksl=ksl, hh=hh: e.activation(
                    ktmp[ksl], c.PS[3][:, hh * 512:(hh + 1) * 512],
                    AF.Copy), inc=(sACT, 1))
                ps_set(c, 3, sACT, vevl, halves=(hh,))
                P.wait("gpsimd", sACT, vevl)
                vst = P.op("gpsimd", lambda e, h=h, tt=tt, ksl=ksl, sp=sp: e.dma_start(
                    out=sp[h][:, tt * 512:(tt + 1) * 512], in_=ktmp[ksl]), inc=(ssp[ksl], 16))
                c.ring_ktmp.release(tt, ssp[ksl], vst)
            c.ring_wf.consumed(idx, sDVE, vlo_)
            whilo_guard = (sPE, vmlast)
    for sm in sVSP + sQSP + sKSP:
        P.wait("sync", sm, sm.n)
        P.wait("gpsimd", sm, sm.n)
    v_projs_done = P.sems["sPE"].n

    c.ring_k.guard = [[(sPE, v_projs_done)] for _ in range(c.ring_k.n)]
    c.ring_q.guard = [[(sPE, v_projs_done)] for _ in range(c.ring_q.n)]
    c.ring_ktmp = Ring(2)
    c.ring_v.guard = [[(sPE, v_projs_done)] for _ in range(c.ring_v.n)]
    c.ring_wo.guard = [[(sPE, v_projs_done)] for _ in range(c.ring_wo.n)]
    c.pT_war = (sPE, v_projs_done)

    for qb in range(8):
        nvk = 2 * (qb + 1)
        for h in range(16):
            tok_q = c.ring_q.load("gpsimd", qb * 16 + h, lambda e, h=h, qb=qb: e.dma_start(
                out=qslot[h % 2], in_=spill["q_sp"][h][:, qb * 256:(qb + 1) * 256]))
            P.wait("tensor", *tok_q)
            nq512 = min((2 * (qb + 1) * 128 + 511) // 512, 4)
            ktoks = []
            for j in range(nq512):
                ktoks.append(c.ring_k.load("sync" if j % 2 == 0 else "gpsimd",
                                           h * 4 + j, lambda e, h=h, j=j: e.dma_start(
                    out=kslot4[j], in_=spill["k_sp"][h][:, j * 512:(j + 1) * 512])))
            for t_ in ktoks:
                P.wait("tensor", *t_)
            vsc_last = None
            for qc2 in range(2):
                qi = qb * 2 + qc2
                span = (qi + 1) * 128
                pa, pb = (0, 1) if qc2 == 0 else (2, 3)
                m_a, m_b = m_a2[qc2], m_b2[qc2]
                negm, d1, d2, rcp = negm2[qc2], d1_2[qc2], d2_2[qc2], rcp2[qc2]
                exp_sb = exp_sb2[qc2]
                ps_wait(c, P, "tensor", pa)
                ps_wait(c, P, "tensor", pb)
                n512 = span // 512
                rem = (span % 512) // 128
                vsc = None
                for ks in range(n512):
                    pst, hh = (pa, ks % 2) if ks < 2 else (pb, ks % 2)
                    vsc = P.op("tensor", lambda e, h=h, qc2=qc2, ks=ks, pst=pst, hh=hh: e.matmul(
                        c.PS[pst][:, hh * 512:(hh + 1) * 512],
                        qslot[h % 2][:, qc2 * 128:(qc2 + 1) * 128],
                        kslot4[ks], start=True, stop=True),
                        inc=(sPE, 1))
                if rem:
                    ks = n512
                    pst, hh = (pa, ks % 2) if ks < 2 else (pb, ks % 2)
                    vsc = P.op("tensor", lambda e, h=h, qc2=qc2, ks=ks, rem=rem, pst=pst, hh=hh: e.matmul(
                        c.PS[pst][:, hh * 512:hh * 512 + rem * 128],
                        qslot[h % 2][:, qc2 * 128:(qc2 + 1) * 128],
                        kslot4[ks][:, 0:rem * 128], start=True, stop=True),
                        inc=(sPE, 1))
                vsc_last = vsc
                dpos = qi * 128
                dpst = pa if dpos < 1024 else pb
                doff = dpos % 1024
                P.wait("vector", sPE, vsc)
                if c.stat_guard2[qc2] is not None:
                    P.wait("vector", c.stat_guard2[qc2][0], c.stat_guard2[qc2][1])
                vmk = P.op("vector", lambda e, dpst=dpst, doff=doff: e.tensor_tensor(
                    c.PS[dpst][:, doff:doff + 128], c.PS[dpst][:, doff:doff + 128],
                    c.maskd, ALU.add), inc=(sDVE, 1))
                P.wait("vector", sDVE, vmk)
                la = min(span, 1024)
                v1 = P.op("vector", lambda e, pa=pa, la=la, m_a=m_a: e.reduce_max(
                    m_a, c.PS[pa][:, 0:la], axis=AX), inc=(sDVE, 1))
                if span > 1024:
                    lb = span - 1024
                    v2 = P.op("vector", lambda e, pb=pb, lb=lb, m_b=m_b: e.reduce_max(
                        m_b, c.PS[pb][:, 0:lb], axis=AX), inc=(sDVE, 1))
                    P.wait("vector", sDVE, v2)
                    v1 = P.op("vector", lambda e, m_a=m_a, m_b=m_b: e.tensor_max(
                        m_a, m_a, m_b), inc=(sDVE, 1))
                P.wait("vector", sDVE, v1)
                v3 = P.op("vector", lambda e, m_a=m_a, negm=negm: e.tensor_scalar(
                    negm, m_a, -SCALE_L1, None, ALU.mult), inc=(sDVE, 1))
                P.wait("scalar", sDVE, v3)
                if c.exp_guard2[qc2] is not None:
                    P.wait("scalar", c.exp_guard2[qc2][0], c.exp_guard2[qc2][1])
                vp = P.op("scalar", lambda e, pa=pa, la=la, negm=negm, d1=d1,
                          exp_sb=exp_sb: e.activation(
                    exp_sb[:, 0:la], c.PS[pa][:, 0:la], AF.Exp, bias=negm, scale=SCALE_L1,
                    accum_out=d1), inc=(sACT, 1))
                if span > 1024:
                    vp = P.op("scalar", lambda e, pb=pb, lb=lb, negm=negm, d2=d2,
                              exp_sb=exp_sb: e.activation(
                        exp_sb[:, 1024:1024 + lb], c.PS[pb][:, 0:lb], AF.Exp, bias=negm,
                        scale=SCALE_L1, accum_out=d2), inc=(sACT, 1))
                ps_set(c, pa, sACT, vp)
                ps_set(c, pb, sACT, vp)
                c.stat_guard2[qc2] = (sACT, vp)
                P.wait("vector", sACT, vp)
                if span > 1024:
                    v4 = P.op("vector", lambda e, d1=d1, d2=d2: e.tensor_add(
                        d1, d1, d2), inc=(sDVE, 1))
                    P.wait("vector", sDVE, v4)
                v5 = P.op("vector", lambda e, d1=d1, rcp=rcp: e.reciprocal(rcp, d1),
                          inc=(sDVE, 1))
                P.wait("vector", sDVE, v5)
                if h == 0:
                    vps = P.op("vector", lambda e, qc2=qc2, span=span, rcp=rcp,
                               exp_sb=exp_sb: e.tensor_scalar(
                        psum_blk[qc2][:, 0:span], exp_sb[:, 0:span], rcp, None, ALU.mult),
                        inc=(sDVE, 1))
                else:
                    vps = P.op("vector", lambda e, qc2=qc2, span=span, rcp=rcp,
                               exp_sb=exp_sb: e.scalar_tensor_tensor(
                        psum_blk[qc2][:, 0:span], exp_sb[:, 0:span], rcp,
                        psum_blk[qc2][:, 0:span], ALU.mult, ALU.add), inc=(sDVE, 1))
                c.exp_guard2[qc2] = (sDVE, vps)
            c.ring_q.consumed(qb * 16 + h, sPE, vsc_last)
            for j in range(nq512):
                c.ring_k.consumed(h * 4 + j, sPE, vsc_last)

        for qc2 in range(2):
            qi = qb * 2 + qc2
            span = (qi + 1) * 128
            pa = 0 if qc2 == 0 else 2
            if c.pbf_guard is not None:
                P.wait("vector", c.pbf_guard[0], c.pbf_guard[1])
            P.wait("vector", sDVE, P.sems["sDVE"].n)
            vcast = P.op("vector", lambda e, qc2=qc2, span=span: e.tensor_copy(
                probs_bf[:, 0:span], psum_blk[qc2][:, 0:span]), inc=(sDVE, 1))
            P.wait("tensor", sDVE, vcast)
            nk = qi + 1
            ps_wait(c, P, "tensor", pa)
            vtr = None
            for j in range(nk):
                vtr = P.op("tensor", lambda e, j=j, pa=pa: e.transpose(
                    c.PS[pa].bitcast(BF16)[:, (j % 8) * 128:((j % 8) + 1) * 128],
                    probs_bf[:, j * 128:(j + 1) * 128], c.idb),
                    inc=(sPE, 1) if (j % 8 == 7 or j == nk - 1) else None)
                if j % 8 == 7 or j == nk - 1:
                    base = (j // 8) * 8
                    cnt = j - base + 1
                    P.wait("vector", sPE, vtr)
                    if qc2 == 0 and base == 0 and c.pT_war is not None:
                        P.wait("vector", c.pT_war[0], c.pT_war[1])
                    vmv = P.op("vector", lambda e, base=base, cnt=cnt, qc2=qc2, pa=pa: e.tensor_copy(
                        probsT3[:, base:base + cnt, qc2 * 128:(qc2 + 1) * 128],
                        c.PS[pa].bitcast(BF16).rearrange("p (j n) -> p j n", j=16)[:, 0:cnt, :]),
                        inc=(sDVE, 1))
                    ps_set(c, pa, sDVE, vmv)
                    c.pbf_guard = (sPE, vtr)
                    if j < nk - 1:
                        ps_wait(c, P, "tensor", pa)
        vz = P.op("vector", lambda e, nvk=nvk: e.memset(
            probsT3[:, nvk - 1, 0:128], 0), inc=(sDVE, 1))

        v_pT = P.sems["sDVE"].n
        P.wait("tensor", sDVE, v_pT)
        pvtile = lambda d8: c.PS[d8 // 2][:, (d8 % 2) * 512:(d8 % 2) * 512 + 256]
        for dhalf in range(2):
            for i in range(4):
                ps_wait(c, P, "tensor", i)
            vpv = None
            order = list(range(nvk)) if dhalf == 0 else list(range(nvk - 1, -1, -1))
            for idx, kc in enumerate(order):
                sl = kc % 6
                if c.vslot_kc[sl] == kc:
                    tok_v = c.v_toks[kc]
                else:
                    tok_v = c.ring_v.load("sync" if kc % 2 == 0 else "gpsimd", sl,
                                          lambda e, kc=kc, sl=sl: e.dma_start(
                        out=vslot6[sl], in_=spill["v_sp"][kc]))
                    c.vslot_kc[sl] = kc
                    c.v_toks[kc] = tok_v
                P.wait("tensor", *tok_v)
                for d8 in range(8):
                    dc = dhalf * 8 + d8
                    vpv = P.op("tensor", lambda e, kc=kc, dc=dc, d8=d8, sl=sl,
                               st=(idx == 0), sp_=(idx == nvk - 1): e.matmul(
                        pvtile(d8), vslot6[sl][:, dc * 128:(dc + 1) * 128],
                        probsT3[:, kc, :], start=st, stop=sp_,
                        skip_group_check=True),
                        inc=(sPE, 1) if d8 == 7 else None)
                c.ring_v.consumed(sl, sPE, vpv)
            P.wait("scalar", sPE, vpv)
            for d8 in range(8):
                dc = dhalf * 8 + d8
                vout = P.op("scalar", lambda e, dc=dc, d8=d8, qb=qb: e.activation(
                    ao(dc)[:, qb * 256:(qb + 1) * 256], pvtile(d8), AF.Copy), inc=(sACT, 1))
            v_evac = P.sems["sACT"].n
            for i in range(4):
                ps_set(c, i, sACT, v_evac)
        c.pT_war = (sPE, P.sems["sPE"].n)
    v_attn = P.sems["sACT"].n

    if globals().get("DEBUG_L1A_STOP", False):
        sDBG = P.sem("sDBG")
        P.wait("gpsimd", sACT, v_attn)
        for dc in range(16):
            P.op("gpsimd", lambda e, dc=dc: e.dma_start(
                out=x_out_ap[dc * 128:(dc + 1) * 128, :], in_=ao(dc)), inc=(sDBG, 16))
        P.wait("gpsimd", sDBG, P.sems["sDBG"].n)
        P.barrier()
        c.reset_phase_guards()
        return

    xsl = [vf(c.C, 0, 2048), vf(c.C, 4096, 2048)]
    xosl = [vf(c.C, 2048, 2048), vf(c.C, 6144, 2048)]
    P.wait("sync", sDVE, P.sems["sDVE"].n)
    P.wait("sync", sPE, P.sems["sPE"].n)
    P.wait("tensor", sACT, v_attn)
    oslot = [whi, wlo]
    for oc in range(16):
        sl = oc % 2
        tok_w = c.ring_wo.load("gpsimd", oc, lambda e, oc=oc: e.dma_start(
            out=oslot[oc % 2], in_=a["wo"][oc]))
        tok_x = c.ring_xep.load("sync", oc, lambda e, oc=oc, sl=sl: e.dma_start(
            out=xsl[sl], in_=x_in_ap[oc * 128:(oc + 1) * 128, :]))
        P.wait("tensor", *tok_w)
        vdv = None
        for tt in range(4):
            hh = tt % 2
            ps_wait(c, P, "tensor", 3, halves=(hh,))
            vm = None
            for kc in range(16):
                vm = P.op("tensor", lambda e, oc=oc, tt=tt, kc=kc, hh=hh: e.matmul(
                    c.PS[3][:, hh * 512:(hh + 1) * 512],
                    oslot[oc % 2][:, kc * 128:(kc + 1) * 128],
                    ao(kc)[:, tt * 512:(tt + 1) * 512],
                    start=(kc == 0), stop=(kc == 15)), inc=(sPE, 1) if kc == 15 else None)
            if tt == 3:
                c.ring_wo.consumed(oc, sPE, vm)
            P.wait("vector", sPE, vm)
            if tt == 0:
                P.wait("vector", *tok_x)
                c.ring_xo.acquire(P, "vector", oc)
            vdv = P.op("vector", lambda e, tt=tt, hh=hh, sl=sl: e.tensor_add(
                xosl[sl][:, tt * 512:(tt + 1) * 512], c.PS[3][:, hh * 512:(hh + 1) * 512],
                xsl[sl][:, tt * 512:(tt + 1) * 512]), inc=(sDVE, 1))
            ps_set(c, 3, sDVE, vdv, halves=(hh,))
        c.ring_xep.consumed(oc, sDVE, vdv)
        P.wait("sync", sDVE, vdv)
        stsem = c.st_sems[sl]
        vst = P.op("sync", lambda e, oc=oc, sl=sl: e.dma_start(
            out=x_out_ap[oc * 128:(oc + 1) * 128, :], in_=xosl[sl]), inc=(stsem, 16))
        c.ring_xo.release(oc, stsem, vst)
    for st in c.st_sems:
        P.wait("sync", st, st.n)
    P.barrier()
    c.reset_phase_guards()
    return


def prep_l1_attn(W_q, W_k, W_v, W_o):
    return {"l1a_wq": lhsT_tiles(W_q), "l1a_wk": lhsT_tiles(W_k),
            "l1a_wv": np.ascontiguousarray(W_v.reshape(16, 128, 2048)),
            "l1a_wo": lhsT_tiles(W_o)}


def declare_l1_attn(nc):
    a = {}
    for n in ("wq", "wk", "wv", "wo"):
        a[n] = nc.dram_tensor(f"l1a_{n}", [16, 128, 2048], F32, kind="ExternalInput").ap()
    return a



def init_guards(c):
    P = c.P
    c.ring_xw = DmaRing(P, "xw", 1)
    c.ring_sq = Ring(1)
    c.ring_bc = Ring(1)
    c.ring_g = DmaRing(P, "wg", 2)
    c.ring_u = DmaRing(P, "wu", 2)
    c.ring_d = DmaRing(P, "wd", 2)
    c.ring_xep = DmaRing(P, "xep", 2)
    c.ring_stg1 = DmaRing(P, "stg1", 1)
    c.ring_stg2 = DmaRing(P, "stg2", 1)
    c.ring_wv = DmaRing(P, "wv", 16)
    c.ring_w = DmaRing(P, "wkqo", 2)
    c.ring_q = DmaRing(P, "qld", 2)
    c.ring_k = DmaRing(P, "kld", 4)
    c.ring_wo = DmaRing(P, "wo", 2)
    c.ring_wf = DmaRing(P, "wf", 1)
    c.ring_v = DmaRing(P, "vld", 6)
    c.ring_vtmp = Ring(2)
    c.ring_ktmp = Ring(2)
    c.exp_guard = None
    c.pbf_guard = None
    c.pT_war = None
    c.stat_guard = None
    c.stat_guard2 = [None, None]
    c.pbf_guard2 = [None, None]
    c.exp_guard2 = [None, None]
    c.mbc_guard = None
    c.rbc_guard = None
    c.ylo_tmp_guard = None
    c.ring_xo = Ring(2)
    c.st_sems = [P.sem("stA"), P.sem("stB")]
    c.oa_g = [None] * 16
    c.y_guard = None
    c.l1_dh_v = 0
    c.l0_negsw_v = 0
    c.bc_pool_guard = None


def _reset_phase_guards(c):
    for r in (c.ring_xw, c.ring_g, c.ring_u, c.ring_d, c.ring_xep, c.ring_stg1,
              c.ring_stg2, c.ring_wv, c.ring_w, c.ring_wo, c.ring_wf, c.ring_q, c.ring_k, c.ring_v):
        r.guard = [None] * r.n
    c.ring_sq = Ring(1)
    c.ring_bc = Ring(1)
    c.ring_xo = Ring(2)
    c.ring_vtmp = Ring(2)
    c.ring_ktmp = Ring(2)
    c.oa_g = [None] * 16
    c.psg = {}
    c.y_guard = None
    c.exp_guard = None
    c.pbf_guard = None
    c.pT_war = None
    c.stat_guard = None
    c.stat_guard2 = [None, None]
    c.pbf_guard2 = [None, None]
    c.exp_guard2 = [None, None]
    c.mbc_guard = None
    c.rbc_guard = None
    c.ylo_tmp_guard = None


def attach_guard_methods(c):
    c.reset_phase_guards = lambda: _reset_phase_guards(c)

    def y_guard_wait(P, eng):
        if c.y_guard is not None:
            P.wait(eng, c.y_guard[0], c.y_guard[1])
    c.y_guard_wait = y_guard_wait

    def exp_guard_wait(P):
        if c.exp_guard is not None:
            P.wait("scalar", c.exp_guard[0], c.exp_guard[1])
    c.exp_guard_wait = exp_guard_wait

    def oa_guard_wait(P, eng, oc):
        if c.oa_g[oc] is not None:
            P.wait(eng, c.oa_g[oc][0], c.oa_g[oc][1])
    c.oa_guard_wait = oa_guard_wait

    def oa_guard_set(oc, sem, val):
        c.oa_g[oc] = (sem, val)
    c.oa_guard_set = oa_guard_set

    def oa_guard_clear(oc):
        c.oa_g[oc] = None
    c.oa_guard_clear = oa_guard_clear



def prep_l1_ffn(gate_w, up_w, down_w, gate_scale, up_scale, down_scale):
    d = {}
    d["l1_gw"] = lhsT_tiles(np.ascontiguousarray(gate_w.T))
    d["l1_uw"] = lhsT_tiles(np.ascontiguousarray(up_w.T))
    d["l1_dw"] = lhsT_tiles(np.ascontiguousarray(down_w.T))
    d["l1_gs"] = np.ascontiguousarray(gate_scale.reshape(64, 128).T)
    d["l1_us"] = np.ascontiguousarray(up_scale.reshape(64, 128).T)
    d["l1_ds"] = np.ascontiguousarray(down_scale.reshape(16, 128).T)
    return d


def declare_l1_ffn(nc):
    aps = {}
    aps["l1_gw"] = nc.dram_tensor("l1_gw", [64, 128, 2048], I32, kind="ExternalInput").ap()
    aps["l1_uw"] = nc.dram_tensor("l1_uw", [64, 128, 2048], I32, kind="ExternalInput").ap()
    aps["l1_dw"] = nc.dram_tensor("l1_dw", [16, 128, 8192], I32, kind="ExternalInput").ap()
    aps["l1_gs"] = nc.dram_tensor("l1_gs", [128, 64], F32, kind="ExternalInput").ap()
    aps["l1_us"] = nc.dram_tensor("l1_us", [128, 64], F32, kind="ExternalInput").ap()
    aps["l1_ds"] = nc.dram_tensor("l1_ds", [128, 16], F32, kind="ExternalInput").ap()
    return aps



def build_test_l1ffn():
    nc = bass.Bass("TRN2", target_bir_lowering=False)
    P = Prog(nc)
    xin = nc.dram_tensor("xin", [D, T], F32, kind="ExternalInput").ap()
    xout = nc.dram_tensor("xout", [D, T], F32, kind="ExternalOutput").ap()
    aps = declare_l1_ffn(nc)
    c = build_common(nc, P)
    init_guards(c)
    attach_guard_methods(c)
    emit_l1_ffn(c, xin, xout, aps["l1_gw"], aps["l1_uw"], aps["l1_dw"],
                aps["l1_gs"], aps["l1_us"], aps["l1_ds"], load_scales=True)
    for st in c.st_sems:
        P.wait("sync", st, st.n)
    return P.finalize(), P


def build_test_l0ffn():
    nc = bass.Bass("TRN2", target_bir_lowering=False)
    P = Prog(nc)
    xin = nc.dram_tensor("xin", [D, T], F32, kind="ExternalInput").ap()
    xout = nc.dram_tensor("xout", [D, T], F32, kind="ExternalOutput").ap()
    aps = declare_l0_ffn(nc)
    c = build_common(nc, P)
    init_guards(c)
    attach_guard_methods(c)
    emit_l0_ffn(c, xin, xout, aps, load_w=True)
    for st in c.st_sems:
        P.wait("sync", st, st.n)
    return P.finalize(), P


def np_l0_ffn_ref(x, rw, gu, dn, shg, shu, shd, sgw):
    def ln(v):
        m = v.mean(-1, keepdims=True)
        var = v.var(-1, keepdims=True)
        return (v - m) / np.sqrt(var + 1e-5)
    def silu(v):
        return v / (1 + np.exp(-v))
    y = ln(x)
    shared = (silu(y @ shg.T) * (y @ shu.T)) @ shd.T
    logits = y @ rw.T
    probs = np.exp(logits - logits.max(-1, keepdims=True))
    probs = probs / probs.sum(-1, keepdims=True)
    order = np.argsort(-probs, axis=-1)
    top2 = order[:, :2]
    tv = np.take_along_axis(probs, top2, axis=-1)
    tv = tv / tv.sum(-1, keepdims=True)
    comb = np.zeros_like(probs)
    np.put_along_axis(comb, top2, tv, axis=-1)
    out = np.zeros_like(x)
    for e in range(8):
        gu_ = y @ gu[e].T
        cur = (silu(gu_[:, :1408]) * gu_[:, 1408:]) @ dn[e].T
        out = out + cur * comb[:, e:e + 1]
    out = out + (1 / (1 + np.exp(-(y @ sgw.T)))) * shared
    return x + 0.5 * out


def _test_l0ffn(sim=False):
    rng = np.random.default_rng(11)
    x = (rng.standard_normal((T, D)) * 1.5).astype(np.float32)
    sc = 0.02
    rw = (rng.standard_normal((8, 2048)) * sc).astype(np.float32)
    gu = (rng.standard_normal((8, 2816, 2048)) * sc).astype(np.float32)
    dn = (rng.standard_normal((8, 2048, 1408)) * sc).astype(np.float32)
    shg = (rng.standard_normal((5632, 2048)) * sc).astype(np.float32)
    shu = (rng.standard_normal((5632, 2048)) * sc).astype(np.float32)
    shd = (rng.standard_normal((2048, 5632)) * sc).astype(np.float32)
    sgw = (rng.standard_normal((1, 2048)) * sc).astype(np.float32)

    nc, P = build_test_l0ffn()
    pre = prep_l0_ffn(rw, gu, dn, shg, shu, shd, sgw)
    inm = {"xin": np.ascontiguousarray(x.T), **host_consts(),
           **{k[3:] if False else k: v for k, v in pre.items()}}
    if sim:
        from concourse.bass_interp import CoreSim
        import time as _t
        sm = CoreSim(nc)
        for k, v in inm.items():
            sm.tensor(k)[:] = v
        t0 = _t.time()
        sm.simulate()
        print("sim took", _t.time() - t0)
        got = np.array(sm.tensor("xout")).T
    else:
        res = run_bass_kernel_spmd(nc, [inm], core_ids=[0])
        got = res.results[0]["xout"].T
    want = np_l0_ffn_ref(x, rw, gu, dn, shg, shu, shd, sgw)
    err = got - want
    rel = np.sqrt((err ** 2).mean() / (want ** 2).mean())
    print("rel l2:", rel, "max abs:", np.abs(err).max(), "scale:", np.abs(want).max())


def build_test_l0attn():
    nc = bass.Bass("TRN2", target_bir_lowering=False)
    P = Prog(nc)
    xin = nc.dram_tensor("xin", [D, T], F32, kind="ExternalInput").ap()
    xout = nc.dram_tensor("xout", [D, T], F32, kind="ExternalOutput").ap()
    aps = declare_l0_attn(nc)
    c = build_common(nc, P)
    init_guards(c)
    attach_guard_methods(c)
    emit_l0_attn(c, xin, xout, aps, load_w=True)
    for st in c.st_sems:
        P.wait("sync", st, st.n)
    return P.finalize(), P


def np_l0_attn_ref(x, basis, phase, amp):
    def ln(v):
        m = v.mean(-1, keepdims=True)
        return (v - m) / np.sqrt(v.var(-1, keepdims=True) + 1e-5)
    def holo(v, b, p, a):
        resn = v @ b.T
        w = a * np.cos(p)
        return resn @ w.T
    y = ln(x)
    q = holo(y, basis[0], phase[0], amp[0]).reshape(T, 16, 128).transpose(1, 0, 2)
    k = holo(y, basis[1], phase[1], amp[1]).reshape(T, 16, 128).transpose(1, 0, 2)
    v = holo(y, basis[2], phase[2], amp[2]).reshape(T, 16, 128).transpose(1, 0, 2)
    out = np.zeros((16, T, 128), dtype=np.float32)
    sc = 1.0 / np.sqrt(128.0)
    for h in range(16):
        s_ = (q[h] @ k[h].T) * sc
        s_ = s_ - s_.max(-1, keepdims=True)
        p_ = np.exp(s_)
        p_ = p_ / p_.sum(-1, keepdims=True)
        out[h] = p_ @ v[h]
    o = out.transpose(1, 0, 2).reshape(T, 2048)
    return x + holo(o, basis[3], phase[3], amp[3])


def _test_l0attn(sim=False):
    rng = np.random.default_rng(13)
    x = (rng.standard_normal((T, D)) * 1.5).astype(np.float32)
    basis = rng.standard_normal((4, 64, 2048)).astype(np.float32)
    basis = basis / np.linalg.norm(basis, axis=2, keepdims=True)
    phase = (rng.standard_normal((4, 2048, 64)) * 0.1).astype(np.float32)
    amp = (np.ones((4, 2048, 64)) / 8.0).astype(np.float32) * (
        1 + 0.3 * rng.standard_normal((4, 2048, 64))).astype(np.float32)

    nc, P = build_test_l0attn()
    pre = prep_l0_attn(basis[0], phase[0], amp[0], basis[1], phase[1], amp[1],
                       basis[2], phase[2], amp[2], basis[3], phase[3], amp[3])
    inm = {"xin": np.ascontiguousarray(x.T), **host_consts(), **pre}
    if sim:
        from concourse.bass_interp import CoreSim
        import time as _t
        sm = CoreSim(nc)
        for kk, vv in inm.items():
            sm.tensor(kk)[:] = vv
        t0 = _t.time()
        sm.simulate()
        print("sim took", _t.time() - t0)
        got = np.array(sm.tensor("xout")).T
    else:
        res = run_bass_kernel_spmd(nc, [inm], core_ids=[0])
        got = res.results[0]["xout"].T
    want = np_l0_attn_ref(x, basis, phase, amp)
    err = got - want
    rel = np.sqrt((err ** 2).mean() / (want ** 2).mean())
    print("rel l2:", rel, "max abs:", np.abs(err).max(), "scale:", np.abs(want).max())


def build_test_l1attn():
    nc = bass.Bass("TRN2", target_bir_lowering=False)
    P = Prog(nc)
    xin = nc.dram_tensor("xin", [D, T], F32, kind="ExternalInput").ap()
    xout = nc.dram_tensor("xout", [D, T], F32, kind="ExternalOutput").ap()
    aps = declare_l1_attn(nc)
    spill = {"v_sp": nc.dram_tensor("v_sp", [16, 128, 2048], BF16).ap(),
             "q_sp": nc.dram_tensor("q_sp", [16, 128, 2048], F32R).ap(),
             "k_sp": nc.dram_tensor("k_sp", [16, 128, 2048], F32R).ap()}
    c = build_common(nc, P)
    init_guards(c)
    attach_guard_methods(c)
    emit_l1_attn(c, xin, xout, aps, spill)
    for st in c.st_sems:
        P.wait("sync", st, st.n)
    return P.finalize(), P


def np_l1_attn_ref(x, W_q, W_k, W_v, W_o):
    def ln(v):
        m = v.mean(-1, keepdims=True)
        return (v - m) / np.sqrt(v.var(-1, keepdims=True) + 1e-5)
    y = ln(x)
    q = (y @ W_q).reshape(T, 16, 128).transpose(1, 0, 2)
    k = (y @ W_k).reshape(T, 16, 128).transpose(1, 0, 2)
    v = y @ W_v
    sc = 1.0 / np.sqrt(128.0)
    mask = np.triu(np.full((T, T), -1e9, dtype=np.float32), k=1)
    psum = np.zeros((T, T), dtype=np.float32)
    for h in range(16):
        s_ = (q[h] @ k[h].T + mask) * sc
        s_ = s_ - s_.max(-1, keepdims=True)
        p_ = np.exp(s_)
        psum += p_ / p_.sum(-1, keepdims=True)
    return x + (psum @ v) @ W_o


def _test_l1attn(sim=False):
    rng = np.random.default_rng(17)
    x = (rng.standard_normal((T, D)) * 1.5).astype(np.float32)
    W_q = rng.standard_normal((D, D)).astype(np.float32)
    W_k = rng.standard_normal((D, D)).astype(np.float32)
    W_v = (rng.standard_normal((D, D)) * np.sqrt(2.0 / D)).astype(np.float32)
    W_o = (rng.standard_normal((D, D)) * np.sqrt(2.0 / D)).astype(np.float32)
    nc, P = build_test_l1attn()
    inm = {"xin": np.ascontiguousarray(x.T), **host_consts(),
           **prep_l1_attn(W_q, W_k, W_v, W_o)}
    if sim:
        from concourse.bass_interp import CoreSim
        import time as _t
        sm = CoreSim(nc)
        for kk, vv in inm.items():
            sm.tensor(kk)[:] = vv
        t0 = _t.time()
        sm.simulate()
        print("sim took", _t.time() - t0)
        got = np.array(sm.tensor("xout")).T
    else:
        res = run_bass_kernel_spmd(nc, [inm], core_ids=[0])
        got = res.results[0]["xout"].T
    want = np_l1_attn_ref(x, W_q, W_k, W_v, W_o)
    err = got - want
    rel = np.sqrt((err ** 2).mean() / (want ** 2).mean())
    print("rel l2:", rel, "max abs:", np.abs(err).max(), "scale:", np.abs(want).max())



_PROGS = {}

FUSED = True


def build_fused():
    nc = bass.Bass("TRN2", target_bir_lowering=False)
    P = Prog(nc)
    xin = nc.dram_tensor("xin", [D, T], F32, kind="ExternalInput").ap()
    xout = nc.dram_tensor("xout", [D, T], F32, kind="ExternalOutput").ap()
    xs0 = nc.dram_tensor("xs0", [D, T], F32).ap()
    xs1 = nc.dram_tensor("xs1", [D, T], F32).ap()
    spill = {"v_sp": nc.dram_tensor("v_sp", [16, 128, 2048], BF16).ap(),
             "q_sp": nc.dram_tensor("q_sp", [16, 128, 2048], F32R).ap(),
             "k_sp": nc.dram_tensor("k_sp", [16, 128, 2048], F32R).ap()}
    a_l0f = declare_l0_ffn(nc)
    a_l0a = declare_l0_attn(nc)
    a_l1f = declare_l1_ffn(nc)
    a_l1a = declare_l1_attn(nc)
    c = build_common(nc, P)
    init_guards(c)
    attach_guard_methods(c)
    emit_l0_ffn(c, xin, xs0, a_l0f, load_w=True)
    emit_l0_attn(c, xs0, xs1, a_l0a, load_w=True)
    emit_l0_ffn(c, xs1, xs0, a_l0f, load_w=False)
    emit_l1_ffn(c, xs0, xs1, a_l1f["l1_gw"], a_l1f["l1_uw"], a_l1f["l1_dw"],
                a_l1f["l1_gs"], a_l1f["l1_us"], a_l1f["l1_ds"], load_scales=True)
    emit_l1_attn(c, xs1, xs0, a_l1a, spill)
    emit_l1_ffn(c, xs0, xout, a_l1f["l1_gw"], a_l1f["l1_uw"], a_l1f["l1_dw"],
                a_l1f["l1_gs"], a_l1f["l1_us"], a_l1f["l1_ds"], load_scales=False)
    for st in c.st_sems:
        P.wait("sync", st, st.n)
    return P.finalize(), P


def _get_stage_prog(kind):
    if kind in _PROGS:
        return _PROGS[kind]
    if kind == "l0ffn":
        nc, P = build_test_l0ffn()
    elif kind == "l0attn":
        nc, P = build_test_l0attn()
    elif kind == "l1ffn":
        nc, P = build_test_l1ffn()
    elif kind == "l1attn":
        nc, P = build_test_l1attn()
    _PROGS[kind] = nc
    return nc


def kernel(**inputs):
    inp = {k: np.asarray(v) for k, v in inputs.items()}
    x = np.asarray(inp["L0_attn_hidden_states"], dtype=np.float32)
    consts = host_consts()
    w_l0f = prep_l0_ffn(inp["L0_ffn_router_weight"], inp["L0_ffn_gate_up_proj"],
                        inp["L0_ffn_down_proj"], inp["L0_ffn_sh_gate_w"],
                        inp["L0_ffn_sh_up_w"], inp["L0_ffn_sh_down_w"],
                        inp["L0_ffn_shared_gate_w"])
    w_l0a = prep_l0_attn(inp["L0_attn_basis_q"], inp["L0_attn_phase_q"], inp["L0_attn_amp_q"],
                         inp["L0_attn_basis_k"], inp["L0_attn_phase_k"], inp["L0_attn_amp_k"],
                         inp["L0_attn_basis_v"], inp["L0_attn_phase_v"], inp["L0_attn_amp_v"],
                         inp["L0_attn_basis_o"], inp["L0_attn_phase_o"], inp["L0_attn_amp_o"])
    w_l1a = prep_l1_attn(np.asarray(inp["L1_attn_W_q"], np.float32),
                         np.asarray(inp["L1_attn_W_k"], np.float32),
                         np.asarray(inp["L1_attn_W_v"], np.float32),
                         np.asarray(inp["L1_attn_W_o"], np.float32))
    w_l1f = prep_l1_ffn(np.asarray(inp["L1_ffn_gate_w"], np.int32),
                        np.asarray(inp["L1_ffn_up_w"], np.int32),
                        np.asarray(inp["L1_ffn_down_w"], np.int32),
                        np.asarray(inp["L1_ffn_gate_scale"], np.float32),
                        np.asarray(inp["L1_ffn_up_scale"], np.float32),
                        np.asarray(inp["L1_ffn_down_scale"], np.float32))

    xs = [np.ascontiguousarray(x[b].T).astype(np.float32) for b in range(8)]
    if FUSED:
        if "fused" not in _PROGS:
            _PROGS["fused"] = build_fused()[0]
        nc = _PROGS["fused"]
        wd = {**consts, **w_l0f, **w_l0a, **w_l1f, **w_l1a}
        in_maps = [{"xin": xs[b], **wd} for b in range(8)]
        res = run_bass_kernel_spmd(nc, in_maps, core_ids=list(range(8)))
        xs = [np.asarray(res.results[b]["xout"]) for b in range(8)]
    else:
        stages = [("l0ffn", w_l0f), ("l0attn", w_l0a), ("l0ffn", w_l0f),
                  ("l1ffn", w_l1f), ("l1attn", w_l1a), ("l1ffn", w_l1f)]
        for kind, wd in stages:
            nc = _get_stage_prog(kind)
            in_maps = [{"xin": xs[b], **consts, **wd} for b in range(8)]
            res = run_bass_kernel_spmd(nc, in_maps, core_ids=list(range(8)))
            xs = [np.asarray(res.results[b]["xout"]) for b in range(8)]
    out = np.stack([xs[b].T for b in range(8)]).astype(np.float32)
    return np.ascontiguousarray(out)


def _test_l1ffn():
    import jax
    import jax.numpy as jnp
    rng = np.random.default_rng(7)
    x = (rng.standard_normal((T, D)) * 1.5).astype(np.float32)
    gate_w = rng.integers(0, 256, (8192, 2048)).astype(np.int32)
    up_w = rng.integers(0, 256, (8192, 2048)).astype(np.int32)
    down_w = rng.integers(0, 256, (2048, 8192)).astype(np.int32)
    gs = (rng.random(8192) * 0.02).astype(np.float32)
    us = (rng.random(8192) * 0.02).astype(np.float32)
    ds = (rng.random(2048) * 0.02).astype(np.float32)

    nc, P = build_test_l1ffn()
    inm = {"xin": np.ascontiguousarray(x.T), **host_consts(),
           **prep_l1_ffn(gate_w, up_w, down_w, gs, us, ds)}
    res = run_bass_kernel_spmd(nc, [inm], core_ids=[0])
    got = res.results[0]["xout"].T

    def ln(v):
        m = v.mean(-1, keepdims=True)
        var = v.var(-1, keepdims=True)
        return (v - m) / np.sqrt(var + 1e-5)
    y = ln(x)
    Wg = (gate_w.astype(np.float32) - 128.0) * gs[:, None]
    Wu = (up_w.astype(np.float32) - 128.0) * us[:, None]
    Wd = (down_w.astype(np.float32) - 128.0) * ds[:, None]
    g = y @ Wg.T
    u = y @ Wu.T
    h = (g / (1 + np.exp(-g))) * u
    out = h @ Wd.T
    want = x + 0.5 * out
    err = np.abs(got - want)
    rel = np.sqrt((err ** 2).mean() / (want ** 2).mean())
    print("rel l2:", rel, "max abs:", err.max(), "want scale:", np.abs(want).max())


if __name__ == "__main__":
    import sys
    if len(sys.argv) > 1 and sys.argv[1] == "l1ffn":
        _test_l1ffn()
    elif len(sys.argv) > 1 and sys.argv[1] == "l0ffn":
        _test_l0ffn(sim=("sim" in sys.argv))
    elif len(sys.argv) > 1 and sys.argv[1] == "l0attn":
        _test_l0attn(sim=("sim" in sys.argv))
    elif len(sys.argv) > 1 and sys.argv[1] == "l1attn":
        _test_l1attn(sim=("sim" in sys.argv))
